# version 03
# baseline (speedup 1.0000x reference)
import math
import numpy as np
import concourse.bass as bass
import concourse.mybir as mybir
from concourse.bass_utils import run_bass_kernel_spmd
from concourse.bass_isa import ReduceOp

F32 = mybir.dt.float32
BF16 = mybir.dt.bfloat16
A = mybir.AluOpType
AF = mybir.ActivationFunctionType
AX = mybir.AxisListType

D = 2048
NC = D // 128
NH = 16
HD = 128
FF = 8192
NFC = FF // 128
EPS_LN = 1e-5
EPS_L0 = 1e-6
EPS_Q = 1e-6
NCORES = 8
GT = D * (FF // NCORES)
DT = FF * (D // NCORES)
TERN = 2 * GT + DT
NW = float(D * FF)
ALL8 = [list(range(NCORES))]
PAIRS = [[0, 1], [2, 3], [4, 5], [6, 7]]
INV_SQRT8 = 1.0 / math.sqrt(8.0)
MAGIC = 256.5


def build(S=2048, upto=4):
    T = S // 2
    TH = T // 2
    NTB = T // 128
    QS = min(256, T)
    NQ = T // QS
    NT = min(512, TH)
    NB_H = TH // NT
    KBq = max(1, T // 512)
    NKB = min(512, T)
    nc = bass.Bass("TRN2", target_bir_lowering=False, debug=False)

    def din(name, shape):
        return nc.dram_tensor(name, shape, F32, kind="ExternalInput").ap()

    x_fm = din("x_fm", [D, T])
    wq0 = din("wq0", [D, D]); wk0 = din("wk0", [D, D])
    wv0 = din("wv0", [D, D]); wo0 = din("wo0", [D, D])
    wq1 = din("wq1", [D, D]); wk1 = din("wk1", [D, D])
    wv1 = din("wv1", [D, D]); wo1 = din("wo1", [D, D])
    fct = din("fct", [D, D])
    bias0 = din("bias0", [128, NC])
    tg_sl = din("tg_sl", [D, FF // NCORES])
    tu_sl = din("tu_sl", [D, FF // NCORES])
    td_sl = din("td_sl", [FF, D // NCORES])
    maskt = din("maskt", [4 * 128, QS])
    ropec = din("ropec", [128, T])
    ropes = din("ropes", [128, T])
    h128t = din("h128t", [128, 128])
    y_fm = nc.dram_tensor("y_fm", [D, T], F32, kind="ExternalOutput").ap()

    f_in = nc.dram_tensor("f_in", [D + 1, D], F32).ap()
    f_out = nc.dram_tensor("f_out", [D + 1, D], F32).ap()
    tern_in = nc.dram_tensor("tern_in", [TERN], BF16).ap()
    tern_out = nc.dram_tensor("tern_out", [NCORES * TERN], BF16).ap()
    k_in = [nc.dram_tensor(f"k_in{l}", [D, T], BF16).ap() for l in range(2)]
    k_out = [nc.dram_tensor(f"k_out{l}", [2 * D, T], BF16).ap() for l in range(2)]
    v_in = [nc.dram_tensor(f"v_in{l}", [NH * T, HD], BF16).ap() for l in range(2)]
    v_out = [nc.dram_tensor(f"v_out{l}", [2 * NH * T, HD], BF16).ap()
             for l in range(2)]

    ctx = []

    def sbt(name, shape, dt=BF16):
        t = nc.sbuf_tensor(name, shape, dt); ctx.append(t); return t.__enter__()

    def pst(name, shape):
        t = nc.psum_tensor(name, shape, F32); ctx.append(t); return t.__enter__()

    stream = sbt("stream", [128, NC, T])
    xbh = sbt("xbh", [128, NC, TH])
    big = sbt("big", [128, 32 * T])
    QV = big.ap().rearrange("p (c t) -> p c t", t=T)
    HID = big.ap().rearrange("p (c t) -> p c t", t=TH)

    masks = sbt("masks", [128, 4, QS])
    pt = sbt("pt", [128, 2, QS])
    qtmp = sbt("qtmp", [128, 2, 512], F32)
    qint = sbt("qint", [128, 512], mybir.dt.int32)
    cbc = sbt("cbc", [128, 512], F32)
    rbc = sbt("rbc", [128, 512], F32)
    kvq = sbt("kvq", [128, 8 * T])
    kd = kvq.ap()[:, 0:4 * T].rearrange("p (b r t) -> p b r t", b=2, r=2)
    vd = kvq.ap()[:, 4 * T:8 * T].rearrange("p (b k e) -> p b k e",
                                            b=2, e=HD)
    xq1 = kvq.ap().rearrange("p (c t) -> p c t", t=TH)[:, 0:NC, :]
    ws = sbt("ws", [128, 2, NC, 512])
    tstage = sbt("tstage", [128, 2, 512], F32)
    ttmp = sbt("ttmp", [128, 512])
    ttrn = sbt("ttrn", [128, 2, 512])
    cosf = sbt("cosf", [128, T])
    sinf = sbt("sinf", [128, T])
    h128 = sbt("h128", [128, 128])
    sc512 = sbt("sc512", [128, 2, 512])
    rk = sbt("rk", [128, 2, 512])
    rsw = sbt("rsw", [128, 512])
    ones_k = sbt("ones_k", [128, 1])
    ones_b = sbt("ones_b", [1, 128])
    ones_k32 = sbt("ones_k32", [128, 1], F32)
    ones_b32 = sbt("ones_b32", [1, 128], F32)
    browz = sbt("browz", [1, 1280])
    bias_sb = sbt("bias_sb", [128, NC], F32)
    scl = sbt("scl", [128, 8], F32)
    msum = sbt("msum", [128, 4], F32)
    rtmp = sbt("rtmp", [128, 132], F32)
    rows = sbt("rows", [1, 2048], F32)

    psA = pst("psA", [128, 2, 512])
    psB = pst("psB", [128, 2, 512])
    psC = pst("psC", [128, 2, 512])
    psD = pst("psD", [128, 512])
    psE = pst("psE", [128, 512])

    blk = nc.Block().__enter__()
    ctx.append(blk)

    SEMS = ["gi", "si", "cf", "cy", "g0", "g1", "s0", "s1", "k0", "k1",
            "ck0", "ck1", "cv0", "cv1", "ct0", "ct1",
            "cc", "gc", "pe", "ve", "ac", "sf", "pf", "zf",
            "kf", "af", "df", "cb", "rp", "q1", "q2", "q3", "q5", "tf", "bm"]
    sems = {}
    cnt = {}
    for sname in SEMS:
        sems[sname] = nc.semaphore(sname).__enter__()
        cnt[sname] = 0

    def inc(inst, name, n=1):
        inst.then_inc(sems[name], n)
        cnt[name] += n
        return cnt[name]

    def W(eng, name, v):
        if v > 0:
            eng.wait_ge(sems[name], v)

    def vchain(v, inst):
        t = inc(inst, "ve")
        v.wait_ge(sems["ve"], t)
        return t

    tok = {}

    def ph0_gp(g):
        inc(g.memset(ones_k[:], 1.0), "gc")
        inc(g.memset(ones_b[:], 1.0), "gc")
        inc(g.memset(ones_k32[:], 1.0), "gc")
        inc(g.memset(ones_b32[:], 1.0), "gc")
        inc(g.memset(msum[:], 0.0), "gc")
        inc(g.memset(rows[:], 0.0), "gc")
        tok["consts"] = cnt["gc"]
        inc(g.dma_start(
            out=masks[:], in_=maskt.rearrange("(m p) q -> p m q", p=128)), "gi", 16)
        inc(g.dma_start(out=cosf[:], in_=ropec), "gi", 16)
        inc(g.dma_start(out=sinf[:], in_=ropes), "gi", 16)
        inc(g.dma_start(out=h128[:], in_=h128t), "gi", 16)
        inc(g.dma_start(out=f_in[0:D, :], in_=fct), "gi", 16)
        g.wait_ge(sems["gc"], tok["consts"])
        inc(g.dma_start(out=f_in[D:D + 1, 4:D],
                        in_=rows.ap()[0:1, 4:D]), "gi", 16)
        inc(g.dma_start(
            out=stream[:], in_=x_fm.rearrange("(c p) t -> p c t", p=128)),
            "gi", 16)
        tok["gi_all"] = cnt["gi"]
    blk.gpsimd(ph0_gp)

    def ph0_sy(s_):
        inc(s_.dma_start(out=bias_sb[:], in_=bias0), "si", 16)
        tok["si_all"] = cnt["si"]
    blk.sync(ph0_sy)

    t_slabs = []
    GCH = (FF // NCORES) // 512
    for r in range(D // 128):
        for ch in range(GCH):
            t_slabs.append((tg_sl[r * 128:(r + 1) * 128,
                                  ch * 512:(ch + 1) * 512], 0, 512))
    for r in range(D // 128):
        for ch in range(GCH):
            t_slabs.append((tu_sl[r * 128:(r + 1) * 128,
                                  ch * 512:(ch + 1) * 512], 1, 512))
    for r in range(FF // 128):
        t_slabs.append((td_sl[r * 128:(r + 1) * 128, :], 2, D // NCORES))
    NSLAB = len(t_slabs)
    ph1_sb = [cnt["s0"], cnt["s1"]]

    def ph1_sy(s_):
        for i, (src, mi, cols) in enumerate(t_slabs):
            if i >= 2:
                W(s_, "tf", i - 1)
            inc(s_.dma_start(out=tstage[:, i % 2, 0:cols], in_=src),
                "s%d" % (i % 2), 16)
    blk.sync(ph1_sy)

    def ph1_ve(v):
        W(v, "gc", tok["consts"])
        for i, (src, mi, cols) in enumerate(t_slabs):
            W(v, "s%d" % (i % 2), ph1_sb[i % 2] + 16 * (i // 2 + 1))
            inc(v.tensor_reduce(rtmp[:, i:i + 1], tstage[:, i % 2, 0:cols],
                                AX.X, A.add, apply_absolute_value=True), "tf")
        W(v, "tf", NSLAB)
        gN = (D // 128) * GCH
        vchain(v, v.tensor_reduce(msum[:, 0:1], rtmp[:, 0:gN], AX.X, A.add))
        vchain(v, v.tensor_reduce(msum[:, 1:2], rtmp[:, gN:2 * gN], AX.X, A.add))
        vchain(v, v.tensor_reduce(msum[:, 2:3],
                                  rtmp[:, 2 * gN:2 * gN + FF // 128],
                                  AX.X, A.add))
        tok["msum"] = cnt["ve"]
    blk.vector(ph1_ve)

    def ph1_te(t):
        W(t, "ve", tok["msum"])
        W(t, "gc", tok["consts"])
        tok["wsum_mm"] = inc(t.matmul(psE[0:1, 0:4], ones_k32[:], msum[:, 0:4],
                                      start=True, stop=True), "pe")
    blk.tensor(ph1_te)

    def ph1_ve2(v):
        W(v, "pe", tok["wsum_mm"])
        tok["scalerow"] = inc(v.tensor_copy(rows.ap()[0:1, 0:4],
                                            psE[0:1, 0:4]), "ve")
    blk.vector(ph1_ve2)

    def ph1_sc(sc):
        W(sc, "ve", tok["scalerow"])
        tok["scalest"] = inc(sc.dma_start(out=f_in[D:D + 1, 0:4],
                                          in_=rows.ap()[0:1, 0:4]), "cf", 16)
    blk.scalar(ph1_sc)

    def ph1_gp(g):
        W(g, "gi", tok["gi_all"])
        W(g, "cf", tok["scalest"])
        tok["ar"] = inc(g.collective_compute(
            "AllReduce", A.add, replica_groups=ALL8,
            ins=[f_in.opt()], outs=[f_out.opt()]), "cc")
    blk.gpsimd(ph1_gp)

    def proj(wsrc, evac_half, cast_wait=None, rhs_z=False, slab_gate=None):
        pe0 = cnt["pe"]; af0 = cnt["af"]
        gb = [cnt["g0"], cnt["g1"]]
        nts = 4 * NB_H
        nth = 16 * NB_H
        info = {"pe0": pe0, "af0": af0, "nth": nth, "cast": [0, 0], "NT": NT}
        for h in range(2):
            if not rhs_z:
                def _cast(v, h=h):
                    if cast_wait is not None:
                        cast_wait(v, h)
                    W(v, "pe", pe0 + (nth if h == 1 else 0))
                    for c in range(NC):
                        inc(v.tensor_copy(xbh[:, c, :],
                                          stream[:, c, h * TH:(h + 1) * TH]),
                            "ve")
                    info["cast"][h] = cnt["ve"]
                blk.vector(_cast)

            def _slabs(g, h=h):
                if slab_gate is not None and h == 0:
                    slab_gate(g)
                for i in range(4):
                    gi = h * 4 + i
                    W(g, "pe", pe0 + (gi - 1) * nts if gi >= 2 else pe0)
                    inc(g.dma_start(
                        out=ws[:, gi % 2, :, :],
                        in_=wsrc[:, i * 512:(i + 1) * 512]
                        .rearrange("(c p) o -> p c o", p=128)),
                        "g%d" % (gi % 2), 16)
            blk.gpsimd(_slabs)

            def _te(t, h=h):
                if not rhs_z:
                    W(t, "ve", info["cast"][h])
                else:
                    W(t, "af", af0)
                n = 0
                for i in range(4):
                    gi0 = h * 4 + i
                    W(t, "g%d" % (gi0 % 2), gb[gi0 % 2] + 16 * (gi0 // 2 + 1))
                    gi = h * 4 + i
                    for j in range(4):
                        for tb in range(NB_H):
                            gn = h * nth + n
                            W(t, "af", af0 + gn - 1 if gn >= 2 else af0)
                            last = None
                            for c in range(NC):
                                if rhs_z:
                                    rhs = QV[:, 16 + c, h * TH + tb * NT:
                                             h * TH + (tb + 1) * NT]
                                else:
                                    rhs = xbh[:, c, tb * NT:(tb + 1) * NT]
                                last = t.matmul(
                                    psA[:, gn % 2, 0:NT],
                                    ws[:, gi % 2, c, j * 128:(j + 1) * 128],
                                    rhs, start=(c == 0), stop=(c == NC - 1))
                            inc(last, "pe")
                            n += 1
            blk.tensor(_te)
            evac_half(h, info)
        return info

    def tile_oc_tb(n):
        return n // NB_H, n % NB_H

    def evac_rope_mk(dest_q, layer):
        st = {"ac0": cnt["ac"], "rp0": cnt["rp"],
              "ckb": [cnt["ck0"], cnt["ck1"]],
              "snap": {k: cnt[k] for k in
                       ("ck0", "ck1", "cv0", "cv1", "ct0", "ct1")}}

        def emit(h, info):
            ac0 = st["ac0"]; rp0 = st["rp0"]; ckb = st["ckb"]
            nth = info["nth"]; pe0 = info["pe0"]; af0 = info["af0"]

            def _sc(sc):
                def store(gn):
                    oc, tb = tile_oc_tb(gn % nth)
                    t0 = (gn // nth) * TH + tb * NT
                    W(sc, "af", af0 + gn + 1)
                    inc(sc.dma_start(
                        out=k_in[layer][oc * 128:(oc + 1) * 128, t0:t0 + NT],
                        in_=rk[:, gn % 2, 0:NT]), "ck%d" % (gn % 2), 16)
                for n in range(nth):
                    gn = h * nth + n
                    W(sc, "pe", pe0 + gn + 1)
                    if gn < 2:
                        for k2, v2 in st["snap"].items():
                            W(sc, k2, v2)
                    if gn >= 2:
                        W(sc, "rp", rp0 + gn - 1)
                    inc(sc.activation(sc512[:, gn % 2, 0:NT],
                                      psA[:, gn % 2, 0:NT], AF.Copy), "ac")
                    if not dest_q and gn >= 2:
                        store(gn - 2)
                if not dest_q and h == 1:
                    store(2 * nth - 2)
                    store(2 * nth - 1)
            blk.scalar(_sc)

            def _ve(v):
                for n in range(nth):
                    gn = h * nth + n
                    oc, tb = tile_oc_tb(n)
                    t0 = h * TH + tb * NT
                    W(v, "ac", ac0 + gn + 1)
                    if gn >= 1:
                        W(v, "af", af0 + gn)
                    if not dest_q and gn >= 2:
                        W(v, "ck%d" % (gn % 2),
                          ckb[gn % 2] + 16 * ((gn - 2) // 2 + 1))
                    inc(v.tensor_copy(rsw[0:64, 0:NT],
                                      sc512[64:128, gn % 2, 0:NT]), "ve")
                    t2 = inc(v.tensor_copy(rsw[64:128, 0:NT],
                                           sc512[0:64, gn % 2, 0:NT]), "ve")
                    v.wait_ge(sems["ve"], t2)
                    t3 = inc(v.tensor_tensor(rsw[:, 0:NT], rsw[:, 0:NT],
                                             sinf[:, t0:t0 + NT], A.mult), "ve")
                    t4 = inc(v.tensor_tensor(rk[:, gn % 2, 0:NT],
                                             sc512[:, gn % 2, 0:NT],
                                             cosf[:, t0:t0 + NT], A.mult), "rp")
                    v.wait_ge(sems["ve"], t3)
                    v.wait_ge(sems["rp"], t4)
                    dst = QV[:, oc, t0:t0 + NT] if dest_q else rk[:, gn % 2, 0:NT]
                    inc(v.tensor_tensor(dst, rk[:, gn % 2, 0:NT],
                                        rsw[:, 0:NT], A.add), "af")
            blk.vector(_ve)
        return emit

    def evac_copy_mk(layer, mode, qscale=None):
        sem_pfx = "cv" if mode == "v" else "ck"
        st = {"cb2": [cnt[sem_pfx + "0"], cnt[sem_pfx + "1"]],
              "snap": {k: cnt[k] for k in
                       ("ck0", "ck1", "cv0", "cv1", "ct0", "ct1")}}
        vv = v_in[layer].rearrange("(h t) e -> t h e", t=T)

        def emit(h, info):
            cb2 = st["cb2"]
            af0 = info["af0"]; nth = info["nth"]; pe0 = info["pe0"]
            ntw = info["NT"]

            def _sc(sc):
                def store(gn):
                    n2 = gn % nth
                    h2 = gn // nth
                    W(sc, "af", af0 + gn + 1)
                    if mode == "v":
                        i = n2 // (TH // 128)
                        tb = n2 % (TH // 128)
                        tt0 = h2 * TH + tb * 128
                        srcap = sc512.ap().rearrange(
                            "p b (hh e) -> p b hh e", e=128)[:, gn % 2, :, :]
                        inc(sc.dma_start(out=vv[tt0:tt0 + 128,
                                                4 * i:4 * i + 4, :],
                                         in_=srcap), "cv%d" % (gn % 2), 16)
                    else:
                        oc, tb = tile_oc_tb(n2)
                        tt0 = h2 * TH + tb * NT
                        inc(sc.dma_start(
                            out=k_in[layer][oc * 128:(oc + 1) * 128,
                                            tt0:tt0 + NT],
                            in_=sc512[:, gn % 2, 0:NT]), "ck%d" % (gn % 2), 16)

                for n in range(nth):
                    gn = h * nth + n
                    W(sc, "pe", pe0 + gn + 1)
                    if gn < 2 and mode != "q":
                        for k2, v2 in st["snap"].items():
                            W(sc, k2, v2)
                    if mode == "q":
                        oc, tb = tile_oc_tb(n)
                        t0 = h * TH + tb * NT
                        if qscale is None:
                            inc(sc.activation(QV[:, oc, t0:t0 + NT],
                                              psA[:, gn % 2, 0:NT],
                                              AF.Copy), "af")
                        else:
                            inc(sc.activation(QV[:, oc, t0:t0 + NT],
                                              psA[:, gn % 2, 0:NT],
                                              AF.Copy, scale=qscale), "af")
                    else:
                        if gn >= 2:
                            store(gn - 2)
                            W(sc, sem_pfx + str(gn % 2),
                              cb2[gn % 2] + 16 * ((gn - 2) // 2 + 1))
                        inc(sc.activation(sc512[:, gn % 2, 0:ntw],
                                          psA[:, gn % 2, 0:ntw], AF.Copy), "af")
                if mode != "q" and h == 1:
                    store(2 * nth - 2)
                    store(2 * nth - 1)
            blk.scalar(_sc)
        return emit

    def proj_tokmaj(wsrc, evac_half, cast_wait=None):
        pe0 = cnt["pe"]; af0 = cnt["af"]
        gb = [cnt["g0"], cnt["g1"]]
        ntb_h = TH // 128
        nth = 4 * ntb_h
        info = {"pe0": pe0, "af0": af0, "nth": nth, "cast": [0, 0], "NT": 512}
        for h in range(2):
            def _cast(v, h=h):
                if cast_wait is not None:
                    cast_wait(v, h)
                W(v, "pe", pe0 + (nth if h == 1 else 0))
                for c in range(NC):
                    inc(v.tensor_copy(xbh[:, c, :],
                                      stream[:, c, h * TH:(h + 1) * TH]), "ve")
                info["cast"][h] = cnt["ve"]
            blk.vector(_cast)

            def _slabs(g, h=h):
                for i in range(4):
                    gi = h * 4 + i
                    W(g, "pe", pe0 + (gi - 1) * ntb_h if gi >= 2 else pe0)
                    inc(g.dma_start(
                        out=ws[:, gi % 2, :, :],
                        in_=wsrc[:, i * 512:(i + 1) * 512]
                        .rearrange("(c p) o -> p c o", p=128)),
                        "g%d" % (gi % 2), 16)
            blk.gpsimd(_slabs)

            def _te(t, h=h):
                W(t, "ve", info["cast"][h])
                n = 0
                for i in range(4):
                    gi0 = h * 4 + i
                    W(t, "g%d" % (gi0 % 2), gb[gi0 % 2] + 16 * (gi0 // 2 + 1))
                    gi = h * 4 + i
                    for tb in range(ntb_h):
                        gn = h * nth + n
                        W(t, "af", af0 + gn - 1 if gn >= 2 else af0)
                        last = None
                        for c in range(NC):
                            last = t.matmul(
                                psA[:, gn % 2, 0:512],
                                xbh[:, c, tb * 128:(tb + 1) * 128],
                                ws[:, gi % 2, c, :],
                                start=(c == 0), stop=(c == NC - 1))
                        inc(last, "pe")
                        n += 1
            blk.tensor(_te)
            evac_half(h, info)
        return info

    def cw_x(v, h):
        W(v, "gi", tok["gi_all"])

    proj(wk0, evac_rope_mk(dest_q=False, layer=0), cast_wait=cw_x)
    kst0 = (cnt["ck0"], cnt["ck1"])
    proj_tokmaj(wv0, evac_copy_mk(0, "v"))
    vst0 = (cnt["cv0"], cnt["cv1"])
    proj(wq0, evac_rope_mk(dest_q=True, layer=0))
    tok["q0_af"] = cnt["af"]

    if upto == -1:
        qaf = cnt["af"]

        def _dbgst(g):
            W(g, "af", qaf)
            inc(g.dma_start(out=y_fm.rearrange("(c p) t -> p c t", p=128),
                            in_=QV[:, 0:16, :]), "gi", 16)
            W(g, "gi", cnt["gi"])
        blk.gpsimd(_dbgst)
        for t_ in reversed(ctx):
            t_.__exit__(None, None, None)
        return nc

    def ag0(g):
        W(g, "ck0", kst0[0]); W(g, "ck1", kst0[1])
        tok["agk0"] = inc(g.collective_compute(
            "AllGather", A.bypass, replica_groups=PAIRS,
            ins=[k_in[0].opt()], outs=[k_out[0].opt()]), "cc")
        W(g, "cv0", vst0[0]); W(g, "cv1", vst0[1])
        tok["agv0"] = inc(g.collective_compute(
            "AllGather", A.bypass, replica_groups=PAIRS,
            ins=[v_in[0].opt()], outs=[v_out[0].opt()]), "cc")
    blk.gpsimd(ag0)

    def ph_tq():
        def _sy(s_):
            W(s_, "cc", tok["ar"])
            inc(s_.dma_start(out=rows.ap()[0:1, 0:4],
                             in_=f_out[D:D + 1, 0:4]), "cf", 16)
            tok["scrow"] = cnt["cf"]
        blk.sync(_sy)

        def _ve(v):
            W(v, "cf", tok["scrow"])
            vchain(v, v.tensor_scalar(rows.ap()[0:1, 4:8], rows.ap()[0:1, 0:4],
                                      1.0 / NW, EPS_Q, A.mult, A.max))
            vchain(v, v.tensor_scalar(rows.ap()[0:1, 0:4], rows.ap()[0:1, 4:8],
                                      0.5, None, A.mult))
            tok["hrow"] = cnt["ve"]
        blk.vector(_ve)

        def _te(t):
            W(t, "ve", tok["hrow"])
            tok["sclmm"] = inc(t.matmul(psD[:, 0:8], ones_b32[:],
                                        rows.ap()[0:1, 0:8],
                                        start=True, stop=True), "pe")
        blk.tensor(_te)

        def _sc(sc):
            W(sc, "pe", tok["sclmm"])
            tok["sclc"] = inc(sc.activation(scl[:], psD[:, 0:8], AF.Copy), "ac")
        blk.scalar(_sc)

        offs = []
        W_G = FF // NCORES
        for r in range(D // 128):
            for ch in range(GCH):
                dst = tern_in[r * 128 * W_G:(r + 1) * 128 * W_G] \
                    .rearrange("(p o) -> p o", o=W_G)[:, ch * 512:(ch + 1) * 512]
                offs.append((tg_sl[r * 128:(r + 1) * 128,
                                   ch * 512:(ch + 1) * 512], 0, 512, dst))
        for r in range(D // 128):
            for ch in range(GCH):
                dst = tern_in[GT + r * 128 * W_G:GT + (r + 1) * 128 * W_G] \
                    .rearrange("(p o) -> p o", o=W_G)[:, ch * 512:(ch + 1) * 512]
                offs.append((tu_sl[r * 128:(r + 1) * 128,
                                   ch * 512:(ch + 1) * 512], 1, 512, dst))
        for r in range(FF // 128):
            dst = tern_in[2 * GT + r * 128 * (D // NCORES):
                          2 * GT + (r + 1) * 128 * (D // NCORES)] \
                .rearrange("(p o) -> p o", o=D // NCORES)
            offs.append((td_sl[r * 128:(r + 1) * 128, :], 2, D // NCORES, dst))
        sqb = [cnt["s0"], cnt["s1"]]
        tf0 = cnt["tf"]
        ctb = [cnt["ct0"], cnt["ct1"]]

        def _sy2(s_):
            for i, (src, mi, cols, off) in enumerate(offs):
                if i >= 2:
                    W(s_, "tf", tf0 + i - 1)
                inc(s_.dma_start(out=tstage[:, i % 2, 0:cols], in_=src),
                    "s%d" % (i % 2), 16)
        blk.sync(_sy2)

        def _ve2(v):
            W(v, "ac", tok["sclc"])
            for i, (src, mi, cols, off) in enumerate(offs):
                W(v, "s%d" % (i % 2), sqb[i % 2] + 16 * (i // 2 + 1))
                if i >= 1:
                    W(v, "tf", tf0 + i)
                if i >= 2:
                    W(v, "ct%d" % (i % 2),
                      ctb[i % 2] + 16 * ((i - 2) // 2 + 1))
                t1 = inc(v.tensor_scalar(ttmp[:, 0:cols],
                                         tstage[:, i % 2, 0:cols],
                                         -1.0, scl[:, mi:mi + 1],
                                         A.mult, A.is_gt), "ve")
                v.wait_ge(sems["ve"], t1)
                inc(v.scalar_tensor_tensor(ttrn[:, i % 2, 0:cols],
                                           tstage[:, i % 2, 0:cols],
                                           scl[:, mi:mi + 1],
                                           ttmp[:, 0:cols],
                                           A.is_gt, A.subtract), "tf")
        blk.vector(_ve2)

        def _st(sc2):
            for i, (src, mi, cols, off) in enumerate(offs):
                W(sc2, "tf", tf0 + i + 1)
                inc(sc2.dma_start(out=off,
                                  in_=ttrn[:, i % 2, 0:cols]),
                    "ct%d" % (i % 2), 16)
            tok["ternst"] = (cnt["ct0"], cnt["ct1"])
        blk.scalar(_st)

        def _gp(g):
            W(g, "ct0", tok["ternst"][0])
            W(g, "ct1", tok["ternst"][1])
            tok["agt"] = inc(g.collective_compute(
                "AllGather", A.bypass, replica_groups=ALL8,
                ins=[tern_in.opt()], outs=[tern_out.opt()]), "cc")
        blk.gpsimd(_gp)
    ph_tq()

    def attn(layer):
        tiles = []
        hg_sizes = []
        for h in range(NH):
            for g in range(NQ):
                lst = []
                for j in range(2 * g):
                    for r in range(2):
                        lst.append((r, j, None))
                for m in range(4):
                    lst.append((m % 2, 2 * g + m // 2, m))
                hg_sizes.append(len(lst))
                for it, (r, j, bm) in enumerate(lst):
                    tiles.append(dict(h=h, g=g, r=r, j=j, bm=bm,
                                      first=(it == 0),
                                      last=(it == len(lst) - 1),
                                      hg=h * NQ + g))
        for i, td in enumerate(tiles):
            td["ti"] = i
        NHG = NH * NQ
        thru = np.cumsum(hg_sizes).tolist()
        band_pre = []
        nb = 0
        for td in tiles:
            if td["bm"] is not None:
                nb += 1
            band_pre.append(nb)
        tiles_by_hg = [[] for _ in range(NHG)]
        for td in tiles:
            tiles_by_hg[td["hg"]].append(td)

        pe0 = cnt["pe"]; sf0 = cnt["sf"]; pf0 = cnt["pf"]; zf0 = cnt["zf"]
        ac0 = cnt["ac"]; bm0 = cnt["bm"]; df0 = cnt["df"]; af0 = cnt["af"]
        rp0 = cnt["rp"]; q10 = cnt["q1"]; q20 = cnt["q2"]; q30 = cnt["q3"]
        q50 = cnt["q5"]; cb0 = cnt["cb"]
        kvb = [cnt["k0"], cnt["k1"]]
        q_af = tok["q%d_af" % layer]
        stab_n = 3 * KBq

        def _sy(s_):
            W(s_, "cc", tok["agk%d" % layer])
            W(s_, "cc", tok["agv%d" % layer])
            for h in range(NH):
                if h >= 2:
                    W(s_, "pf", pf0 + thru[(h - 1) * NQ - 1])
                for r in range(2):
                    inc(s_.dma_start(
                        out=kd[:, h % 2, r, :],
                        in_=k_out[layer][r * D + h * 128:
                                         r * D + (h + 1) * 128, :]),
                        "k%d" % (h % 2), 16)
                    inc(s_.dma_start(
                        out=vd[:, h % 2, r * NTB:(r + 1) * NTB, :],
                        in_=v_out[layer][(r * NH + h) * T:
                                         (r * NH + h + 1) * T, :]
                        .rearrange("(b p) e -> p b e", p=128)),
                        "k%d" % (h % 2), 16)
        blk.sync(_sy)

        def kv_wait(eng, h):
            W(eng, "k%d" % (h % 2), kvb[h % 2] + 64 * (h // 2 + 1))

        def q2_of_cq(hg):
            return q20 + (1 if hg == 0 else 2 * hg)

        def q2_of_rb(hg):
            if layer == 0:
                return q20 + hg + 1
            NHGl = NH * NQ
            return q20 + (2 * NHGl if hg == NHGl - 1 else 2 * hg + 3)

        def _sc(sc):
            for h in range(NH):
                if layer == 1:
                    for r in range(2):
                        for b in range(KBq):
                            i = r * KBq + b
                            kv_wait(sc, h)
                            if h > 0 or i >= 2:
                                W(sc, "cb", cb0 + stab_n * h + i - 1)
                            inc(sc.activation(
                                sc512[:, i % 2, 0:NKB],
                                kd[:, h % 2, r, b * NKB:(b + 1) * NKB],
                                AF.Square), "q1")
                    for b in range(KBq):
                        i = 2 * KBq + b
                        W(sc, "af", q_af)
                        if h > 0 or i >= 2:
                            W(sc, "cb", cb0 + stab_n * h + i - 1)
                        inc(sc.activation(sc512[:, i % 2, 0:NKB],
                                          QV[:, h, b * NKB:(b + 1) * NKB],
                                          AF.Square), "q1")
                for g in range(NQ):
                    hg = h * NQ + g
                    if layer == 1:
                        W(sc, "q2", q2_of_cq(hg))
                        if hg >= 2:
                            W(sc, "sf", sf0 + thru[hg - 2])
                        inc(sc.activation(cbc[:, (hg % 2) * QS:(hg % 2 + 1) * QS],
                                          psD[:, 0:QS], AF.Sqrt), "rp")
                    for td in tiles_by_hg[hg]:
                        ti = td["ti"]
                        if layer == 1:
                            W(sc, "sf", sf0 + ti + 1)
                            if ti >= 2:
                                W(sc, "pf", pf0 + ti - 1)
                            inc(sc.activation(pt[:, ti % 2, :],
                                              qtmp[:, ti % 2, 0:QS], AF.Exp),
                                "ac")
                        else:
                            W(sc, "pe", pe0 + ti + 1)
                            if ti >= 2:
                                W(sc, "ac", ac0 + ti - 1)
                            inc(sc.activation(qtmp[:, ti % 2, 0:QS],
                                              psB[:, ti % 2, 0:QS],
                                              AF.Square), "sf")
                            if td["bm"] is not None:
                                W(sc, "bm", bm0 + band_pre[ti])
                            else:
                                W(sc, "sf", sf0 + ti + 1)
                            if ti >= 2:
                                W(sc, "pf", pf0 + ti - 1)
                            inc(sc.activation(pt[:, ti % 2, :],
                                              qtmp[:, ti % 2, 0:QS],
                                              AF.Square), "ac")
                    if hg >= 1:
                        W(sc, "q2", q2_of_rb(hg - 1))
                        if hg >= 3:
                            W(sc, "af", af0 + hg - 2)
                        inc(sc.activation(
                            cbc[:, 0:QS] if False else
                            rbc[:, ((hg - 1) % 2) * QS:((hg - 1) % 2 + 1) * QS],
                            psE[:, 0:QS], AF.Copy), "q3")
            W(sc, "q2", q2_of_rb(NHG - 1))
            W(sc, "af", af0 + NHG - 2)
            inc(sc.activation(rbc[:, ((NHG - 1) % 2) * QS:
                                   ((NHG - 1) % 2 + 1) * QS],
                              psE[:, 0:QS], AF.Copy), "q3")
        blk.scalar(_sc)

        def _ve(v):
            W(v, "gi", tok["gi_all"])
            for h in range(NH):
                if layer == 1:
                    for i in range(stab_n):
                        W(v, "cb", cb0 + stab_n * h + i + 1)
                        if i < 2 * KBq:
                            inc(v.tensor_reduce(
                                rows.ap()[0:1, 1536 + i:1537 + i],
                                psE[0:1, 0:NKB], AX.X, A.max), "q5")
                        else:
                            dst = rows.ap()[0:1, (i - 2 * KBq) * NKB:
                                            (i - 2 * KBq + 1) * NKB]
                            inc(v.tensor_copy(dst, psE[0:1, 0:NKB]), "q5")
                    W(v, "q5", q50 + (stab_n + 1) * h + stab_n)
                    vchain(v, v.tensor_reduce(rows.ap()[0:1, 1552:1553],
                                              rows.ap()[0:1, 1536:1536 + 2 * KBq],
                                              AX.X, A.max))
                    with nc.allow_low_precision(reason="bf16 c_q row"):
                        t5 = inc(v.tensor_scalar(browz.ap()[0:1, 0:T],
                                                 rows.ap()[0:1, 0:T],
                                                 rows.ap()[0:1, 1552:1553],
                                                 None, A.mult), "q5")
                    v.wait_ge(sems["q5"], t5)
                for g in range(NQ):
                    hg = h * NQ + g
                    if hg >= 1:
                        phg = hg - 1
                        W(v, "pf", pf0 + thru[phg])
                        if phg >= 1:
                            W(v, "q2", q2_of_rb(phg - 1))
                        if layer == 0:
                            t6 = inc(v.tensor_scalar(
                                rows.ap()[0:1, 1024:1024 + QS],
                                psA[0:1, phg % 2, 0:QS],
                                EPS_L0, None, A.max), "ve")
                        else:
                            t6 = inc(v.tensor_copy(
                                rows.ap()[0:1, 1024:1024 + QS],
                                psA[0:1, phg % 2, 0:QS]), "ve")
                        v.wait_ge(sems["ve"], t6)
                        t7 = inc(v.reciprocal(rows.ap()[0:1, 1280:1280 + QS],
                                              rows.ap()[0:1, 1024:1024 + QS]),
                                 "ve")
                        v.wait_ge(sems["ve"], t7)
                        inc(v.tensor_copy(browz.ap()[0:1, 1024:1024 + QS],
                                          rows.ap()[0:1, 1280:1280 + QS]),
                            "df")
                    if hg >= 2:
                        zhg = hg - 2
                        zh, zg = zhg // NQ, zhg % NQ
                        W(v, "zf", zf0 + thru[zhg])
                        W(v, "q3", q30 + zhg + 1)
                        inc(v.tensor_tensor(
                            QV[:, 16 + zh, zg * QS:(zg + 1) * QS],
                            psC[:, zhg % 2, 0:QS],
                            rbc[:, (zhg % 2) * QS:(zhg % 2 + 1) * QS],
                            A.mult), "af")
                    for td in tiles_by_hg[hg]:
                        ti = td["ti"]
                        if layer == 1:
                            W(v, "pe", pe0 + ti + 1)
                            W(v, "rp", rp0 + hg + 1)
                            if ti >= 2:
                                W(v, "ac", ac0 + ti - 1)
                            inc(v.tensor_tensor(
                                qtmp[:, ti % 2, 0:QS], psB[:, ti % 2, 0:QS],
                                cbc[:, (hg % 2) * QS:(hg % 2 + 1) * QS],
                                A.subtract), "sf")
                            if td["bm"] is not None:
                                W(v, "ac", ac0 + ti + 1)
                                inc(v.tensor_tensor(pt[:, ti % 2, :],
                                                    pt[:, ti % 2, :],
                                                    masks[:, td["bm"], :],
                                                    A.mult), "bm")
                        else:
                            if td["bm"] is not None:
                                W(v, "sf", sf0 + ti + 1)
                                inc(v.tensor_tensor(qtmp[:, ti % 2, 0:QS],
                                                    qtmp[:, ti % 2, 0:QS],
                                                    masks[:, td["bm"], :],
                                                    A.mult), "bm")
            phg = NHG - 1
            W(v, "pf", pf0 + thru[phg])
            W(v, "q2", q2_of_rb(phg - 1))
            if layer == 0:
                t6 = inc(v.tensor_scalar(rows.ap()[0:1, 1024:1024 + QS],
                                         psA[0:1, phg % 2, 0:QS],
                                         EPS_L0, None, A.max), "ve")
            else:
                t6 = inc(v.tensor_copy(rows.ap()[0:1, 1024:1024 + QS],
                                       psA[0:1, phg % 2, 0:QS]), "ve")
            v.wait_ge(sems["ve"], t6)
            t7 = inc(v.reciprocal(rows.ap()[0:1, 1280:1280 + QS],
                                  rows.ap()[0:1, 1024:1024 + QS]), "ve")
            v.wait_ge(sems["ve"], t7)
            inc(v.tensor_copy(browz.ap()[0:1, 1024:1024 + QS],
                              rows.ap()[0:1, 1280:1280 + QS]), "df")
            for zhg in (NHG - 2, NHG - 1):
                zh, zg = zhg // NQ, zhg % NQ
                W(v, "zf", zf0 + thru[zhg])
                W(v, "q3", q30 + zhg + 1)
                inc(v.tensor_tensor(QV[:, 16 + zh, zg * QS:(zg + 1) * QS],
                                    psC[:, zhg % 2, 0:QS],
                                    rbc[:, (zhg % 2) * QS:(zhg % 2 + 1) * QS],
                                    A.mult), "af")
        blk.vector(_ve)

        def _te(t):
            W(t, "af", q_af)
            for h in range(NH):
                kv_wait(t, h)
                if layer == 1:
                    for i in range(stab_n):
                        W(t, "q1", q10 + stab_n * h + i + 1)
                        if h > 0 or i > 0:
                            W(t, "q5", q50 + (stab_n + 1) * h + i)
                        W(t, "q3", q30 + h * NQ - 1 if h > 0 else 0)
                        inc(t.matmul(psE[0:1, 0:NKB], ones_k[:],
                                     sc512[:, i % 2, 0:NKB],
                                     start=True, stop=True), "cb")
                for g in range(NQ):
                    hg = h * NQ + g
                    if layer == 1:
                        W(t, "q5", q50 + (stab_n + 1) * (h + 1))
                        if hg >= 1:
                            W(t, "rp", rp0 + hg)
                        inc(t.matmul(psD[:, 0:QS], ones_b[:],
                                     browz.ap()[0:1, g * QS:(g + 1) * QS],
                                     start=True, stop=True), "q2")
                    for td in tiles_by_hg[hg]:
                        ti = td["ti"]
                        W(t, "sf", sf0 + ti - 1 if ti >= 2 else sf0)
                        inc(t.matmul(psB[:, ti % 2, 0:QS],
                                     kd[:, h % 2, td["r"],
                                        td["j"] * 128:(td["j"] + 1) * 128],
                                     QV[:, h, g * QS:(g + 1) * QS],
                                     start=True, stop=True), "pe")
                        if layer == 1 and td["bm"] is not None:
                            W(t, "bm", bm0 + band_pre[ti])
                        else:
                            W(t, "ac", ac0 + ti + 1)
                        if td["first"] and hg >= 2:
                            W(t, "af", af0 + hg - 1)
                        inc(t.matmul(psC[:, hg % 2, 0:QS],
                                     vd[:, h % 2,
                                        td["r"] * NTB + td["j"], :],
                                     pt[:, ti % 2, :],
                                     start=td["first"], stop=td["last"]), "zf")
                        W(t, "zf", zf0 + ti + 1)
                        if td["first"] and hg >= 2:
                            W(t, "df", df0 + hg - 1)
                        inc(t.matmul(psA[0:1, hg % 2, 0:QS], ones_k[:],
                                     pt[:, ti % 2, :],
                                     start=td["first"], stop=td["last"]), "pf")
                    if hg >= 1:
                        W(t, "df", df0 + hg)
                        if hg >= 2:
                            W(t, "q3", q30 + hg - 1)
                        inc(t.matmul(psE[:, 0:QS], ones_b[:],
                                     browz.ap()[0:1, 1024:1024 + QS],
                                     start=True, stop=True), "q2")
            W(t, "df", df0 + NHG)
            W(t, "q3", q30 + NHG - 1)
            inc(t.matmul(psE[:, 0:QS], ones_b[:],
                         browz.ap()[0:1, 1024:1024 + QS],
                         start=True, stop=True), "q2")
        blk.tensor(_te)
        tok["attn%d_af" % layer] = cnt["af"]
        tok["attn%d_zf" % layer] = cnt["zf"]


    attn(0)

    if upto == -3:
        zaf = cnt["af"]

        def _dbgst3(g):
            W(g, "af", zaf)
            inc(g.dma_start(out=y_fm.rearrange("(c p) t -> p c t", p=128),
                            in_=QV[:, 16:32, :]), "gi", 16)
            W(g, "gi", cnt["gi"])
        blk.gpsimd(_dbgst3)
        for t_ in reversed(ctx):
            t_.__exit__(None, None, None)
        return nc
    if upto == -4:
        zaf = cnt["af"]

        def _dbgst4(g):
            W(g, "af", zaf)
            inc(g.dma_start(out=y_fm.rearrange("(c p) t -> p c t", p=128),
                            in_=QV[:, 0:16, :]), "gi", 16)
            W(g, "gi", cnt["gi"])
        blk.gpsimd(_dbgst4)
        for t_ in reversed(ctx):
            t_.__exit__(None, None, None)
        return nc

    def mk_castwait(sem_name):
        snap = cnt[sem_name]
        def f(v, h):
            W(v, sem_name, snap)
        return f

    def _finish():
        def fin(sc):
            W(sc, "cy", tok["yst1"])
        blk.scalar(fin)

    def _close(nc_):
        for t in reversed(ctx):
            t.__exit__(None, None, None)
        return nc_

    def evac_res_half(h, info):
        nth = info["nth"]; pe0 = info["pe0"]

        def _ve(v):
            for n in range(nth):
                gn = h * nth + n
                oc, tb = tile_oc_tb(n)
                t0 = h * TH + tb * NT
                W(v, "pe", pe0 + gn + 1)
                inc(v.scalar_tensor_tensor(
                    stream[:, oc, t0:t0 + NT], psA[:, gn % 2, 0:NT], 1.0,
                    stream[:, oc, t0:t0 + NT], A.mult, A.add), "af")
        blk.vector(_ve)

    proj(wo0, evac_res_half, rhs_z=True)
    if upto == -6:
        paf = cnt["af"]

        def _dbgst6(g):
            W(g, "af", paf)
            inc(g.dma_start(out=y_fm.rearrange("(c p) t -> p c t", p=128),
                            in_=QV[:, 16:32, :]), "gi", 16)
            W(g, "gi", cnt["gi"])
        blk.gpsimd(_dbgst6)
        for t_ in reversed(ctx):
            t_.__exit__(None, None, None)
        return nc
    if upto == -5:
        paf = cnt["af"]

        def _dbgst5(g):
            W(g, "af", paf)
            inc(g.dma_start(out=y_fm.rearrange("(c p) t -> p c t", p=128),
                            in_=stream[:]), "gi", 16)
            W(g, "gi", cnt["gi"])
        blk.gpsimd(_dbgst5)
        for t_ in reversed(ctx):
            t_.__exit__(None, None, None)
        return nc

    def layernorm(store_half=False, limit=99, halves=2):
        for h in range(halves):
            af_in = cnt["af"]
            ve0 = cnt["ve"]; ac0 = cnt["ac"]; pe0 = cnt["pe"]

            def _cast(v, h=h):
                W(v, "af", af_in)
                for c in range(NC):
                    inc(v.tensor_copy(xbh[:, c, :],
                                      stream[:, c, h * TH:(h + 1) * TH]), "ve")
            blk.vector(_cast)
            cast_tok = cnt["ve"]

            def _sq(sc, h=h):
                W(sc, "ve", cast_tok)
                for c in range(NC):
                    if c >= 2:
                        W(sc, "pe", pe0 + 2 * (c - 1))
                    inc(sc.activation(sc512[:, c % 2, 0:TH], xbh[:, c, :],
                                      AF.Square), "ac")
            if limit < 1:
                continue
            blk.scalar(_sq)

            def _mm(t, h=h):
                W(t, "ve", cast_tok)
                for c in range(NC):
                    inc(t.matmul(psD[0:1, 0:TH], ones_k[:], xbh[:, c, :],
                                 start=(c == 0), stop=(c == NC - 1)), "pe")
                    W(t, "ac", ac0 + c + 1)
                    inc(t.matmul(psE[0:1, 0:TH], ones_k[:],
                                 sc512[:, c % 2, 0:TH],
                                 start=(c == 0), stop=(c == NC - 1)), "pe")
            if limit < 2:
                continue
            blk.tensor(_mm)
            mm_tok = cnt["pe"]

            def _rows(v, h=h):
                W(v, "pe", mm_tok)
                vchain(v, v.tensor_scalar(rows.ap()[0:1, 0:TH], psD[0:1, 0:TH],
                                          1.0 / D, None, A.mult))
                vchain(v, v.tensor_scalar(rows.ap()[0:1, 512:512 + TH],
                                          psE[0:1, 0:TH], 1.0 / D, None,
                                          A.mult))
                vchain(v, v.tensor_tensor(rows.ap()[0:1, 1024:1024 + TH],
                                          rows.ap()[0:1, 0:TH],
                                          rows.ap()[0:1, 0:TH], A.mult))
                vchain(v, v.tensor_tensor(rows.ap()[0:1, 512:512 + TH],
                                          rows.ap()[0:1, 512:512 + TH],
                                          rows.ap()[0:1, 1024:1024 + TH],
                                          A.subtract))
                inc(v.tensor_scalar(rows.ap()[0:1, 512:512 + TH],
                                    rows.ap()[0:1, 512:512 + TH],
                                    EPS_LN, None, A.add), "df")
            if limit < 3:
                continue
            blk.vector(_rows)
            var_tok = cnt["df"]

            def _sqrt(sc, h=h):
                W(sc, "df", var_tok)
                inc(sc.activation(rows.ap()[0:1, 1024:1024 + TH],
                                  rows.ap()[0:1, 512:512 + TH], AF.Sqrt),
                    "q1")
            if limit < 4:
                continue
            blk.scalar(_sqrt)
            sd_tok = cnt["q1"]

            def _ab(v, h=h):
                W(v, "q1", sd_tok)
                vchain(v, v.reciprocal(rows.ap()[0:1, 1536:1536 + TH],
                                       rows.ap()[0:1, 1024:1024 + TH]))
                vchain(v, v.tensor_tensor(rows.ap()[0:1, 0:TH],
                                          rows.ap()[0:1, 0:TH],
                                          rows.ap()[0:1, 1536:1536 + TH],
                                          A.mult))
                inc(v.tensor_scalar(rows.ap()[0:1, 0:TH],
                                    rows.ap()[0:1, 0:TH],
                                    -1.0, None, A.mult), "q3")
            if limit < 5:
                continue
            blk.vector(_ab)
            ab_tok = cnt["q3"]

            def _bc(t, h=h):
                W(t, "q3", ab_tok)
                inc(t.matmul(psD[:, 0:TH], ones_b32[:],
                             rows.ap()[0:1, 1536:1536 + TH],
                             start=True, stop=True), "pe")
                inc(t.matmul(psE[:, 0:TH], ones_b32[:],
                             rows.ap()[0:1, 0:TH],
                             start=True, stop=True), "pe")
            if limit < 6:
                continue
            blk.tensor(_bc)
            bc_tok = cnt["pe"]

            def _norm(v, h=h):
                W(v, "pe", bc_tok)
                for c in range(NC):
                    t1 = inc(v.tensor_tensor(qtmp[:, c % 2, 0:TH],
                                             stream[:, c, h * TH:(h + 1) * TH],
                                             psD[:, 0:TH], A.mult), "ve")
                    if c >= 2:
                        pass
                    v.wait_ge(sems["ve"], t1)
                    inc(v.tensor_tensor(stream[:, c, h * TH:(h + 1) * TH],
                                        qtmp[:, c % 2, 0:TH],
                                        psE[:, 0:TH], A.add), "af")
            if limit < 7:
                continue
            blk.vector(_norm)
            norm_tok = cnt["af"]

            if store_half:
                def _yst(g2, h=h):
                    W(g2, "af", norm_tok)
                    inc(g2.dma_start(
                        out=y_fm[:, h * TH:(h + 1) * TH]
                        .rearrange("(c p) t -> p c t", p=128),
                        in_=stream[:, :, h * TH:(h + 1) * TH]), "cy", 16)
                    tok["yst%d" % h] = cnt["cy"]
                blk.gpsimd(_yst)

    layernorm(store_half=(upto == 1))
    if upto == 1:
        _finish()
        return _close(nc)

    def ffn0_gate(g):
        W(g, "cc", tok["ar"])

    ffn0_q3 = {"q30": None}

    def evac_ffn0_half(h, info):
        nth = info["nth"]; pe0 = info["pe0"]
        if ffn0_q3["q30"] is None:
            ffn0_q3["q30"] = cnt["q3"]
        q30 = ffn0_q3["q30"]
        af0 = info["af0"]

        def _ve(v):
            for n in range(nth):
                gn = h * nth + n
                oc, tb = tile_oc_tb(n)
                t0 = h * TH + tb * NT
                W(v, "pe", pe0 + gn + 1)
                if gn >= 2:
                    W(v, "af", af0 + gn - 1)
                inc(v.scalar_tensor_tensor(
                    qtmp[:, gn % 2, 0:NT], psA[:, gn % 2, 0:NT],
                    INV_SQRT8, stream[:, oc, t0:t0 + NT],
                    A.mult, A.add), "q3")
        blk.vector(_ve)

        def _sc(sc):
            W(sc, "si", tok["si_all"])
            for n in range(nth):
                gn = h * nth + n
                oc, tb = tile_oc_tb(n)
                t0 = h * TH + tb * NT
                W(sc, "q3", q30 + gn + 1)
                inc(sc.activation(stream[:, oc, t0:t0 + NT],
                                  qtmp[:, gn % 2, 0:NT], AF.Identity,
                                  bias=bias_sb[:, oc:oc + 1]), "af")
        blk.scalar(_sc)

    proj(f_out[0:D, :], evac_ffn0_half, cast_wait=mk_castwait("af"),
         slab_gate=ffn0_gate)
    layernorm(store_half=(upto == 2))
    if upto == 2:
        _finish()
        return _close(nc)

    proj(wk1, evac_copy_mk(1, "k"), cast_wait=mk_castwait("af"))
    kst1 = (cnt["ck0"], cnt["ck1"])
    proj_tokmaj(wv1, evac_copy_mk(1, "v"))
    vst1 = (cnt["cv0"], cnt["cv1"])
    proj(wq1, evac_copy_mk(1, "q", qscale=1.0 / math.sqrt(HD)))
    tok["q1_af"] = cnt["af"]

    def ag1(g):
        W(g, "ck0", kst1[0]); W(g, "ck1", kst1[1])
        tok["agk1"] = inc(g.collective_compute(
            "AllGather", A.bypass, replica_groups=PAIRS,
            ins=[k_in[1].opt()], outs=[k_out[1].opt()]), "cc")
        W(g, "cv0", vst1[0]); W(g, "cv1", vst1[1])
        tok["agv1"] = inc(g.collective_compute(
            "AllGather", A.bypass, replica_groups=PAIRS,
            ins=[v_in[1].opt()], outs=[v_out[1].opt()]), "cc")
    blk.gpsimd(ag1)

    attn(1)

    proj(wo1, evac_res_half, rhs_z=True)
    layernorm(store_half=(upto == 3))
    if upto == 3:
        _finish()
        return _close(nc)

    def ffn1_half(h, dbg=0):
        af_in = cnt["af"]

        def _cast(v):
            W(v, "af", af_in)
            for c in range(NC):
                inc(v.tensor_copy(xbh[:, c, :],
                                  stream[:, c, h * TH:(h + 1) * TH]), "ve")
        blk.vector(_cast)
        cast_tok = cnt["ve"]
        ac0 = cnt["ac"]

        def _h128(t):
            W(t, "ve", cast_tok)
            W(t, "gi", tok["gi_all"])
            for c in range(NC):
                if c >= 2:
                    W(t, "ac", ac0 + c - 1)
                inc(t.matmul(psB[:, c % 2, 0:TH], h128[:], xbh[:, c, :],
                             start=True, stop=True), "pe")
        blk.tensor(_h128)
        h128_pe = cnt["pe"] - NC

        def _h128ev(sc):
            for c in range(NC):
                W(sc, "pe", h128_pe + c + 1)
                inc(sc.activation(HID[:, c, :], psB[:, c % 2, 0:TH],
                                  AF.Copy), "ac")
        blk.scalar(_h128ev)
        h128_ac = cnt["ac"]

        def _h16(v):
            W(v, "ac", h128_ac)
            W(v, "tf", cnt["tf"])
            for s in (1, 2, 4, 8):
                for i in range(NC):
                    if i & s:
                        continue
                    a, b = i, i + s
                    t1 = inc(v.tensor_copy(ttmp[:, 0:TH], HID[:, a, :]), "ve")
                    v.wait_ge(sems["ve"], t1)
                    t2 = inc(v.tensor_tensor(HID[:, a, :], HID[:, a, :],
                                             HID[:, b, :], A.add), "ve")
                    v.wait_ge(sems["ve"], t2)
                    t3 = inc(v.tensor_tensor(HID[:, b, :], ttmp[:, 0:TH],
                                             HID[:, b, :], A.subtract), "ve")
                    v.wait_ge(sems["ve"], t3)
            t1 = inc(v.tensor_tensor(qtmp[:, 0, 0:TH], HID[:, 0, :],
                                     HID[:, 1, :], A.max), "ve")
            t2 = inc(v.tensor_tensor(qtmp[:, 1, 0:TH], HID[:, 0, :],
                                     HID[:, 1, :], A.min), "ve")
            v.wait_ge(sems["ve"], t2)
            for c in range(2, NC):
                t1 = inc(v.tensor_tensor(qtmp[:, 0, 0:TH], qtmp[:, 0, 0:TH],
                                         HID[:, c, :], A.max), "ve")
                t2 = inc(v.tensor_tensor(qtmp[:, 1, 0:TH], qtmp[:, 1, 0:TH],
                                         HID[:, c, :], A.min), "ve")
                v.wait_ge(sems["ve"], t2)
            t1 = inc(v.scalar_tensor_tensor(qtmp[:, 0, 0:TH], qtmp[:, 1, 0:TH],
                                            -1.0, qtmp[:, 0, 0:TH],
                                            A.mult, A.max), "ve")
            v.wait_ge(sems["ve"], t1)
            tok["fold1"] = cnt["ve"]
        blk.vector(_h16)

        def _par(g):
            W(g, "ve", tok["fold1"])
            tok["par1"] = inc(g.reduce_max(rows.ap()[0:1, 1536:1536 + TH],
                                           qtmp[:, 0, 0:TH], AX.C), "gc")
        blk.gpsimd(_par)

        def _rows1(v):
            W(v, "gc", tok["par1"])
            vchain(v, v.tensor_scalar(rows.ap()[0:1, 0:TH],
                                      rows.ap()[0:1, 1536:1536 + TH],
                                      1.0 / math.sqrt(D), EPS_Q,
                                      A.mult, A.max))
            vchain(v, v.reciprocal(rows.ap()[0:1, 512:512 + TH],
                                   rows.ap()[0:1, 0:TH]))
            vchain(v, v.tensor_scalar(rows.ap()[0:1, 512:512 + TH],
                                      rows.ap()[0:1, 512:512 + TH],
                                      127.0 / math.sqrt(D), None, A.mult))
            inc(v.tensor_scalar(rows.ap()[0:1, 1024:1024 + TH],
                                rows.ap()[0:1, 0:TH],
                                1.0 / 127.0, None, A.mult), "q3")
        blk.vector(_rows1)
        r1_tok = cnt["q3"]

        def _bc1(t):
            W(t, "q3", r1_tok)
            inc(t.matmul(psD[:, 0:TH], ones_b32[:],
                         rows.ap()[0:1, 512:512 + TH],
                         start=True, stop=True), "pe")
            inc(t.matmul(psE[:, 0:TH], ones_b32[:],
                         rows.ap()[0:1, 1024:1024 + TH],
                         start=True, stop=True), "pe")
        blk.tensor(_bc1)
        bc1_tok = cnt["pe"]

        def _cp1(sc):
            W(sc, "pe", bc1_tok)
            t1 = inc(sc.activation(cbc[:, 0:TH], psD[:, 0:TH], AF.Copy), "ac")
            inc(sc.activation(rbc[:, 0:TH], psE[:, 0:TH], AF.Copy), "ac")
        blk.scalar(_cp1)
        cp1_tok = cnt["ac"]

        def _round1(v):
            W(v, "ac", cp1_tok)
            tfb = cnt["tf"]
            for c in range(NC):
                if c >= 1:
                    W(v, "tf", tfb + c)
                t1 = inc(v.tensor_tensor(qtmp[:, 0, 0:TH], HID[:, c, :],
                                         cbc[:, 0:TH], A.mult), "ve")
                v.wait_ge(sems["ve"], t1)
                t2 = inc(v.tensor_scalar(qint[:, 0:TH], qtmp[:, 0, 0:TH],
                                         256.5, None, A.add), "ve")
                v.wait_ge(sems["ve"], t2)
                inc(v.scalar_tensor_tensor(xq1[:, c, :], qint[:, 0:TH],
                                           -256.0, rbc[:, 0:TH],
                                           A.add, A.mult), "tf")
        blk.vector(_round1)
        xq1_tok = cnt["tf"]

        if dbg == -20:
            def _d20(g):
                W(g, "tf", xq1_tok)
                inc(g.dma_start(out=y_fm[:, 0:TH]
                                .rearrange("(c p) t -> p c t", p=128),
                                in_=xq1[:]), "gi", 16)
                W(g, "gi", cnt["gi"])
            blk.gpsimd(_d20)
            return

        ssb = [cnt["s0"], cnt["s1"]]
        pe2 = cnt["pe"]; zf2 = cnt["zf"]
        q12 = cnt["q1"]; q32 = cnt["q3"]
        ngrp = (FF // NCORES // 512) * NCORES // 1
        NGRP = NCORES * 2

        def gate_off(r):
            return r * TERN

        def _syg(s_):
            W(s_, "cc", tok["agt"])
            nsl = 0
            for r in range(NCORES):
                for sub in range(2):
                    for mat, moff in ((0, 0), (1, GT)):
                        if nsl >= 2:
                            pg, pm = (nsl - 2) // 2, (nsl - 2) % 2
                            W(s_, "pe" if pm == 0 else "zf",
                              (pe2 if pm == 0 else zf2) + 4 * (pg + 1))
                        else:
                            W(s_, "pe", pe2)
                        base = r * TERN + moff
                        src = tern_out[base:base + GT] \
                            .rearrange("(c p o) -> p c o", p=128, o=1024)
                        inc(s_.dma_start(out=ws[:, nsl % 2, :, :],
                                         in_=src[:, :, sub * 512:
                                                 (sub + 1) * 512]),
                            "s%d" % (nsl % 2), 16)
                        nsl += 1
        blk.sync(_syg)

        gbanks = [psA[:, 0, 0:TH], psA[:, 1, 0:TH],
                  psB[:, 0, 0:TH], psB[:, 1, 0:TH]]
        ubanks = [psC[:, 0, 0:TH], psC[:, 1, 0:TH],
                  psD[:, 0:TH], psE[:, 0:TH]]

        def _teg(t):
            W(t, "tf", xq1_tok)
            nsl = 0
            for grp in range(NGRP):
                for mat in range(2):
                    W(t, "s%d" % (nsl % 2), ssb[nsl % 2] + 16 * (nsl // 2 + 1))
                    if mat == 0:
                        if grp >= 1:
                            W(t, "q1", q12 + 4 * grp)
                    else:
                        if grp >= 1:
                            W(t, "q3", q32 + 4 * grp)
                    banks = gbanks if mat == 0 else ubanks
                    for c in range(NC):
                        for oc in range(4):
                            iv = t.matmul(banks[oc],
                                          ws[:, nsl % 2, c,
                                             oc * 128:(oc + 1) * 128],
                                          xq1[:, c, :],
                                          start=(c == 0), stop=(c == NC - 1))
                            if c == NC - 1:
                                inc(iv, "pe" if mat == 0 else "zf")
                    nsl += 1
        blk.tensor(_teg)

        def _silu(sc):
            for grp in range(NGRP):
                for oc in range(4):
                    W(sc, "pe", pe2 + 4 * grp + oc + 1)
                    if grp >= 1 or oc >= 2:
                        W(sc, "q3", q32 + 4 * grp + oc - 1)
                    inc(sc.activation(sc512[:, oc % 2, 0:TH],
                                      gbanks[oc], AF.Silu,
                                      scale=scl[:, 4:5]), "q1")
        blk.scalar(_silu)

        def _stt(v):
            for grp in range(NGRP):
                for oc in range(4):
                    fc = grp * 4 + oc
                    W(v, "zf", zf2 + 4 * grp + oc + 1)
                    W(v, "q1", q12 + 4 * grp + oc + 1)
                    inc(v.scalar_tensor_tensor(HID[:, fc, :], ubanks[oc],
                                               scl[:, 5:6],
                                               sc512[:, oc % 2, 0:TH],
                                               A.mult, A.mult), "q3")
        blk.vector(_stt)
        gu_done = cnt["q3"]

        if dbg == -21:
            def _d21(g):
                W(g, "q3", gu_done)
                inc(g.dma_start(out=y_fm[:, 0:TH]
                                .rearrange("(c p) t -> p c t", p=128),
                                in_=HID[:, 0:16, :]), "gi", 16)
                inc(g.dma_start(out=y_fm[:, TH:2 * TH]
                                .rearrange("(c p) t -> p c t", p=128),
                                in_=HID[:, 16:32, :]), "gi", 16)
                W(g, "gi", cnt["gi"])
            blk.gpsimd(_d21)
            return

        q1_all = cnt["q1"]
        hb_pe = cnt["pe"]
        hb_ac0 = cnt["ac"]

        def _h128b(t):
            W(t, "q3", gu_done)
            W(t, "q1", q1_all)
            for c in range(NFC):
                if c >= 2:
                    W(t, "ac", hb_ac0 + c - 1)
                inc(t.matmul(psB[:, c % 2, 0:TH], h128[:], HID[:, c, :],
                             start=True, stop=True), "pe")
        blk.tensor(_h128b)

        def _h128bev(sc):
            for c in range(NFC):
                W(sc, "pe", hb_pe + c + 1)
                inc(sc.activation(HID[:, c, :], psB[:, c % 2, 0:TH],
                                  AF.Copy), "ac")
        blk.scalar(_h128bev)
        hb_ac = cnt["ac"]

        def _h64(v):
            W(v, "ac", hb_ac)
            for s in (1, 2, 4, 8, 16, 32):
                for i in range(NFC):
                    if i & s:
                        continue
                    a, b = i, i + s
                    t1 = inc(v.tensor_copy(ttmp[:, 0:TH], HID[:, a, :]), "ve")
                    v.wait_ge(sems["ve"], t1)
                    t2 = inc(v.tensor_tensor(HID[:, a, :], HID[:, a, :],
                                             HID[:, b, :], A.add), "ve")
                    v.wait_ge(sems["ve"], t2)
                    t3 = inc(v.tensor_tensor(HID[:, b, :], ttmp[:, 0:TH],
                                             HID[:, b, :], A.subtract), "ve")
                    v.wait_ge(sems["ve"], t3)
            t1 = inc(v.tensor_tensor(qtmp[:, 0, 0:TH], HID[:, 0, :],
                                     HID[:, 1, :], A.max), "ve")
            t2 = inc(v.tensor_tensor(qtmp[:, 1, 0:TH], HID[:, 0, :],
                                     HID[:, 1, :], A.min), "ve")
            v.wait_ge(sems["ve"], t2)
            for c in range(2, NFC):
                t1 = inc(v.tensor_tensor(qtmp[:, 0, 0:TH], qtmp[:, 0, 0:TH],
                                         HID[:, c, :], A.max), "ve")
                t2 = inc(v.tensor_tensor(qtmp[:, 1, 0:TH], qtmp[:, 1, 0:TH],
                                         HID[:, c, :], A.min), "ve")
                v.wait_ge(sems["ve"], t2)
            t1 = inc(v.scalar_tensor_tensor(qtmp[:, 0, 0:TH], qtmp[:, 1, 0:TH],
                                            -1.0, qtmp[:, 0, 0:TH],
                                            A.mult, A.max), "ve")
            v.wait_ge(sems["ve"], t1)
            tok["fold2"] = cnt["ve"]
        blk.vector(_h64)

        def _par2(g):
            W(g, "ve", tok["fold2"])
            tok["par2"] = inc(g.reduce_max(rows.ap()[0:1, 1536:1536 + TH],
                                           qtmp[:, 0, 0:TH], AX.C), "gc")
        blk.gpsimd(_par2)

        def _rows2(v):
            W(v, "gc", tok["par2"])
            vchain(v, v.tensor_scalar(rows.ap()[0:1, 0:TH],
                                      rows.ap()[0:1, 1536:1536 + TH],
                                      1.0 / math.sqrt(FF), EPS_Q,
                                      A.mult, A.max))
            vchain(v, v.reciprocal(rows.ap()[0:1, 512:512 + TH],
                                   rows.ap()[0:1, 0:TH]))
            vchain(v, v.tensor_scalar(rows.ap()[0:1, 512:512 + TH],
                                      rows.ap()[0:1, 512:512 + TH],
                                      127.0 / math.sqrt(FF), None, A.mult))
            inc(v.tensor_scalar(rows.ap()[0:1, 1024:1024 + TH],
                                rows.ap()[0:1, 0:TH],
                                1.0 / 127.0, None, A.mult), "q3")
        blk.vector(_rows2)
        r2_tok = cnt["q3"]

        def _bc2(t):
            W(t, "q3", r2_tok)
            inc(t.matmul(psD[:, 0:TH], ones_b32[:],
                         rows.ap()[0:1, 512:512 + TH],
                         start=True, stop=True), "pe")
            inc(t.matmul(psE[:, 0:TH], ones_b32[:],
                         rows.ap()[0:1, 1024:1024 + TH],
                         start=True, stop=True), "pe")
        blk.tensor(_bc2)
        bc2_tok = cnt["pe"]

        def _cp2(sc):
            W(sc, "pe", bc2_tok)
            inc(sc.activation(cbc[:, 0:TH], psD[:, 0:TH], AF.Copy), "ac")
            inc(sc.activation(rbc[:, 0:TH], psE[:, 0:TH], AF.Copy), "ac")
        blk.scalar(_cp2)
        cp2_tok = cnt["ac"]

        tf2 = cnt["tf"]

        def _round2(v):
            W(v, "ac", cp2_tok)
            tfb = cnt["tf"]
            for c in range(NFC):
                if c >= 1:
                    W(v, "tf", tfb + c)
                t1 = inc(v.tensor_tensor(qtmp[:, 0, 0:TH], HID[:, c, :],
                                         cbc[:, 0:TH], A.mult), "ve")
                v.wait_ge(sems["ve"], t1)
                t2 = inc(v.tensor_scalar(qint[:, 0:TH], qtmp[:, 0, 0:TH],
                                         256.5, None, A.add), "ve")
                v.wait_ge(sems["ve"], t2)
                inc(v.scalar_tensor_tensor(HID[:, c, :], qint[:, 0:TH],
                                           -256.0, rbc[:, 0:TH],
                                           A.add, A.mult), "tf")
        blk.vector(_round2)

        if dbg == -22:
            def _d22(g):
                W(g, "tf", cnt["tf"])
                inc(g.dma_start(out=y_fm[:, 0:TH]
                                .rearrange("(c p) t -> p c t", p=128),
                                in_=HID[:, 0:16, :]), "gi", 16)
                inc(g.dma_start(out=y_fm[:, TH:2 * TH]
                                .rearrange("(c p) t -> p c t", p=128),
                                in_=HID[:, 16:32, :]), "gi", 16)
                W(g, "gi", cnt["gi"])
            blk.gpsimd(_d22)
            return

        sdb = [cnt["s0"], cnt["s1"]]
        pf3 = cnt["pf"]; af3 = cnt["af"]

        def _syd(s_):
            nsl = 0
            for r in range(NCORES):
                base = r * TERN + 2 * GT
                src = tern_out[base:base + DT] \
                    .rearrange("(c p o) -> p c o", p=128, o=D // NCORES)
                for s4 in range(4):
                    if nsl >= 2:
                        W(s_, "pf", pf3 + nsl - 1)
                    else:
                        W(s_, "pe", pe2 + 4 * NGRP)
                        W(s_, "zf", zf2 + 4 * NGRP)
                    inc(s_.dma_start(out=ws[:, nsl % 2, :, 0:256],
                                     in_=src[:, s4 * 16:(s4 + 1) * 16, :]),
                        "s%d" % (nsl % 2), 16)
                    nsl += 1
        blk.sync(_syd)

        def _ted(t):
            nsl = 0
            for r in range(NCORES):
                if r >= 1:
                    W(t, "af", af3 + 2 * r)
                for s4 in range(4):
                    W(t, "s%d" % (nsl % 2), sdb[nsl % 2] + 16 * (nsl // 2 + 1))
                    W(t, "tf", tf2 + (s4 + 1) * 16)
                    for c16 in range(16):
                        cg = s4 * 16 + c16
                        for oc in range(2):
                            iv = t.matmul(psA[:, oc, 0:TH],
                                          ws[:, nsl % 2, c16,
                                             oc * 128:(oc + 1) * 128],
                                          HID[:, cg, :],
                                          start=(cg == 0), stop=(cg == NFC - 1))
                            if c16 == 15 and oc == 1:
                                inc(iv, "pf")
                    nsl += 1
        blk.tensor(_ted)

        def _evd(v):
            for r in range(NCORES):
                for oc in range(2):
                    W(v, "pf", pf3 + 4 * (r + 1))
                    ocg = 2 * r + oc
                    inc(v.scalar_tensor_tensor(
                        stream[:, ocg, h * TH:(h + 1) * TH],
                        psA[:, oc, 0:TH], scl[:, 6:7],
                        stream[:, ocg, h * TH:(h + 1) * TH],
                        A.mult, A.add), "af")
        blk.vector(_evd)

        if dbg == -23:
            def _d23(g):
                W(g, "af", cnt["af"])
                inc(g.dma_start(out=y_fm.rearrange("(c p) t -> p c t", p=128),
                                in_=stream[:]), "gi", 16)
                W(g, "gi", cnt["gi"])
            blk.gpsimd(_d23)
            return

    if upto in (-20, -21, -22, -23):
        import os as _os
        ffn1_half(0, dbg=upto)
        for t_ in reversed(ctx):
            t_.__exit__(None, None, None)
        return nc

    for h in range(2):
        ffn1_half(h)
    layernorm(store_half=True)

    _finish()
    return _close(nc)


_CACHE = {}


def _host_stage(inputs, S):
    T = S // 2
    QS = min(256, T)
    d = {}
    d["wq0"] = np.ascontiguousarray(inputs["L0_attn_Wq"].T)
    d["wk0"] = np.ascontiguousarray(inputs["L0_attn_Wk"].T)
    d["wv0"] = np.ascontiguousarray(inputs["L0_attn_Wv"].T)
    d["wo0"] = np.ascontiguousarray(inputs["L0_attn_Wo"].T)
    d["fct"] = None
    d["bias0"] = np.ascontiguousarray(
        np.asarray(inputs["L0_ffn_bias"]).reshape(NC, 128).T)
    d["wq1"] = np.ascontiguousarray(
        np.transpose(inputs["L1_attn_W_Q"], (2, 0, 1)).reshape(D, D))
    d["wk1"] = np.ascontiguousarray(
        np.transpose(inputs["L1_attn_W_K"], (2, 0, 1)).reshape(D, D))
    d["wv1"] = np.ascontiguousarray(
        np.transpose(inputs["L1_attn_W_V"], (2, 0, 1)).reshape(D, D))
    d["wo1"] = np.ascontiguousarray(
        np.transpose(inputs["L1_attn_W_O"], (0, 2, 1)).reshape(D, D))
    gT = np.ascontiguousarray(inputs["L1_ffn_w_gate"].T)
    uT = np.ascontiguousarray(inputs["L1_ffn_w_up"].T)
    dT = np.ascontiguousarray(inputs["L1_ffn_w_down"].T)
    H = np.array([[1.0]], dtype=np.float32)
    while H.shape[0] < 128:
        H = np.block([[H, H], [H, -H]])
    d["h128t"] = np.ascontiguousarray(H.astype(np.float32))

    x = np.asarray(inputs["L0_attn_x"])
    B = x.shape[0]
    maps = []
    for core in range(NCORES):
        b = core // 2
        p = core % 2
        blocks = [2 * j + p for j in range(T // 128)]
        cols = np.concatenate([np.arange(128 * t, 128 * (t + 1))
                               for t in blocks])
        m = dict(d)
        m["fct"] = np.asarray(inputs["L0_ffn_factors"][core])
        xc = np.ascontiguousarray(x[b].T[:, cols])
        m["x_fm"] = xc
        m["tg_sl"] = np.ascontiguousarray(
            gT[:, core * (FF // NCORES):(core + 1) * (FF // NCORES)])
        m["tu_sl"] = np.ascontiguousarray(
            uT[:, core * (FF // NCORES):(core + 1) * (FF // NCORES)])
        m["td_sl"] = np.ascontiguousarray(
            dT[:, core * (D // NCORES):(core + 1) * (D // NCORES)])
        mk = np.zeros((4, 128, QS), dtype=np.float32)
        kp = np.arange(128)
        qc = np.arange(QS)
        qblk = qc // 128
        qoff = 128 * (2 * qblk + p) + (qc % 128)
        for mm in range(4):
            koff = 128 * mm + kp
            mk[mm] = (koff[:, None] <= qoff[None, :]).astype(np.float32)
        m["maskt"] = mk.reshape(4 * 128, QS)
        pos = cols.astype(np.float32)
        half = HD // 2
        theta = (10000.0 ** (-np.arange(half, dtype=np.float32) * 2.0 / HD))
        ang = theta[:, None] * pos[None, :]
        cosv = np.cos(ang).astype(np.float32)
        sinv = np.sin(ang).astype(np.float32)
        m["ropec"] = np.ascontiguousarray(
            np.concatenate([cosv, cosv], axis=0))
        m["ropes"] = np.ascontiguousarray(
            np.concatenate([-sinv, sinv], axis=0))
        maps.append(m)
    return maps, x.shape


def kernel(**inputs):
    S = np.asarray(inputs["L0_attn_x"]).shape[1]
    if S not in _CACHE:
        _CACHE[S] = build(S)
    nc = _CACHE[S]
    maps, xshape = _host_stage(inputs, S)
    res = run_bass_kernel_spmd(nc, maps, core_ids=list(range(NCORES)))
    B, S_, D_ = xshape
    T = S // 2
    out = np.empty((B, S_, D_), dtype=np.float32)
    for core in range(NCORES):
        b = core // 2
        p = core % 2
        y = res.results[core]["y_fm"]
        for j in range(T // 128):
            tblk = 2 * j + p
            out[b, 128 * tblk:128 * (tblk + 1), :] = \
                y[:, 128 * j:128 * (j + 1)].T
    return out


# version 06
# speedup vs baseline: 1.2749x; 1.2749x over previous
import math
import numpy as np
import concourse.bass as bass
import concourse.mybir as mybir
from concourse.bass_utils import run_bass_kernel_spmd
from concourse.bass_isa import ReduceOp

F32 = mybir.dt.float32
BF16 = mybir.dt.bfloat16
A = mybir.AluOpType
AF = mybir.ActivationFunctionType
AX = mybir.AxisListType

D = 2048
NC = D // 128
NH = 16
HD = 128
FF = 8192
NFC = FF // 128
EPS_LN = 1e-5
EPS_L0 = 1e-6
EPS_Q = 1e-6
NCORES = 8
GT = D * (FF // NCORES)
DT = FF * (D // NCORES)
TERN = 2 * GT + DT
NW = float(D * FF)
ALL8 = [list(range(NCORES))]
PAIRS = [[0, 1], [2, 3], [4, 5], [6, 7]]
INV_SQRT8 = 1.0 / math.sqrt(8.0)
MAGIC = 256.5


def build(S=2048, upto=4):
    T = S // 2
    TH = T // 2
    NTB = T // 128
    QS = min(512, T)
    NQ = T // QS
    LQB = QS // 128
    NBM = 2 * LQB
    NT = min(512, TH)
    NB_H = TH // NT
    KBq = max(1, T // 512)
    NKB = min(512, T)
    nc = bass.Bass("TRN2", target_bir_lowering=False, debug=False)

    def din(name, shape):
        return nc.dram_tensor(name, shape, F32, kind="ExternalInput").ap()

    x_fm = din("x_fm", [D, T])
    wq0 = din("wq0", [D, D]); wk0 = din("wk0", [D, D])
    wv0 = din("wv0", [D, D]); wo0 = din("wo0", [D, D])
    wq1 = din("wq1", [D, D]); wk1 = din("wk1", [D, D])
    wv1 = din("wv1", [D, D]); wo1 = din("wo1", [D, D])
    fct = din("fct", [D, D])
    bias0 = din("bias0", [128, NC])
    tg_sl = din("tg_sl", [D, FF // NCORES])
    tu_sl = din("tu_sl", [D, FF // NCORES])
    td_sl = din("td_sl", [FF, D // NCORES])
    maskt = din("maskt", [NBM * 128, QS])
    ropec = din("ropec", [128, T])
    ropes = din("ropes", [128, T])
    h128t = din("h128t", [128, 128])
    y_fm = nc.dram_tensor("y_fm", [D, T], F32, kind="ExternalOutput").ap()

    f_in = nc.dram_tensor("f_in", [D, D], F32).ap()
    f_out = nc.dram_tensor("f_out", [D, D], F32, addr_space="Shared").ap()
    f_sc_in = nc.dram_tensor("f_sc_in", [128, 4], F32).ap()
    f_sc_out = nc.dram_tensor("f_sc_out", [128, 4], F32).ap()
    tern_in = nc.dram_tensor("tern_in", [TERN], BF16).ap()
    tern_out = nc.dram_tensor("tern_out", [NCORES * TERN], BF16,
                              addr_space="Shared").ap()
    k_in = [nc.dram_tensor(f"k_in{l}", [D, T], BF16).ap() for l in range(2)]
    k_out = [nc.dram_tensor(f"k_out{l}", [2 * D, T], BF16).ap() for l in range(2)]
    v_in = [nc.dram_tensor(f"v_in{l}", [NH * T, HD], BF16).ap() for l in range(2)]
    v_out = [nc.dram_tensor(f"v_out{l}", [2 * NH * T, HD], BF16).ap()
             for l in range(2)]

    ctx = []

    def sbt(name, shape, dt=BF16):
        t = nc.sbuf_tensor(name, shape, dt); ctx.append(t); return t.__enter__()

    def pst(name, shape):
        t = nc.psum_tensor(name, shape, F32); ctx.append(t); return t.__enter__()

    stream = sbt("stream", [128, NC, T])
    xbh = sbt("xbh", [128, NC, TH])
    big = sbt("big", [128, 32 * T])
    QV = big.ap().rearrange("p (c t) -> p c t", t=T)
    HID = big.ap().rearrange("p (c t) -> p c t", t=TH)

    masks = sbt("masks", [128, NBM, QS])
    pt = sbt("pt", [128, 2, QS])
    qtmp = sbt("qtmp", [128, 2, 512], F32)
    cbc = sbt("cbc", [128, 512], F32)
    rbc = sbt("rbc", [128, 2 * QS])
    kvq = sbt("kvq", [128, 8 * T])
    kd = kvq.ap()[:, 0:4 * T].rearrange("p (b r t) -> p b r t", b=2, r=2)
    vd = kvq.ap()[:, 4 * T:8 * T].rearrange("p (b k e) -> p b k e",
                                            b=2, e=HD)
    xq1 = kvq.ap().rearrange("p (c t) -> p c t", t=TH)[:, 0:NC, :]
    ws = sbt("ws", [128, 2, NC, 512])
    tstage = sbt("tstage", [128, 2, 512], F32)
    qint = tstage.ap()[:, 0, :].bitcast(mybir.dt.int32)
    ttmp = sbt("ttmp", [128, 512])
    ttrn = sbt("ttrn", [128, 2, 512])
    cosf = sbt("cosf", [128, T])
    sinf = sbt("sinf", [128, T])
    h128 = sbt("h128", [128, 128])
    sc512 = sbt("sc512", [128, 2, 512])
    rk = sbt("rk", [128, 2, 512])
    rsw = sbt("rsw", [128, 512])
    ones_k = sbt("ones_k", [128, 1])
    ones_b = sbt("ones_b", [1, 128])
    neg_b = sbt("neg_b", [1, 128])
    ones_k32 = sbt("ones_k32", [128, 1], F32)
    ones_b32 = sbt("ones_b32", [1, 128], F32)
    browz = sbt("browz", [1, 1536])
    bias_sb = sbt("bias_sb", [128, NC], F32)
    scl = sbt("scl", [128, 8], F32)
    rtmp = sbt("rtmp", [128, 136], F32)
    rows = sbt("rows", [1, 2048], F32)

    psA = pst("psA", [128, 2, 512])
    psB = pst("psB", [128, 2, 512])
    psC = pst("psC", [128, 2, 512])
    psD = pst("psD", [128, 512])
    psE = pst("psE", [128, 512])

    blk = nc.Block().__enter__()
    ctx.append(blk)

    SEMS = ["gi", "si", "cf", "cy", "g0", "g1", "s0", "s1", "k0", "k1",
            "ck0", "ck1", "cv0", "cv1", "ct0", "ct1", "cu0", "cu1",
            "t0", "t1", "u0", "u1", "ab", "ts",
            "cc", "gc", "pe", "ve", "ac", "sf", "pf", "zf",
            "kf", "af", "df", "cb", "rp", "q1", "q2", "q3", "q5", "tf", "bm"]
    sems = {}
    cnt = {}
    for sname in SEMS:
        sems[sname] = nc.semaphore(sname).__enter__()
        cnt[sname] = 0

    def inc(inst, name, n=1):
        inst.then_inc(sems[name], n)
        cnt[name] += n
        return cnt[name]

    def W(eng, name, v):
        if v > 0:
            eng.wait_ge(sems[name], v)

    def vchain(v, inst):
        t = inc(inst, "ve")
        v.wait_ge(sems["ve"], t)
        return t

    tok = {}

    def ph0_gp(g):
        inc(g.memset(ones_k[:], 1.0), "gc")
        inc(g.memset(ones_b[:], 1.0), "gc")
        inc(g.memset(neg_b[:], -1.0), "gc")
        inc(g.memset(ones_k32[:], 1.0), "gc")
        inc(g.memset(ones_b32[:], 1.0), "gc")
        inc(g.memset(rows[:], 0.0), "gc")
        tok["consts"] = cnt["gc"]
        inc(g.dma_start(out=f_in, in_=fct), "ts", 16)
        tok["fct_ld"] = cnt["ts"]
        inc(g.dma_start(
            out=masks[:], in_=maskt.rearrange("(m p) q -> p m q", p=128)), "gi", 16)
        inc(g.dma_start(out=cosf[:], in_=ropec), "gi", 16)
        inc(g.dma_start(out=sinf[:], in_=ropes), "gi", 16)
        inc(g.dma_start(out=h128[:], in_=h128t), "gi", 16)
        inc(g.dma_start(
            out=stream[:], in_=x_fm.rearrange("(c p) t -> p c t", p=128)),
            "gi", 16)
        tok["gi_all"] = cnt["gi"]
        W(g, "ts", tok["fct_ld"])
        tok["ar1"] = inc(g.collective_compute(
            "AllReduce", A.add, replica_groups=ALL8,
            ins=[f_in.opt()], outs=[f_out.opt()]), "cc")
    blk.gpsimd(ph0_gp)

    def ph0_sy(s_):
        inc(s_.dma_start(out=bias_sb[:], in_=bias0), "si", 16)
        tok["si_all"] = cnt["si"]
    blk.sync(ph0_sy)

    t_slabs = []
    GCH = (FF // NCORES) // 512
    for r in range(D // 128):
        for ch in range(GCH):
            t_slabs.append((tg_sl[r * 128:(r + 1) * 128,
                                  ch * 512:(ch + 1) * 512], 0, 512))
    for r in range(D // 128):
        for ch in range(GCH):
            t_slabs.append((tu_sl[r * 128:(r + 1) * 128,
                                  ch * 512:(ch + 1) * 512], 1, 512))
    for r in range(FF // 128):
        t_slabs.append((td_sl[r * 128:(r + 1) * 128, :], 2, D // NCORES))
    NSLAB = len(t_slabs)
    mi_first = [min(i for i, s in enumerate(t_slabs) if s[1] == m)
                for m in range(3)]
    mi_last = [max(i for i, s in enumerate(t_slabs) if s[1] == m)
               for m in range(3)]
    sh_state = {"sb": [cnt["s0"], cnt["s1"]], "ab0": cnt["ab"],
                "tfs0": cnt["tf"], "sc_i": 0, "te_i": 0, "ntile": 0}

    def ph1_sy(s_):
        ab0 = sh_state["ab0"]
        for i, (src, mi, cols) in enumerate(t_slabs):
            if i >= 2:
                W(s_, "ab", ab0 + i - 1)
            inc(s_.dma_start(out=tstage[:, i % 2, 0:cols], in_=src),
                "s%d" % (i % 2), 16)
    blk.sync(ph1_sy)

    SROWS = [psD[0:1, 0:512], psE[0:1, 0:512], psB[0:1, 0, 0:512]]

    def _acc_mm(t, i):
        src, mi, cols = t_slabs[i]
        W(t, "ab", sh_state["ab0"] + i + 1)
        inc(t.matmul(SROWS[mi][:, 0:cols], ones_k[:],
                     ttrn[:, i % 2, 0:cols],
                     start=(i == mi_first[mi]),
                     stop=(i == mi_last[mi])), "tf")

    def mk_scale_hook(ntiles):
        per = -(-NSLAB // ntiles)
        sh_state["te_done"] = 0

        def sc_hook(sc):
            i = sh_state["sc_i"]
            for j in range(per):
                if i >= NSLAB:
                    break
                src, mi, cols = t_slabs[i]
                W(sc, "s%d" % (i % 2),
                  sh_state["sb"][i % 2] + 16 * (i // 2 + 1))
                if i >= 2:
                    W(sc, "tf", sh_state["tfs0"] + i - 1)
                inc(sc.activation(ttrn[:, i % 2, 0:cols],
                                  tstage[:, i % 2, 0:cols], AF.Abs), "ab")
                i += 1
            sh_state["sc_i"] = i

        def te_hook(t):
            k = sh_state["te_i"]
            for i in range(sh_state["te_done"], min(k * per, NSLAB)):
                _acc_mm(t, i)
                sh_state["te_done"] = i + 1
            sh_state["te_i"] = k + 1

        def te_flush():
            def _tf(t):
                for i in range(sh_state["te_done"],
                               min(sh_state["sc_i"], NSLAB)):
                    _acc_mm(t, i)
                    sh_state["te_done"] = i + 1
            blk.tensor(_tf)
        return sc_hook, te_hook, te_flush

    def ph1_finish():
        tfs0 = sh_state["tfs0"]

        def _tf(t):
            for i in range(sh_state["te_done"], NSLAB):
                _acc_mm(t, i)
                sh_state["te_done"] = i + 1
        blk.tensor(_tf)

        def _ve(v):
            W(v, "tf", tfs0 + NSLAB)
            for m in range(3):
                cols = 512 if m < 2 else (D // NCORES)
                vchain(v, v.tensor_reduce(rows.ap()[0:1, m:m + 1],
                                          SROWS[m][:, 0:cols], AX.X, A.add))
            tok["sctot"] = cnt["ve"]
        blk.vector(_ve)

        def _te(t):
            W(t, "ve", tok["sctot"])
            tok["scbc"] = inc(t.matmul(psD[:, 0:4], ones_b32[:],
                                       rows.ap()[0:1, 0:4],
                                       start=True, stop=True), "pe")
        blk.tensor(_te)

        def _sc(sc):
            W(sc, "pe", tok["scbc"])
            t1 = inc(sc.activation(rtmp[:, 0:4], psD[:, 0:4], AF.Copy), "ac")
            sc.wait_ge(sems["ac"], t1)
            tok["scst"] = inc(sc.dma_start(out=f_sc_in, in_=rtmp[:, 0:4]),
                              "cf", 16)
        blk.scalar(_sc)

        def _gp(g):
            W(g, "cf", tok["scst"])
            tok["ar2"] = inc(g.collective_compute(
                "AllReduce", A.add, replica_groups=ALL8,
                ins=[f_sc_in.opt()], outs=[f_sc_out.opt()]), "cc")
        blk.gpsimd(_gp)

    def proj(wsrc, evac_half, cast_wait=None, rhs_z=False, slab_gate=None,
             te_hook=None, use_cast=False, te_gate=None):
        pe0 = cnt["pe"]; af0 = cnt["af"]
        gb = [cnt["g0"], cnt["g1"]]
        nts = 4 * NB_H
        nth = 16 * NB_H
        info = {"pe0": pe0, "af0": af0, "nth": nth, "cast": [0, 0], "NT": NT}
        for h in range(2):
            if use_cast:
                def _cast(v, h=h):
                    if cast_wait is not None:
                        cast_wait(v, h)
                    W(v, "pe", pe0 + (nth if h == 1 else 0))
                    for c in range(NC):
                        inc(v.tensor_copy(xbh[:, c, :],
                                          stream[:, c, h * TH:(h + 1) * TH]),
                            "ve")
                    info["cast"][h] = cnt["ve"]
                blk.vector(_cast)

            def _slabs(g, h=h):
                if slab_gate is not None and h == 0:
                    slab_gate(g)
                for i in range(4):
                    gi = h * 4 + i
                    W(g, "pe", pe0 + (gi - 1) * nts if gi >= 2 else pe0)
                    inc(g.dma_start(
                        out=ws[:, gi % 2, :, :],
                        in_=wsrc[:, i * 512:(i + 1) * 512]
                        .rearrange("(c p) o -> p c o", p=128)),
                        "g%d" % (gi % 2), 16)
            blk.gpsimd(_slabs)

            def _te(t, h=h):
                if use_cast:
                    W(t, "ve", info["cast"][h])
                elif rhs_z:
                    W(t, "af", af0)
                elif te_gate is not None and h == 0:
                    te_gate(t)
                n = 0
                for i in range(4):
                    gi0 = h * 4 + i
                    W(t, "g%d" % (gi0 % 2), gb[gi0 % 2] + 16 * (gi0 // 2 + 1))
                    gi = h * 4 + i
                    for j in range(4):
                        for tb in range(NB_H):
                            gn = h * nth + n
                            W(t, "af", af0 + gn - 1 if gn >= 2 else af0)
                            last = None
                            for c in range(NC):
                                if rhs_z:
                                    rhs = QV[:, 16 + c, h * TH + tb * NT:
                                             h * TH + (tb + 1) * NT]
                                elif use_cast:
                                    rhs = xbh[:, c, tb * NT:(tb + 1) * NT]
                                else:
                                    rhs = stream[:, c,
                                                 h * TH + tb * NT:
                                                 h * TH + (tb + 1) * NT]
                                last = t.matmul(
                                    psA[:, gn % 2, 0:NT],
                                    ws[:, gi % 2, c, j * 128:(j + 1) * 128],
                                    rhs, start=(c == 0), stop=(c == NC - 1))
                            inc(last, "pe")
                            if te_hook is not None:
                                te_hook(t)
                            n += 1
            blk.tensor(_te)
            evac_half(h, info)
        return info

    def tile_oc_tb(n):
        return n // NB_H, n % NB_H

    def evac_rope_mk(dest_q, layer, sc_hook=None):
        st = {"ac0": cnt["ac"], "rp0": cnt["rp"],
              "ckb": [cnt["ck0"], cnt["ck1"]],
              "snap": {k: cnt[k] for k in
                       ("ck0", "ck1", "cv0", "cv1", "ct0", "ct1")}}

        def emit(h, info):
            ac0 = st["ac0"]; rp0 = st["rp0"]; ckb = st["ckb"]
            nth = info["nth"]; pe0 = info["pe0"]; af0 = info["af0"]

            def _sc(sc):
                def store(gn):
                    oc, tb = tile_oc_tb(gn % nth)
                    t0 = (gn // nth) * TH + tb * NT
                    W(sc, "af", af0 + gn + 1)
                    inc(sc.dma_start(
                        out=k_in[layer][oc * 128:(oc + 1) * 128, t0:t0 + NT],
                        in_=rk[:, gn % 2, 0:NT]), "ck%d" % (gn % 2), 16)
                for n in range(nth):
                    gn = h * nth + n
                    W(sc, "pe", pe0 + gn + 1)
                    if gn < 2:
                        for k2, v2 in st["snap"].items():
                            W(sc, k2, v2)
                    if gn >= 2:
                        W(sc, "rp", rp0 + gn - 1)
                    if not dest_q and gn >= 2:
                        store(gn - 2)
                    inc(sc.activation(sc512[:, gn % 2, 0:NT],
                                      psA[:, gn % 2, 0:NT], AF.Copy), "ac")
                    if sc_hook is not None:
                        sc_hook(sc)
                if not dest_q and h == 1:
                    store(2 * nth - 2)
                    store(2 * nth - 1)
            blk.scalar(_sc)

            def _ve(v):
                for n in range(nth):
                    gn = h * nth + n
                    oc, tb = tile_oc_tb(n)
                    t0 = h * TH + tb * NT
                    W(v, "ac", ac0 + gn + 1)
                    if gn >= 1:
                        W(v, "af", af0 + gn)
                    if not dest_q and gn >= 2:
                        W(v, "ck%d" % (gn % 2),
                          ckb[gn % 2] + 16 * ((gn - 2) // 2 + 1))
                    inc(v.tensor_copy(rsw[0:64, 0:NT],
                                      sc512[64:128, gn % 2, 0:NT]), "ve")
                    t2 = inc(v.tensor_copy(rsw[64:128, 0:NT],
                                           sc512[0:64, gn % 2, 0:NT]), "ve")
                    v.wait_ge(sems["ve"], t2)
                    t3 = inc(v.tensor_tensor(rsw[:, 0:NT], rsw[:, 0:NT],
                                             sinf[:, t0:t0 + NT], A.mult), "ve")
                    t4 = inc(v.tensor_tensor(rk[:, gn % 2, 0:NT],
                                             sc512[:, gn % 2, 0:NT],
                                             cosf[:, t0:t0 + NT], A.mult), "rp")
                    v.wait_ge(sems["ve"], t3)
                    v.wait_ge(sems["rp"], t4)
                    dst = QV[:, oc, t0:t0 + NT] if dest_q else rk[:, gn % 2, 0:NT]
                    inc(v.tensor_tensor(dst, rk[:, gn % 2, 0:NT],
                                        rsw[:, 0:NT], A.add), "af")
            blk.vector(_ve)
        return emit

    def evac_copy_mk(layer, mode, qscale=None, sc_hook=None):
        sem_pfx = "cv" if mode == "v" else "ck"
        st = {"cb2": [cnt[sem_pfx + "0"], cnt[sem_pfx + "1"]],
              "snap": {k: cnt[k] for k in
                       ("ck0", "ck1", "cv0", "cv1", "ct0", "ct1")}}
        vv = v_in[layer].rearrange("(h t) e -> t h e", t=T)

        def emit(h, info):
            cb2 = st["cb2"]
            af0 = info["af0"]; nth = info["nth"]; pe0 = info["pe0"]
            ntw = info["NT"]

            def _sc(sc):
                def store(gn):
                    n2 = gn % nth
                    h2 = gn // nth
                    W(sc, "af", af0 + gn + 1)
                    if mode == "v":
                        i = n2 // (TH // 128)
                        tb = n2 % (TH // 128)
                        tt0 = h2 * TH + tb * 128
                        srcap = sc512.ap().rearrange(
                            "p b (hh e) -> p b hh e", e=128)[:, gn % 2, :, :]
                        inc(sc.dma_start(out=vv[tt0:tt0 + 128,
                                                4 * i:4 * i + 4, :],
                                         in_=srcap), "cv%d" % (gn % 2), 16)
                    else:
                        oc, tb = tile_oc_tb(n2)
                        tt0 = h2 * TH + tb * NT
                        inc(sc.dma_start(
                            out=k_in[layer][oc * 128:(oc + 1) * 128,
                                            tt0:tt0 + NT],
                            in_=sc512[:, gn % 2, 0:NT]), "ck%d" % (gn % 2), 16)

                for n in range(nth):
                    gn = h * nth + n
                    W(sc, "pe", pe0 + gn + 1)
                    if gn < 2 and mode != "q":
                        for k2, v2 in st["snap"].items():
                            W(sc, k2, v2)
                    if mode == "q":
                        oc, tb = tile_oc_tb(n)
                        t0 = h * TH + tb * NT
                        if qscale is None:
                            inc(sc.activation(QV[:, oc, t0:t0 + NT],
                                              psA[:, gn % 2, 0:NT],
                                              AF.Copy), "af")
                        else:
                            inc(sc.activation(QV[:, oc, t0:t0 + NT],
                                              psA[:, gn % 2, 0:NT],
                                              AF.Copy, scale=qscale), "af")
                    else:
                        if gn >= 2:
                            store(gn - 2)
                            W(sc, sem_pfx + str(gn % 2),
                              cb2[gn % 2] + 16 * ((gn - 2) // 2 + 1))
                        inc(sc.activation(sc512[:, gn % 2, 0:ntw],
                                          psA[:, gn % 2, 0:ntw], AF.Copy), "af")
                    if sc_hook is not None:
                        sc_hook(sc)
                if mode != "q" and h == 1:
                    store(2 * nth - 2)
                    store(2 * nth - 1)
            blk.scalar(_sc)
        return emit

    def proj_tokmaj(wsrc, evac_half, te_hook=None, te_gate=None):
        pe0 = cnt["pe"]; af0 = cnt["af"]
        gb = [cnt["g0"], cnt["g1"]]
        ntb_h = TH // 128
        nth = 4 * ntb_h
        info = {"pe0": pe0, "af0": af0, "nth": nth, "cast": [0, 0], "NT": 512}
        for h in range(2):
            def _slabs(g, h=h):
                for i in range(4):
                    gi = h * 4 + i
                    W(g, "pe", pe0 + (gi - 1) * ntb_h if gi >= 2 else pe0)
                    inc(g.dma_start(
                        out=ws[:, gi % 2, :, :],
                        in_=wsrc[:, i * 512:(i + 1) * 512]
                        .rearrange("(c p) o -> p c o", p=128)),
                        "g%d" % (gi % 2), 16)
            blk.gpsimd(_slabs)

            def _te(t, h=h):
                if te_gate is not None and h == 0:
                    te_gate(t)
                n = 0
                for i in range(4):
                    gi0 = h * 4 + i
                    W(t, "g%d" % (gi0 % 2), gb[gi0 % 2] + 16 * (gi0 // 2 + 1))
                    gi = h * 4 + i
                    for tb in range(ntb_h):
                        gn = h * nth + n
                        W(t, "af", af0 + gn - 1 if gn >= 2 else af0)
                        last = None
                        for c in range(NC):
                            last = t.matmul(
                                psA[:, gn % 2, 0:512],
                                stream[:, c, h * TH + tb * 128:
                                       h * TH + (tb + 1) * 128],
                                ws[:, gi % 2, c, :],
                                start=(c == 0), stop=(c == NC - 1))
                        inc(last, "pe")
                        if te_hook is not None:
                            te_hook(t)
                        n += 1
            blk.tensor(_te)
            evac_half(h, info)
        return info

    def tg_x(t):
        W(t, "gi", tok["gi_all"])

    NTILE_K = 2 * 16 * NB_H
    NTILE_V = 2 * 4 * (TH // 128)
    sc_hook, te_hook, te_flush = mk_scale_hook(NTILE_K + NTILE_V)
    proj(wk0, evac_rope_mk(dest_q=False, layer=0, sc_hook=sc_hook),
         te_gate=tg_x, te_hook=te_hook)
    te_flush()
    kst0 = (cnt["ck0"], cnt["ck1"])
    proj_tokmaj(wv0, evac_copy_mk(0, "v", sc_hook=sc_hook), te_hook=te_hook)
    vst0 = (cnt["cv0"], cnt["cv1"])
    ph1_finish()
    proj(wq0, evac_rope_mk(dest_q=True, layer=0))
    tok["q0_af"] = cnt["af"]

    if upto == -1:
        qaf = cnt["af"]

        def _dbgst(g):
            W(g, "af", qaf)
            inc(g.dma_start(out=y_fm.rearrange("(c p) t -> p c t", p=128),
                            in_=QV[:, 0:16, :]), "gi", 16)
            W(g, "gi", cnt["gi"])
        blk.gpsimd(_dbgst)
        for t_ in reversed(ctx):
            t_.__exit__(None, None, None)
        return nc

    def ag0(g):
        W(g, "ck0", kst0[0]); W(g, "ck1", kst0[1])
        tok["agk0"] = inc(g.collective_compute(
            "AllGather", A.bypass, replica_groups=PAIRS,
            ins=[k_in[0].opt()], outs=[k_out[0].opt()]), "cc")
        W(g, "cv0", vst0[0]); W(g, "cv1", vst0[1])
        tok["agv0"] = inc(g.collective_compute(
            "AllGather", A.bypass, replica_groups=PAIRS,
            ins=[v_in[0].opt()], outs=[v_out[0].opt()]), "cc")
    blk.gpsimd(ag0)

    DN = D // NCORES
    GB = 128 * 16 * 512
    DB = 128 * 16 * DN

    def d_src(i):
        return td_sl[i * 128:(i + 1) * 128, :]

    def d_dst(i):
        base = 2 * GT + (i // 16) * DB
        return tern_in[base:base + DB].rearrange(
            "(p c o) -> p c o", c=16, o=DN)[:, i % 16, :]

    def g_src(j):
        mat, r, ch = j // 32, (j % 32) // GCH, j % GCH
        wsrc = tg_sl if mat == 0 else tu_sl
        return wsrc[r * 128:(r + 1) * 128, ch * 512:(ch + 1) * 512]

    def g_dst(j):
        mat, r, ch = j // 32, (j % 32) // GCH, j % GCH
        base = mat * GT + ch * GB
        return tern_in[base:base + GB].rearrange(
            "(p c o) -> p c o", c=16, o=512)[:, r, :]

    NP = FF // 128
    NV = 64
    tq_st = {}

    def tq_gp(g):
        tb = [cnt["t0"], cnt["t1"]]
        ctb = [cnt["ct0"], cnt["ct1"]]
        tfq0 = cnt["tf"]
        W(g, "ab", sh_state["ab0"] + NSLAB)
        for i in (0, 1):
            inc(g.dma_start(out=tstage[:, i % 2, 0:DN], in_=d_src(i)),
                "t%d" % (i % 2), 16)
        W(g, "cc", tok["ar2"])
        inc(g.dma_start(out=scl[:, 0:4], in_=f_sc_out), "ts", 16)
        g.wait_ge(sems["ts"], cnt["ts"])
        t1 = inc(g.tensor_scalar(scl[:, 4:8], scl[:, 0:4], 1.0 / NW, EPS_Q,
                                 A.mult, A.max), "gc")
        g.wait_ge(sems["gc"], t1)
        t2 = inc(g.tensor_scalar(scl[:, 0:4], scl[:, 4:8], 0.5, None,
                                 A.mult), "gc")
        g.wait_ge(sems["gc"], t2)
        tq_st["scl"] = t2
        for i in range(NP):
            W(g, "t%d" % (i % 2), tb[i % 2] + 16 * (i // 2 + 1))
            if i >= 2:
                W(g, "ct%d" % (i % 2), ctb[i % 2] + 16 * ((i - 2) // 2 + 1))
            else:
                W(g, "tf", tfq0)
            t1 = inc(g.tensor_scalar(ttmp[:, 0:DN], tstage[:, i % 2, 0:DN],
                                     -1.0, scl[:, 2:3], A.mult, A.is_gt),
                     "gc")
            t2 = inc(g.tensor_scalar(ttrn[:, i % 2, 0:DN],
                                     tstage[:, i % 2, 0:DN],
                                     scl[:, 2:3], None, A.is_gt), "gc")
            g.wait_ge(sems["gc"], t2)
            t3 = inc(g.tensor_tensor(ttrn[:, i % 2, 0:DN],
                                     ttrn[:, i % 2, 0:DN],
                                     ttmp[:, 0:DN], A.subtract), "tf")
            g.wait_ge(sems["tf"], cnt["tf"])
            inc(g.dma_start(out=d_dst(i), in_=ttrn[:, i % 2, 0:DN]),
                "ct%d" % (i % 2), 16)
            if i + 2 < NP:
                inc(g.dma_start(out=tstage[:, i % 2, 0:DN],
                                in_=d_src(i + 2)), "t%d" % (i % 2), 16)
        tq_st["tf_pool"] = cnt["tf"]
        tq_st["ct"] = (cnt["ct0"], cnt["ct1"])
    blk.gpsimd(tq_gp)

    def mk_vq_hook():
        ub = [cnt["u0"], cnt["u1"]]
        cub = [cnt["cu0"], cnt["cu1"]]
        tfv0 = cnt["tf"]
        st = {"j": 0}

        def _ld(s_):
            for j in range(NV):
                if j >= 2:
                    W(s_, "tf", tfv0 + j - 1)
                else:
                    W(s_, "tf", tq_st["tf_pool"])
                inc(s_.dma_start(out=tstage[:, j % 2, 0:512],
                                 in_=g_src(j)), "u%d" % (j % 2), 16)
        blk.sync(_ld)

        def vq(v):
            j0 = st["j"]
            if j0 >= NV:
                return
            if j0 == 0:
                W(v, "gc", tq_st["scl"])
                W(v, "tf", tq_st["tf_pool"])
                W(v, "ct0", tq_st["ct"][0])
                W(v, "ct1", tq_st["ct"][1])
            for j in range(j0, min(j0 + 2, NV)):
                mat = j // 32
                W(v, "u%d" % (j % 2), ub[j % 2] + 16 * (j // 2 + 1))
                if j >= 2:
                    W(v, "cu%d" % (j % 2),
                      cub[j % 2] + 16 * ((j - 2) // 2 + 1))
                if j >= 1:
                    W(v, "tf", tfv0 + j)
                t1 = inc(v.tensor_scalar(ttmp[:, 0:512],
                                         tstage[:, j % 2, 0:512],
                                         -1.0, scl[:, mat:mat + 1],
                                         A.mult, A.is_gt), "ve")
                v.wait_ge(sems["ve"], t1)
                inc(v.scalar_tensor_tensor(ttrn[:, j % 2, 0:512],
                                           tstage[:, j % 2, 0:512],
                                           scl[:, mat:mat + 1],
                                           ttmp[:, 0:512],
                                           A.is_gt, A.subtract), "tf")
            st["j"] = j0 + 2

        def stores():
            def _st(sc):
                for j in range(NV):
                    W(sc, "tf", tfv0 + j + 1)
                    inc(sc.dma_start(out=g_dst(j),
                                     in_=ttrn[:, j % 2, 0:512]),
                        "cu%d" % (j % 2), 16)
            blk.scalar(_st)
        return vq, stores

    def tq_ag():
        def _gp(g):
            W(g, "ct0", tq_st["ct"][0])
            W(g, "ct1", tq_st["ct"][1])
            W(g, "cu0", cnt["cu0"])
            W(g, "cu1", cnt["cu1"])
            tok["agt"] = inc(g.collective_compute(
                "AllGather", A.bypass, replica_groups=ALL8,
                ins=[tern_in.opt()], outs=[tern_out.opt()]), "cc")
        blk.gpsimd(_gp)

    def attn(layer):
        tiles = []
        hg_sizes = []
        for h in range(NH):
            for g in range(NQ):
                lst = []
                for j in range(LQB * g):
                    for r in range(2):
                        lst.append((r, j, None))
                for m in range(NBM):
                    lst.append((m % 2, LQB * g + m // 2, m))
                hg_sizes.append(len(lst))
                for it, (r, j, bm) in enumerate(lst):
                    tiles.append(dict(h=h, g=g, r=r, j=j, bm=bm,
                                      first=(it == 0),
                                      last=(it == len(lst) - 1),
                                      hg=h * NQ + g))
        for i, td in enumerate(tiles):
            td["ti"] = i
        NHG = NH * NQ
        thru = np.cumsum(hg_sizes).tolist()
        band_pre = []
        nb = 0
        for td in tiles:
            if td["bm"] is not None:
                nb += 1
            band_pre.append(nb)
        tiles_by_hg = [[] for _ in range(NHG)]
        for td in tiles:
            tiles_by_hg[td["hg"]].append(td)

        pe0 = cnt["pe"]; sf0 = cnt["sf"]; pf0 = cnt["pf"]; zf0 = cnt["zf"]
        ac0 = cnt["ac"]; bm0 = cnt["bm"]; df0 = cnt["df"]; af0 = cnt["af"]
        rp0 = cnt["rp"]; q10 = cnt["q1"]; q20 = cnt["q2"]; q30 = cnt["q3"]
        q50 = cnt["q5"]; cb0 = cnt["cb"]
        kvb = [cnt["k0"], cnt["k1"]]
        q_af = tok["q%d_af" % layer]
        stab_n = 3 * KBq
        RIN, ROUT = 1024, 1536

        def _sy(s_):
            W(s_, "cc", tok["agk%d" % layer])
            W(s_, "cc", tok["agv%d" % layer])
            for h in range(NH):
                if h >= 2:
                    W(s_, "zf", zf0 + thru[(h - 1) * NQ - 1])
                for r in range(2):
                    inc(s_.dma_start(
                        out=kd[:, h % 2, r, :],
                        in_=k_out[layer][r * D + h * 128:
                                         r * D + (h + 1) * 128, :]),
                        "k%d" % (h % 2), 16)
                    inc(s_.dma_start(
                        out=vd[:, h % 2, r * NTB:(r + 1) * NTB, :],
                        in_=v_out[layer][(r * NH + h) * T:
                                         (r * NH + h + 1) * T, :]
                        .rearrange("(b p) e -> p b e", p=128)),
                        "k%d" % (h % 2), 16)
        blk.sync(_sy)

        def kv_wait(eng, h):
            W(eng, "k%d" % (h % 2), kvb[h % 2] + 64 * (h // 2 + 1))

        def q2_of_rb(hg):
            return q20 + hg + 1

        def _sc(sc):
            for h in range(NH):
                if layer == 1:
                    for r in range(2):
                        for b in range(KBq):
                            i = r * KBq + b
                            kv_wait(sc, h)
                            if h > 0 or i >= 2:
                                W(sc, "cb", cb0 + stab_n * h + i - 1)
                            inc(sc.activation(
                                sc512[:, i % 2, 0:NKB],
                                kd[:, h % 2, r, b * NKB:(b + 1) * NKB],
                                AF.Square), "q1")
                    for b in range(KBq):
                        i = 2 * KBq + b
                        W(sc, "af", q_af)
                        if h > 0 or i >= 2:
                            W(sc, "cb", cb0 + stab_n * h + i - 1)
                        inc(sc.activation(sc512[:, i % 2, 0:NKB],
                                          QV[:, h, b * NKB:(b + 1) * NKB],
                                          AF.Square), "q1")
                    W(sc, "q5", q50 + (stab_n + 1) * (h + 1))
                    inc(sc.activation(browz.ap()[0:1, 0:T],
                                      browz.ap()[0:1, 0:T], AF.Sqrt), "rp")
                for g in range(NQ):
                    hg = h * NQ + g
                    for td in tiles_by_hg[hg]:
                        ti = td["ti"]
                        if layer == 1:
                            W(sc, "pe", pe0 + ti + 1)
                            if ti >= 2:
                                W(sc, "pf", pf0 + ti - 1)
                                W(sc, "zf", zf0 + ti - 1)
                            inc(sc.activation(pt[:, ti % 2, :],
                                              psB[:, ti % 2, 0:QS], AF.Exp),
                                "sf")
                        else:
                            W(sc, "pe", pe0 + ti + 1)
                            if ti >= 2:
                                W(sc, "ac", ac0 + ti - 1)
                            inc(sc.activation(qtmp[:, ti % 2, 0:QS],
                                              psB[:, ti % 2, 0:QS],
                                              AF.Square), "sf")
                            if td["bm"] is not None:
                                W(sc, "bm", bm0 + band_pre[ti])
                            else:
                                W(sc, "sf", sf0 + ti + 1)
                            if ti >= 2:
                                W(sc, "pf", pf0 + ti - 1)
                                W(sc, "zf", zf0 + ti - 1)
                            inc(sc.activation(pt[:, ti % 2, :],
                                              qtmp[:, ti % 2, 0:QS],
                                              AF.Square), "ac")
                    if hg >= 1:
                        W(sc, "q2", q2_of_rb(hg - 1))
                        if hg >= 3:
                            W(sc, "af", af0 + hg - 2)
                        inc(sc.activation(
                            rbc[:, ((hg - 1) % 2) * QS:((hg - 1) % 2 + 1) * QS],
                            psE[:, 0:QS], AF.Copy), "q3")
            W(sc, "q2", q2_of_rb(NHG - 1))
            W(sc, "af", af0 + NHG - 2)
            inc(sc.activation(rbc[:, ((NHG - 1) % 2) * QS:
                                   ((NHG - 1) % 2 + 1) * QS],
                              psE[:, 0:QS], AF.Copy), "q3")
        blk.scalar(_sc)

        def _epilogue(v, phg):
            W(v, "pf", pf0 + thru[phg])
            if phg >= 1:
                W(v, "q2", q2_of_rb(phg - 1))
            if layer == 0:
                t6 = inc(v.tensor_scalar(rows.ap()[0:1, RIN:RIN + QS],
                                         psA[0:1, phg % 2, 0:QS],
                                         EPS_L0, None, A.max), "ve")
            else:
                t6 = inc(v.tensor_copy(rows.ap()[0:1, RIN:RIN + QS],
                                       psA[0:1, phg % 2, 0:QS]), "ve")
            v.wait_ge(sems["ve"], t6)
            t7 = inc(v.reciprocal(rows.ap()[0:1, ROUT:ROUT + QS],
                                  rows.ap()[0:1, RIN:RIN + QS]), "ve")
            v.wait_ge(sems["ve"], t7)
            inc(v.tensor_copy(browz.ap()[0:1, 1024:1024 + QS],
                              rows.ap()[0:1, ROUT:ROUT + QS]), "df")

        def _zevac(v, zhg):
            zh, zg = zhg // NQ, zhg % NQ
            W(v, "zf", zf0 + thru[zhg])
            W(v, "q3", q30 + zhg + 1)
            inc(v.tensor_tensor(QV[:, 16 + zh, zg * QS:(zg + 1) * QS],
                                psC[:, zhg % 2, 0:QS],
                                rbc[:, (zhg % 2) * QS:(zhg % 2 + 1) * QS],
                                A.mult), "af")

        def _ve(v):
            W(v, "gi", tok["gi_all"])
            for h in range(NH):
                if layer == 1:
                    for i in range(stab_n):
                        W(v, "cb", cb0 + stab_n * h + i + 1)
                        if i < 2 * KBq:
                            inc(v.tensor_reduce(
                                rtmp[0:1, 128 + i:129 + i],
                                psE[0:1, 0:NKB], AX.X, A.max), "q5")
                        else:
                            dst = rows.ap()[0:1, (i - 2 * KBq) * NKB:
                                            (i - 2 * KBq + 1) * NKB]
                            inc(v.tensor_copy(dst, psE[0:1, 0:NKB]), "q5")
                    W(v, "q5", q50 + (stab_n + 1) * h + stab_n)
                    vchain(v, v.tensor_reduce(rtmp[0:1, 132:133],
                                              rtmp[0:1, 128:128 + 2 * KBq],
                                              AX.X, A.max))
                    if h >= 1:
                        W(v, "pe", pe0 + thru[h * NQ - 1])
                    with nc.allow_low_precision(reason="bf16 c_q row"):
                        t5 = inc(v.tensor_scalar(browz.ap()[0:1, 0:T],
                                                 rows.ap()[0:1, 0:T],
                                                 rtmp[0:1, 132:133],
                                                 None, A.mult), "q5")
                    v.wait_ge(sems["q5"], t5)
                for g in range(NQ):
                    hg = h * NQ + g
                    if hg >= 1:
                        _epilogue(v, hg - 1)
                    if hg >= 2:
                        _zevac(v, hg - 2)
                    for td in tiles_by_hg[hg]:
                        ti = td["ti"]
                        if td["bm"] is not None:
                            W(v, "sf", sf0 + ti + 1)
                            dst = (pt[:, ti % 2, :] if layer == 1
                                   else qtmp[:, ti % 2, 0:QS])
                            inc(v.tensor_tensor(dst, dst,
                                                masks[:, td["bm"], :],
                                                A.mult), "bm")
            _epilogue(v, NHG - 1)
            for zhg in (NHG - 2, NHG - 1):
                _zevac(v, zhg)
        blk.vector(_ve)

        def _te(t):
            def s_mm(td):
                ti = td["ti"]
                W(t, "sf", sf0 + ti - 1 if ti >= 2 else sf0)
                mm = t.matmul(psB[:, ti % 2, 0:QS],
                              kd[:, td["h"] % 2, td["r"],
                                 td["j"] * 128:(td["j"] + 1) * 128],
                              QV[:, td["h"], td["g"] * QS:
                                 (td["g"] + 1) * QS],
                              start=True, stop=(layer == 0))
                if layer == 1:
                    mm = t.matmul(psB[:, ti % 2, 0:QS], neg_b[:],
                                  browz.ap()[0:1, td["g"] * QS:
                                             (td["g"] + 1) * QS],
                                  start=False, stop=True)
                inc(mm, "pe")

            def p_ready(td):
                ti = td["ti"]
                if layer == 1 and td["bm"] is not None:
                    W(t, "bm", bm0 + band_pre[ti])
                elif layer == 1:
                    W(t, "sf", sf0 + ti + 1)
                else:
                    W(t, "ac", ac0 + ti + 1)

            def avd(td):
                ti = td["ti"]; hg = td["hg"]
                p_ready(td)
                if td["first"] and hg >= 2:
                    W(t, "df", df0 + hg - 1)
                inc(t.matmul(psA[0:1, hg % 2, 0:QS], ones_k[:],
                             pt[:, ti % 2, :],
                             start=td["first"], stop=td["last"]), "pf")
                p_ready(td)
                if td["first"] and hg >= 2:
                    W(t, "af", af0 + hg - 1)
                inc(t.matmul(psC[:, hg % 2, 0:QS],
                             vd[:, td["h"] % 2,
                                td["r"] * NTB + td["j"], :],
                             pt[:, ti % 2, :],
                             start=td["first"], stop=td["last"]), "zf")
                if td["last"]:
                    if hg >= 1:
                        W(t, "df", df0 + hg)
                        if hg >= 2:
                            W(t, "q3", q30 + hg - 1)
                        inc(t.matmul(psE[:, 0:QS], ones_b[:],
                                     browz.ap()[0:1, 1024:1024 + QS],
                                     start=True, stop=True), "q2")

            W(t, "af", q_af)
            pend = None
            for h in range(NH):
                if pend is not None:
                    avd(pend)
                    pend = None
                kv_wait(t, h)
                if layer == 1:
                    for i in range(stab_n):
                        W(t, "q1", q10 + stab_n * h + i + 1)
                        if h > 0 or i > 0:
                            W(t, "q5", q50 + (stab_n + 1) * h + i)
                        W(t, "q3", q30 + h * NQ - 1 if h > 0 else 0)
                        inc(t.matmul(psE[0:1, 0:NKB], ones_k[:],
                                     sc512[:, i % 2, 0:NKB],
                                     start=True, stop=True), "cb")
                    W(t, "rp", rp0 + h + 1)
                for g in range(NQ):
                    hg = h * NQ + g
                    for td in tiles_by_hg[hg]:
                        s_mm(td)
                        if pend is not None:
                            avd(pend)
                        pend = td
            avd(pend)
            W(t, "df", df0 + NHG)
            W(t, "q3", q30 + NHG - 1)
            inc(t.matmul(psE[:, 0:QS], ones_b[:],
                         browz.ap()[0:1, 1024:1024 + QS],
                         start=True, stop=True), "q2")
        blk.tensor(_te)
        tok["attn%d_af" % layer] = cnt["af"]
        tok["attn%d_zf" % layer] = cnt["zf"]

    attn(0)

    if upto == -3:
        zaf = cnt["af"]

        def _dbgst3(g):
            W(g, "af", zaf)
            inc(g.dma_start(out=y_fm.rearrange("(c p) t -> p c t", p=128),
                            in_=QV[:, 16:32, :]), "gi", 16)
            W(g, "gi", cnt["gi"])
        blk.gpsimd(_dbgst3)
        for t_ in reversed(ctx):
            t_.__exit__(None, None, None)
        return nc
    if upto == -4:
        zaf = cnt["af"]

        def _dbgst4(g):
            W(g, "af", zaf)
            inc(g.dma_start(out=y_fm.rearrange("(c p) t -> p c t", p=128),
                            in_=QV[:, 0:16, :]), "gi", 16)
            W(g, "gi", cnt["gi"])
        blk.gpsimd(_dbgst4)
        for t_ in reversed(ctx):
            t_.__exit__(None, None, None)
        return nc

    def mk_castwait(sem_name):
        snap = cnt[sem_name]
        def f(v, h):
            W(v, sem_name, snap)
        return f

    def _finish():
        def fin(sc):
            W(sc, "cy", tok["yst1"])
        blk.scalar(fin)

    def _close(nc_):
        for t in reversed(ctx):
            t.__exit__(None, None, None)
        return nc_

    def evac_res_mk(vq=None):
        def emit(h, info):
            nth = info["nth"]; pe0 = info["pe0"]

            def _ve(v):
                for n in range(nth):
                    gn = h * nth + n
                    oc, tb = tile_oc_tb(n)
                    t0 = h * TH + tb * NT
                    W(v, "pe", pe0 + gn + 1)
                    inc(v.scalar_tensor_tensor(
                        stream[:, oc, t0:t0 + NT], psA[:, gn % 2, 0:NT], 1.0,
                        stream[:, oc, t0:t0 + NT], A.mult, A.add), "af")
                    if vq is not None:
                        vq(v)
            blk.vector(_ve)
        return emit

    vq_hook, vq_stores = mk_vq_hook()
    proj(wo0, evac_res_mk(vq=vq_hook), rhs_z=True)
    vq_stores()
    if upto == -6:
        paf = cnt["af"]

        def _dbgst6(g):
            W(g, "af", paf)
            inc(g.dma_start(out=y_fm.rearrange("(c p) t -> p c t", p=128),
                            in_=QV[:, 16:32, :]), "gi", 16)
            W(g, "gi", cnt["gi"])
        blk.gpsimd(_dbgst6)
        for t_ in reversed(ctx):
            t_.__exit__(None, None, None)
        return nc
    if upto == -5:
        paf = cnt["af"]

        def _dbgst5(g):
            W(g, "af", paf)
            inc(g.dma_start(out=y_fm.rearrange("(c p) t -> p c t", p=128),
                            in_=stream[:]), "gi", 16)
            W(g, "gi", cnt["gi"])
        blk.gpsimd(_dbgst5)
        for t_ in reversed(ctx):
            t_.__exit__(None, None, None)
        return nc

    def layernorm(store_half=False, limit=99, halves=2):
        for h in range(halves):
            af_in = cnt["af"]
            ve0 = cnt["ve"]; ac0 = cnt["ac"]; pe0 = cnt["pe"]

            def _cast(v, h=h):
                W(v, "af", af_in)
                for c in range(NC):
                    inc(v.tensor_copy(xbh[:, c, :],
                                      stream[:, c, h * TH:(h + 1) * TH]), "ve")
            blk.vector(_cast)
            cast_tok = cnt["ve"]

            def _sq(sc, h=h):
                W(sc, "ve", cast_tok)
                for c in range(NC):
                    if c >= 2:
                        W(sc, "pe", pe0 + 2 * (c - 1))
                    inc(sc.activation(sc512[:, c % 2, 0:TH], xbh[:, c, :],
                                      AF.Square), "ac")
            if limit < 1:
                continue
            blk.scalar(_sq)

            def _mm(t, h=h):
                W(t, "ve", cast_tok)
                for c in range(NC):
                    inc(t.matmul(psD[0:1, 0:TH], ones_k[:], xbh[:, c, :],
                                 start=(c == 0), stop=(c == NC - 1)), "pe")
                    W(t, "ac", ac0 + c + 1)
                    inc(t.matmul(psE[0:1, 0:TH], ones_k[:],
                                 sc512[:, c % 2, 0:TH],
                                 start=(c == 0), stop=(c == NC - 1)), "pe")
            if limit < 2:
                continue
            blk.tensor(_mm)
            mm_tok = cnt["pe"]

            def _rows(v, h=h):
                W(v, "pe", mm_tok)
                vchain(v, v.tensor_scalar(rows.ap()[0:1, 0:TH], psD[0:1, 0:TH],
                                          1.0 / D, None, A.mult))
                vchain(v, v.tensor_scalar(rows.ap()[0:1, 512:512 + TH],
                                          psE[0:1, 0:TH], 1.0 / D, None,
                                          A.mult))
                vchain(v, v.tensor_tensor(rows.ap()[0:1, 1024:1024 + TH],
                                          rows.ap()[0:1, 0:TH],
                                          rows.ap()[0:1, 0:TH], A.mult))
                vchain(v, v.tensor_tensor(rows.ap()[0:1, 512:512 + TH],
                                          rows.ap()[0:1, 512:512 + TH],
                                          rows.ap()[0:1, 1024:1024 + TH],
                                          A.subtract))
                inc(v.tensor_scalar(rows.ap()[0:1, 512:512 + TH],
                                    rows.ap()[0:1, 512:512 + TH],
                                    EPS_LN, None, A.add), "df")
            if limit < 3:
                continue
            blk.vector(_rows)
            var_tok = cnt["df"]

            def _sqrt(sc, h=h):
                W(sc, "df", var_tok)
                inc(sc.activation(rows.ap()[0:1, 1024:1024 + TH],
                                  rows.ap()[0:1, 512:512 + TH], AF.Sqrt),
                    "q1")
            if limit < 4:
                continue
            blk.scalar(_sqrt)
            sd_tok = cnt["q1"]

            def _ab(v, h=h):
                W(v, "q1", sd_tok)
                vchain(v, v.reciprocal(rows.ap()[0:1, 1536:1536 + TH],
                                       rows.ap()[0:1, 1024:1024 + TH]))
                vchain(v, v.tensor_tensor(rows.ap()[0:1, 0:TH],
                                          rows.ap()[0:1, 0:TH],
                                          rows.ap()[0:1, 1536:1536 + TH],
                                          A.mult))
                inc(v.tensor_scalar(rows.ap()[0:1, 0:TH],
                                    rows.ap()[0:1, 0:TH],
                                    -1.0, None, A.mult), "q3")
            if limit < 5:
                continue
            blk.vector(_ab)
            ab_tok = cnt["q3"]

            def _bc(t, h=h):
                W(t, "q3", ab_tok)
                inc(t.matmul(psD[:, 0:TH], ones_b32[:],
                             rows.ap()[0:1, 1536:1536 + TH],
                             start=True, stop=True), "pe")
                inc(t.matmul(psE[:, 0:TH], ones_b32[:],
                             rows.ap()[0:1, 0:TH],
                             start=True, stop=True), "pe")
            if limit < 6:
                continue
            blk.tensor(_bc)
            bc_tok = cnt["pe"]

            def _norm(v, h=h):
                W(v, "pe", bc_tok)
                for c in range(NC):
                    t1 = inc(v.tensor_tensor(qtmp[:, c % 2, 0:TH],
                                             stream[:, c, h * TH:(h + 1) * TH],
                                             psD[:, 0:TH], A.mult), "ve")
                    if c >= 2:
                        pass
                    v.wait_ge(sems["ve"], t1)
                    inc(v.tensor_tensor(stream[:, c, h * TH:(h + 1) * TH],
                                        qtmp[:, c % 2, 0:TH],
                                        psE[:, 0:TH], A.add), "af")
            if limit < 7:
                continue
            blk.vector(_norm)
            norm_tok = cnt["af"]

            if store_half:
                def _yst(g2, h=h):
                    W(g2, "af", norm_tok)
                    inc(g2.dma_start(
                        out=y_fm[:, h * TH:(h + 1) * TH]
                        .rearrange("(c p) t -> p c t", p=128),
                        in_=stream[:, :, h * TH:(h + 1) * TH]), "cy", 16)
                    tok["yst%d" % h] = cnt["cy"]
                blk.gpsimd(_yst)

    layernorm(store_half=(upto == 1))
    if upto == 1:
        _finish()
        return _close(nc)

    def ffn0_gate(g):
        W(g, "cc", tok["ar1"])

    ffn0_q3 = {"q30": None}

    def evac_ffn0_half(h, info):
        nth = info["nth"]; pe0 = info["pe0"]
        if ffn0_q3["q30"] is None:
            ffn0_q3["q30"] = cnt["q3"]
        q30 = ffn0_q3["q30"]
        af0 = info["af0"]

        def _ve(v):
            for n in range(nth):
                gn = h * nth + n
                oc, tb = tile_oc_tb(n)
                t0 = h * TH + tb * NT
                W(v, "pe", pe0 + gn + 1)
                if gn >= 2:
                    W(v, "af", af0 + gn - 1)
                inc(v.scalar_tensor_tensor(
                    qtmp[:, gn % 2, 0:NT], psA[:, gn % 2, 0:NT],
                    INV_SQRT8, stream[:, oc, t0:t0 + NT],
                    A.mult, A.add), "q3")
        blk.vector(_ve)

        def _sc(sc):
            W(sc, "si", tok["si_all"])
            for n in range(nth):
                gn = h * nth + n
                oc, tb = tile_oc_tb(n)
                t0 = h * TH + tb * NT
                W(sc, "q3", q30 + gn + 1)
                inc(sc.activation(stream[:, oc, t0:t0 + NT],
                                  qtmp[:, gn % 2, 0:NT], AF.Identity,
                                  bias=bias_sb[:, oc:oc + 1]), "af")
        blk.scalar(_sc)

    proj(f_out, evac_ffn0_half, cast_wait=mk_castwait("af"),
         slab_gate=ffn0_gate, use_cast=True)
    tq_ag()
    layernorm(store_half=(upto == 2))
    if upto == 2:
        _finish()
        return _close(nc)

    lnaf = cnt["af"]

    def tg_ln(t):
        W(t, "af", lnaf)
    proj(wk1, evac_copy_mk(1, "k"), te_gate=tg_ln)
    kst1 = (cnt["ck0"], cnt["ck1"])
    proj_tokmaj(wv1, evac_copy_mk(1, "v"))
    vst1 = (cnt["cv0"], cnt["cv1"])
    proj(wq1, evac_copy_mk(1, "q", qscale=1.0 / math.sqrt(HD)))
    tok["q1_af"] = cnt["af"]

    def ag1(g):
        W(g, "ck0", kst1[0]); W(g, "ck1", kst1[1])
        tok["agk1"] = inc(g.collective_compute(
            "AllGather", A.bypass, replica_groups=PAIRS,
            ins=[k_in[1].opt()], outs=[k_out[1].opt()]), "cc")
        W(g, "cv0", vst1[0]); W(g, "cv1", vst1[1])
        tok["agv1"] = inc(g.collective_compute(
            "AllGather", A.bypass, replica_groups=PAIRS,
            ins=[v_in[1].opt()], outs=[v_out[1].opt()]), "cc")
    blk.gpsimd(ag1)

    attn(1)

    proj(wo1, evac_res_mk(), rhs_z=True)
    layernorm(store_half=(upto == 3))
    if upto == 3:
        _finish()
        return _close(nc)

    def ffn1_half(h, dbg=0):
        af_in = cnt["af"]

        ac0 = cnt["ac"]

        def _h128(t):
            W(t, "af", af_in)
            W(t, "gi", tok["gi_all"])
            for c in range(NC):
                if c >= 2:
                    W(t, "ac", ac0 + c - 1)
                inc(t.matmul(psB[:, c % 2, 0:TH], h128[:],
                             stream[:, c, h * TH:(h + 1) * TH],
                             start=True, stop=True), "pe")
        blk.tensor(_h128)
        h128_pe = cnt["pe"] - NC

        def _h128ev(sc):
            for c in range(NC):
                W(sc, "pe", h128_pe + c + 1)
                inc(sc.activation(HID[:, c, :], psB[:, c % 2, 0:TH],
                                  AF.Copy), "ac")
        blk.scalar(_h128ev)
        h128_ac = cnt["ac"]

        def _h16(v):
            W(v, "ac", h128_ac)
            W(v, "tf", cnt["tf"])
            for s in (1, 2, 4, 8):
                for i in range(NC):
                    if i & s:
                        continue
                    a, b = i, i + s
                    t1 = inc(v.tensor_copy(ttmp[:, 0:TH], HID[:, a, :]), "ve")
                    v.wait_ge(sems["ve"], t1)
                    t2 = inc(v.tensor_tensor(HID[:, a, :], HID[:, a, :],
                                             HID[:, b, :], A.add), "ve")
                    v.wait_ge(sems["ve"], t2)
                    t3 = inc(v.tensor_tensor(HID[:, b, :], ttmp[:, 0:TH],
                                             HID[:, b, :], A.subtract), "ve")
                    v.wait_ge(sems["ve"], t3)
            t1 = inc(v.tensor_tensor(qtmp[:, 0, 0:TH], HID[:, 0, :],
                                     HID[:, 1, :], A.max), "ve")
            t2 = inc(v.tensor_tensor(qtmp[:, 1, 0:TH], HID[:, 0, :],
                                     HID[:, 1, :], A.min), "ve")
            v.wait_ge(sems["ve"], t2)
            for c in range(2, NC):
                t1 = inc(v.tensor_tensor(qtmp[:, 0, 0:TH], qtmp[:, 0, 0:TH],
                                         HID[:, c, :], A.max), "ve")
                t2 = inc(v.tensor_tensor(qtmp[:, 1, 0:TH], qtmp[:, 1, 0:TH],
                                         HID[:, c, :], A.min), "ve")
                v.wait_ge(sems["ve"], t2)
            t1 = inc(v.scalar_tensor_tensor(qtmp[:, 0, 0:TH], qtmp[:, 1, 0:TH],
                                            -1.0, qtmp[:, 0, 0:TH],
                                            A.mult, A.max), "ve")
            v.wait_ge(sems["ve"], t1)
            tok["fold1"] = cnt["ve"]
        blk.vector(_h16)

        def _par(g):
            W(g, "ve", tok["fold1"])
            tok["par1"] = inc(g.reduce_max(rows.ap()[0:1, 1536:1536 + TH],
                                           qtmp[:, 0, 0:TH], AX.C), "gc")
        blk.gpsimd(_par)

        def _rows1(v):
            W(v, "gc", tok["par1"])
            vchain(v, v.tensor_scalar(rows.ap()[0:1, 0:TH],
                                      rows.ap()[0:1, 1536:1536 + TH],
                                      1.0 / math.sqrt(D), EPS_Q,
                                      A.mult, A.max))
            vchain(v, v.reciprocal(rows.ap()[0:1, 512:512 + TH],
                                   rows.ap()[0:1, 0:TH]))
            vchain(v, v.tensor_scalar(rows.ap()[0:1, 512:512 + TH],
                                      rows.ap()[0:1, 512:512 + TH],
                                      127.0 / math.sqrt(D), None, A.mult))
            inc(v.tensor_scalar(rows.ap()[0:1, 1024:1024 + TH],
                                rows.ap()[0:1, 0:TH],
                                1.0 / 127.0, None, A.mult), "q3")
        blk.vector(_rows1)
        r1_tok = cnt["q3"]

        def _bc1(t):
            W(t, "q3", r1_tok)
            inc(t.matmul(psD[:, 0:TH], ones_b32[:],
                         rows.ap()[0:1, 512:512 + TH],
                         start=True, stop=True), "pe")
            inc(t.matmul(psE[:, 0:TH], ones_b32[:],
                         rows.ap()[0:1, 1024:1024 + TH],
                         start=True, stop=True), "pe")
        blk.tensor(_bc1)
        bc1_tok = cnt["pe"]

        def _cp1(sc):
            W(sc, "pe", bc1_tok)
            t1 = inc(sc.activation(cbc[:, 0:TH], psD[:, 0:TH], AF.Copy), "ac")
            inc(sc.activation(qtmp[:, 1, 0:TH], psE[:, 0:TH], AF.Copy), "ac")
        blk.scalar(_cp1)
        cp1_tok = cnt["ac"]

        def _round1(v):
            W(v, "ac", cp1_tok)
            tfb = cnt["tf"]
            for c in range(NC):
                if c >= 1:
                    W(v, "tf", tfb + c)
                t1 = inc(v.tensor_tensor(qtmp[:, 0, 0:TH], HID[:, c, :],
                                         cbc[:, 0:TH], A.mult), "ve")
                v.wait_ge(sems["ve"], t1)
                t2 = inc(v.tensor_scalar(qint[:, 0:TH], qtmp[:, 0, 0:TH],
                                         256.5, None, A.add), "ve")
                v.wait_ge(sems["ve"], t2)
                inc(v.scalar_tensor_tensor(xq1[:, c, :], qint[:, 0:TH],
                                           -256.0, qtmp[:, 1, 0:TH],
                                           A.add, A.mult), "tf")
        blk.vector(_round1)
        xq1_tok = cnt["tf"]

        if dbg == -20:
            def _d20(g):
                W(g, "tf", xq1_tok)
                inc(g.dma_start(out=y_fm[:, 0:TH]
                                .rearrange("(c p) t -> p c t", p=128),
                                in_=xq1[:]), "gi", 16)
                W(g, "gi", cnt["gi"])
            blk.gpsimd(_d20)
            return

        ssb = [cnt["s0"], cnt["s1"]]
        pe2 = cnt["pe"]; zf2 = cnt["zf"]
        q12 = cnt["q1"]; q32 = cnt["q3"]
        ngrp = (FF // NCORES // 512) * NCORES // 1
        NGRP = NCORES * 2

        def gate_off(r):
            return r * TERN

        def _syg(s_):
            W(s_, "cc", tok["agt"])
            nsl = 0
            for r in range(NCORES):
                for sub in range(2):
                    for mat, moff in ((0, 0), (1, GT)):
                        if nsl >= 2:
                            pg, pm = (nsl - 2) // 2, (nsl - 2) % 2
                            W(s_, "pe" if pm == 0 else "zf",
                              (pe2 if pm == 0 else zf2) + 4 * (pg + 1))
                        else:
                            W(s_, "pe", pe2)
                        base = r * TERN + moff + sub * GB
                        src = tern_out[base:base + GB] \
                            .rearrange("(p c o) -> p c o", c=16, o=512)
                        inc(s_.dma_start(out=ws[:, nsl % 2, :, :], in_=src),
                            "s%d" % (nsl % 2), 16)
                        nsl += 1
        blk.sync(_syg)

        gbanks = [psA[:, 0, 0:TH], psA[:, 1, 0:TH],
                  psB[:, 0, 0:TH], psB[:, 1, 0:TH]]
        ubanks = [psC[:, 0, 0:TH], psC[:, 1, 0:TH],
                  psD[:, 0:TH], psE[:, 0:TH]]

        def _teg(t):
            W(t, "tf", xq1_tok)
            nsl = 0
            for grp in range(NGRP):
                for mat in range(2):
                    W(t, "s%d" % (nsl % 2), ssb[nsl % 2] + 16 * (nsl // 2 + 1))
                    if mat == 0:
                        if grp >= 1:
                            W(t, "q1", q12 + 4 * grp)
                    else:
                        if grp >= 1:
                            W(t, "q3", q32 + 4 * grp)
                    banks = gbanks if mat == 0 else ubanks
                    for c in range(NC):
                        for oc in range(4):
                            iv = t.matmul(banks[oc],
                                          ws[:, nsl % 2, c,
                                             oc * 128:(oc + 1) * 128],
                                          xq1[:, c, :],
                                          start=(c == 0), stop=(c == NC - 1))
                            if c == NC - 1:
                                inc(iv, "pe" if mat == 0 else "zf")
                    nsl += 1
        blk.tensor(_teg)

        def _silu(sc):
            for grp in range(NGRP):
                for oc in range(4):
                    W(sc, "pe", pe2 + 4 * grp + oc + 1)
                    if grp >= 1 or oc >= 2:
                        W(sc, "q3", q32 + 4 * grp + oc - 1)
                    inc(sc.activation(sc512[:, oc % 2, 0:TH],
                                      gbanks[oc], AF.Silu,
                                      scale=scl[:, 4:5]), "q1")
        blk.scalar(_silu)

        def _stt(v):
            for grp in range(NGRP):
                for oc in range(4):
                    fc = grp * 4 + oc
                    W(v, "zf", zf2 + 4 * grp + oc + 1)
                    W(v, "q1", q12 + 4 * grp + oc + 1)
                    inc(v.scalar_tensor_tensor(HID[:, fc, :], ubanks[oc],
                                               scl[:, 5:6],
                                               sc512[:, oc % 2, 0:TH],
                                               A.mult, A.mult), "q3")
        blk.vector(_stt)
        gu_done = cnt["q3"]

        if dbg == -21:
            def _d21(g):
                W(g, "q3", gu_done)
                inc(g.dma_start(out=y_fm[:, 0:TH]
                                .rearrange("(c p) t -> p c t", p=128),
                                in_=HID[:, 0:16, :]), "gi", 16)
                inc(g.dma_start(out=y_fm[:, TH:2 * TH]
                                .rearrange("(c p) t -> p c t", p=128),
                                in_=HID[:, 16:32, :]), "gi", 16)
                W(g, "gi", cnt["gi"])
            blk.gpsimd(_d21)
            return

        q1_all = cnt["q1"]
        hb_pe = cnt["pe"]
        hb_ac0 = cnt["ac"]

        def _h128b(t):
            W(t, "q3", gu_done)
            W(t, "q1", q1_all)
            for c in range(NFC):
                if c >= 2:
                    W(t, "ac", hb_ac0 + c - 1)
                inc(t.matmul(psB[:, c % 2, 0:TH], h128[:], HID[:, c, :],
                             start=True, stop=True), "pe")
        blk.tensor(_h128b)

        def _h128bev(sc):
            for c in range(NFC):
                W(sc, "pe", hb_pe + c + 1)
                inc(sc.activation(HID[:, c, :], psB[:, c % 2, 0:TH],
                                  AF.Copy), "ac")
        blk.scalar(_h128bev)
        hb_ac = cnt["ac"]

        def _h64(v):
            W(v, "ac", hb_ac)
            for s in (1, 2, 4, 8, 16, 32):
                for i in range(NFC):
                    if i & s:
                        continue
                    a, b = i, i + s
                    t1 = inc(v.tensor_copy(ttmp[:, 0:TH], HID[:, a, :]), "ve")
                    v.wait_ge(sems["ve"], t1)
                    t2 = inc(v.tensor_tensor(HID[:, a, :], HID[:, a, :],
                                             HID[:, b, :], A.add), "ve")
                    v.wait_ge(sems["ve"], t2)
                    t3 = inc(v.tensor_tensor(HID[:, b, :], ttmp[:, 0:TH],
                                             HID[:, b, :], A.subtract), "ve")
                    v.wait_ge(sems["ve"], t3)
            t1 = inc(v.tensor_tensor(qtmp[:, 0, 0:TH], HID[:, 0, :],
                                     HID[:, 1, :], A.max), "ve")
            t2 = inc(v.tensor_tensor(qtmp[:, 1, 0:TH], HID[:, 0, :],
                                     HID[:, 1, :], A.min), "ve")
            v.wait_ge(sems["ve"], t2)
            for c in range(2, NFC):
                t1 = inc(v.tensor_tensor(qtmp[:, 0, 0:TH], qtmp[:, 0, 0:TH],
                                         HID[:, c, :], A.max), "ve")
                t2 = inc(v.tensor_tensor(qtmp[:, 1, 0:TH], qtmp[:, 1, 0:TH],
                                         HID[:, c, :], A.min), "ve")
                v.wait_ge(sems["ve"], t2)
            t1 = inc(v.scalar_tensor_tensor(qtmp[:, 0, 0:TH], qtmp[:, 1, 0:TH],
                                            -1.0, qtmp[:, 0, 0:TH],
                                            A.mult, A.max), "ve")
            v.wait_ge(sems["ve"], t1)
            tok["fold2"] = cnt["ve"]
        blk.vector(_h64)

        def _par2(g):
            W(g, "ve", tok["fold2"])
            tok["par2"] = inc(g.reduce_max(rows.ap()[0:1, 1536:1536 + TH],
                                           qtmp[:, 0, 0:TH], AX.C), "gc")
        blk.gpsimd(_par2)

        def _rows2(v):
            W(v, "gc", tok["par2"])
            vchain(v, v.tensor_scalar(rows.ap()[0:1, 0:TH],
                                      rows.ap()[0:1, 1536:1536 + TH],
                                      1.0 / math.sqrt(FF), EPS_Q,
                                      A.mult, A.max))
            vchain(v, v.reciprocal(rows.ap()[0:1, 512:512 + TH],
                                   rows.ap()[0:1, 0:TH]))
            vchain(v, v.tensor_scalar(rows.ap()[0:1, 512:512 + TH],
                                      rows.ap()[0:1, 512:512 + TH],
                                      127.0 / math.sqrt(FF), None, A.mult))
            inc(v.tensor_scalar(rows.ap()[0:1, 1024:1024 + TH],
                                rows.ap()[0:1, 0:TH],
                                1.0 / 127.0, None, A.mult), "q3")
        blk.vector(_rows2)
        r2_tok = cnt["q3"]

        def _bc2(t):
            W(t, "q3", r2_tok)
            inc(t.matmul(psD[:, 0:TH], ones_b32[:],
                         rows.ap()[0:1, 512:512 + TH],
                         start=True, stop=True), "pe")
            inc(t.matmul(psE[:, 0:TH], ones_b32[:],
                         rows.ap()[0:1, 1024:1024 + TH],
                         start=True, stop=True), "pe")
        blk.tensor(_bc2)
        bc2_tok = cnt["pe"]

        def _cp2(sc):
            W(sc, "pe", bc2_tok)
            inc(sc.activation(cbc[:, 0:TH], psD[:, 0:TH], AF.Copy), "ac")
            inc(sc.activation(qtmp[:, 1, 0:TH], psE[:, 0:TH], AF.Copy), "ac")
        blk.scalar(_cp2)
        cp2_tok = cnt["ac"]

        tf2 = cnt["tf"]

        def _round2(v):
            W(v, "ac", cp2_tok)
            tfb = cnt["tf"]
            for c in range(NFC):
                if c >= 1:
                    W(v, "tf", tfb + c)
                t1 = inc(v.tensor_tensor(qtmp[:, 0, 0:TH], HID[:, c, :],
                                         cbc[:, 0:TH], A.mult), "ve")
                v.wait_ge(sems["ve"], t1)
                t2 = inc(v.tensor_scalar(qint[:, 0:TH], qtmp[:, 0, 0:TH],
                                         256.5, None, A.add), "ve")
                v.wait_ge(sems["ve"], t2)
                inc(v.scalar_tensor_tensor(HID[:, c, :], qint[:, 0:TH],
                                           -256.0, qtmp[:, 1, 0:TH],
                                           A.add, A.mult), "tf")
        blk.vector(_round2)

        if dbg == -22:
            def _d22(g):
                W(g, "tf", cnt["tf"])
                inc(g.dma_start(out=y_fm[:, 0:TH]
                                .rearrange("(c p) t -> p c t", p=128),
                                in_=HID[:, 0:16, :]), "gi", 16)
                inc(g.dma_start(out=y_fm[:, TH:2 * TH]
                                .rearrange("(c p) t -> p c t", p=128),
                                in_=HID[:, 16:32, :]), "gi", 16)
                W(g, "gi", cnt["gi"])
            blk.gpsimd(_d22)
            return

        sdb = [cnt["s0"], cnt["s1"]]
        pf3 = cnt["pf"]; af3 = cnt["af"]

        def _syd(s_):
            nsl = 0
            for r in range(NCORES):
                for s4 in range(4):
                    base = r * TERN + 2 * GT + s4 * DB
                    src = tern_out[base:base + DB] \
                        .rearrange("(p c o) -> p c o", c=16, o=DN)
                    if nsl >= 2:
                        W(s_, "pf", pf3 + nsl - 1)
                    else:
                        W(s_, "pe", pe2 + 4 * NGRP)
                        W(s_, "zf", zf2 + 4 * NGRP)
                    inc(s_.dma_start(out=ws[:, nsl % 2, :, 0:256],
                                     in_=src), "s%d" % (nsl % 2), 16)
                    nsl += 1
        blk.sync(_syd)

        def _ted(t):
            nsl = 0
            for r in range(NCORES):
                if r >= 1:
                    W(t, "af", af3 + 2 * r)
                for s4 in range(4):
                    W(t, "s%d" % (nsl % 2), sdb[nsl % 2] + 16 * (nsl // 2 + 1))
                    W(t, "tf", tf2 + (s4 + 1) * 16)
                    for c16 in range(16):
                        cg = s4 * 16 + c16
                        for oc in range(2):
                            iv = t.matmul(psA[:, oc, 0:TH],
                                          ws[:, nsl % 2, c16,
                                             oc * 128:(oc + 1) * 128],
                                          HID[:, cg, :],
                                          start=(cg == 0), stop=(cg == NFC - 1))
                            if c16 == 15 and oc == 1:
                                inc(iv, "pf")
                    nsl += 1
        blk.tensor(_ted)

        def _evd(v):
            for r in range(NCORES):
                for oc in range(2):
                    W(v, "pf", pf3 + 4 * (r + 1))
                    ocg = 2 * r + oc
                    inc(v.scalar_tensor_tensor(
                        stream[:, ocg, h * TH:(h + 1) * TH],
                        psA[:, oc, 0:TH], scl[:, 6:7],
                        stream[:, ocg, h * TH:(h + 1) * TH],
                        A.mult, A.add), "af")
        blk.vector(_evd)

        if dbg == -23:
            def _d23(g):
                W(g, "af", cnt["af"])
                inc(g.dma_start(out=y_fm.rearrange("(c p) t -> p c t", p=128),
                                in_=stream[:]), "gi", 16)
                W(g, "gi", cnt["gi"])
            blk.gpsimd(_d23)
            return

    if upto in (-20, -21, -22, -23):
        import os as _os
        ffn1_half(0, dbg=upto)
        for t_ in reversed(ctx):
            t_.__exit__(None, None, None)
        return nc

    for h in range(2):
        ffn1_half(h)
    layernorm(store_half=True)

    _finish()
    return _close(nc)


_CACHE = {}


def _host_stage(inputs, S):
    T = S // 2
    QS = min(512, T)
    NBM = 2 * (QS // 128)
    d = {}
    d["wq0"] = np.ascontiguousarray(inputs["L0_attn_Wq"].T)
    d["wk0"] = np.ascontiguousarray(inputs["L0_attn_Wk"].T)
    d["wv0"] = np.ascontiguousarray(inputs["L0_attn_Wv"].T)
    d["wo0"] = np.ascontiguousarray(inputs["L0_attn_Wo"].T)
    d["fct"] = None
    d["bias0"] = np.ascontiguousarray(
        np.asarray(inputs["L0_ffn_bias"]).reshape(NC, 128).T)
    d["wq1"] = np.ascontiguousarray(
        np.transpose(inputs["L1_attn_W_Q"], (2, 0, 1)).reshape(D, D))
    d["wk1"] = np.ascontiguousarray(
        np.transpose(inputs["L1_attn_W_K"], (2, 0, 1)).reshape(D, D))
    d["wv1"] = np.ascontiguousarray(
        np.transpose(inputs["L1_attn_W_V"], (2, 0, 1)).reshape(D, D))
    d["wo1"] = np.ascontiguousarray(
        np.transpose(inputs["L1_attn_W_O"], (0, 2, 1)).reshape(D, D))
    gT = np.ascontiguousarray(inputs["L1_ffn_w_gate"].T)
    uT = np.ascontiguousarray(inputs["L1_ffn_w_up"].T)
    dT = np.ascontiguousarray(inputs["L1_ffn_w_down"].T)
    H = np.array([[1.0]], dtype=np.float32)
    while H.shape[0] < 128:
        H = np.block([[H, H], [H, -H]])
    d["h128t"] = np.ascontiguousarray(H.astype(np.float32))

    x = np.asarray(inputs["L0_attn_x"])
    B = x.shape[0]
    maps = []
    for core in range(NCORES):
        b = core // 2
        p = core % 2
        blocks = [2 * j + p for j in range(T // 128)]
        cols = np.concatenate([np.arange(128 * t, 128 * (t + 1))
                               for t in blocks])
        m = dict(d)
        m["fct"] = np.asarray(inputs["L0_ffn_factors"][core])
        xc = np.ascontiguousarray(x[b].T[:, cols])
        m["x_fm"] = xc
        m["tg_sl"] = np.ascontiguousarray(
            gT[:, core * (FF // NCORES):(core + 1) * (FF // NCORES)])
        m["tu_sl"] = np.ascontiguousarray(
            uT[:, core * (FF // NCORES):(core + 1) * (FF // NCORES)])
        m["td_sl"] = np.ascontiguousarray(
            dT[:, core * (D // NCORES):(core + 1) * (D // NCORES)])
        mk = np.zeros((NBM, 128, QS), dtype=np.float32)
        kp = np.arange(128)
        qc = np.arange(QS)
        qblk = qc // 128
        qoff = 128 * (2 * qblk + p) + (qc % 128)
        for mm in range(NBM):
            koff = 128 * mm + kp
            mk[mm] = (koff[:, None] <= qoff[None, :]).astype(np.float32)
        m["maskt"] = mk.reshape(NBM * 128, QS)
        pos = cols.astype(np.float32)
        half = HD // 2
        theta = (10000.0 ** (-np.arange(half, dtype=np.float32) * 2.0 / HD))
        ang = theta[:, None] * pos[None, :]
        cosv = np.cos(ang).astype(np.float32)
        sinv = np.sin(ang).astype(np.float32)
        m["ropec"] = np.ascontiguousarray(
            np.concatenate([cosv, cosv], axis=0))
        m["ropes"] = np.ascontiguousarray(
            np.concatenate([-sinv, sinv], axis=0))
        maps.append(m)
    return maps, x.shape


def kernel(**inputs):
    S = np.asarray(inputs["L0_attn_x"]).shape[1]
    if S not in _CACHE:
        _CACHE[S] = build(S)
    nc = _CACHE[S]
    maps, xshape = _host_stage(inputs, S)
    res = run_bass_kernel_spmd(nc, maps, core_ids=list(range(NCORES)))
    B, S_, D_ = xshape
    T = S // 2
    out = np.empty((B, S_, D_), dtype=np.float32)
    for core in range(NCORES):
        b = core // 2
        p = core % 2
        y = res.results[core]["y_fm"]
        for j in range(T // 128):
            tblk = 2 * j + p
            out[b, 128 * tblk:128 * (tblk + 1), :] = \
                y[:, 128 * j:128 * (j + 1)].T
    return out


# version 07
# speedup vs baseline: 1.3247x; 1.0390x over previous
import math
import numpy as np
import concourse.bass as bass
import concourse.mybir as mybir
from concourse.bass_utils import run_bass_kernel_spmd
from concourse.bass_isa import ReduceOp

F32 = mybir.dt.float32
BF16 = mybir.dt.bfloat16
A = mybir.AluOpType
AF = mybir.ActivationFunctionType
AX = mybir.AxisListType

D = 2048
NC = D // 128
NH = 16
HD = 128
FF = 8192
NFC = FF // 128
EPS_LN = 1e-5
EPS_L0 = 1e-6
EPS_Q = 1e-6
NCORES = 8
GT = D * (FF // NCORES)
DT = FF * (D // NCORES)
TERN = 2 * GT + DT
NW = float(D * FF)
ALL8 = [list(range(NCORES))]
PAIRS = [[0, 1], [2, 3], [4, 5], [6, 7]]
INV_SQRT8 = 1.0 / math.sqrt(8.0)
MAGIC = 256.5


def build(S=2048, upto=4):
    T = S // 2
    TH = T // 2
    NTB = T // 128
    QS = min(512, T)
    NQ = T // QS
    LQB = QS // 128
    NBM = 2 * LQB
    NT = min(512, TH)
    NB_H = TH // NT
    KBq = max(1, T // 512)
    NKB = min(512, T)
    nc = bass.Bass("TRN2", target_bir_lowering=False, debug=False)

    def din(name, shape):
        return nc.dram_tensor(name, shape, F32, kind="ExternalInput").ap()

    x_fm = din("x_fm", [D, T])
    wq0 = din("wq0", [D, D]); wk0 = din("wk0", [D, D])
    wv0 = din("wv0", [D, D]); wo0 = din("wo0", [D, D])
    wq1 = din("wq1", [D, D]); wk1 = din("wk1", [D, D])
    wv1 = din("wv1", [D, D]); wo1 = din("wo1", [D, D])
    fct = din("fct", [D, D])
    bias0 = din("bias0", [128, NC])
    tg_sl = din("tg_sl", [D, FF // NCORES])
    tu_sl = din("tu_sl", [D, FF // NCORES])
    td_sl = din("td_sl", [FF, D // NCORES])
    maskt = din("maskt", [NBM * 128, QS])
    ropec = din("ropec", [128, T])
    ropes = din("ropes", [128, T])
    h128t = din("h128t", [128, 128])
    y_fm = nc.dram_tensor("y_fm", [D, T], F32, kind="ExternalOutput").ap()

    f_in = nc.dram_tensor("f_in", [D, D], F32).ap()
    f_out = nc.dram_tensor("f_out", [D, D], F32, addr_space="Shared").ap()
    f_sc_in = nc.dram_tensor("f_sc_in", [128, 4], F32).ap()
    f_sc_out = nc.dram_tensor("f_sc_out", [128, 4], F32).ap()
    tern_in = nc.dram_tensor("tern_in", [TERN], BF16).ap()
    tern_out = nc.dram_tensor("tern_out", [NCORES * TERN], BF16,
                              addr_space="Shared").ap()
    k_in = [nc.dram_tensor(f"k_in{l}", [D, T], BF16).ap() for l in range(2)]
    k_out = [nc.dram_tensor(f"k_out{l}", [2 * D, T], BF16).ap() for l in range(2)]
    v_in = [nc.dram_tensor(f"v_in{l}", [NH * T, HD], BF16).ap() for l in range(2)]
    v_out = [nc.dram_tensor(f"v_out{l}", [2 * NH * T, HD], BF16).ap()
             for l in range(2)]

    ctx = []

    def sbt(name, shape, dt=BF16):
        t = nc.sbuf_tensor(name, shape, dt); ctx.append(t); return t.__enter__()

    def pst(name, shape):
        t = nc.psum_tensor(name, shape, F32); ctx.append(t); return t.__enter__()

    stream = sbt("stream", [128, NC, T])
    xbh = sbt("xbh", [128, NC, TH])
    big = sbt("big", [128, 32 * T])
    QV = big.ap().rearrange("p (c t) -> p c t", t=T)
    HID = big.ap().rearrange("p (c t) -> p c t", t=TH)

    masks = sbt("masks", [128, NBM, QS])
    pt = sbt("pt", [128, 2, QS])
    qtmp = sbt("qtmp", [128, 2, 512], F32)
    cbc = sbt("cbc", [128, 512], F32)
    rbc = sbt("rbc", [128, 2 * QS])
    kvq = sbt("kvq", [128, 8 * T])
    kd = kvq.ap()[:, 0:4 * T].rearrange("p (b r t) -> p b r t", b=2, r=2)
    vd = kvq.ap()[:, 4 * T:8 * T].rearrange("p (b k e) -> p b k e",
                                            b=2, e=HD)
    xq1 = kvq.ap().rearrange("p (c t) -> p c t", t=TH)[:, 0:NC, :]
    ws = sbt("ws", [128, 2, NC, 512])
    tstage = sbt("tstage", [128, 2, 512], F32)
    qint = tstage.ap()[:, 0, :].bitcast(mybir.dt.int32)
    ttmp = sbt("ttmp", [128, 512])
    ttrn = sbt("ttrn", [128, 2, 512])
    cosf = sbt("cosf", [128, T])
    sinf = sbt("sinf", [128, T])
    h128 = sbt("h128", [128, 128])
    sc512 = sbt("sc512", [128, 2, 512])
    rk = sbt("rk", [128, 2, 512])
    rsw = sbt("rsw", [128, 512])
    ones_k = sbt("ones_k", [128, 1])
    ones_b = sbt("ones_b", [1, 128])
    neg_b = sbt("neg_b", [1, 128])
    ones_k32 = sbt("ones_k32", [128, 1], F32)
    ones_b32 = sbt("ones_b32", [1, 128], F32)
    browz = sbt("browz", [1, 1536])
    bias_sb = sbt("bias_sb", [128, NC], F32)
    scl = sbt("scl", [128, 8], F32)
    rtmp = sbt("rtmp", [128, 136], F32)
    rows = sbt("rows", [1, 2048], F32)

    psA = pst("psA", [128, 2, 512])
    psB = pst("psB", [128, 2, 512])
    psC = pst("psC", [128, 2, 512])
    psD = pst("psD", [128, 512])
    psE = pst("psE", [128, 512])

    blk = nc.Block().__enter__()
    ctx.append(blk)

    SEMS = ["gi", "si", "cf", "cy", "g0", "g1", "s0", "s1", "k0", "k1",
            "ck0", "ck1", "cv0", "cv1", "ct0", "ct1", "cu0", "cu1",
            "t0", "t1", "u0", "u1", "ab", "ts",
            "cc", "gc", "pe", "ve", "ac", "sf", "pf", "zf",
            "kf", "af", "df", "cb", "rp", "q1", "q2", "q3", "q5", "tf", "bm"]
    sems = {}
    cnt = {}
    for sname in SEMS:
        sems[sname] = nc.semaphore(sname).__enter__()
        cnt[sname] = 0

    def inc(inst, name, n=1):
        inst.then_inc(sems[name], n)
        cnt[name] += n
        return cnt[name]

    def W(eng, name, v):
        if v > 0:
            eng.wait_ge(sems[name], v)

    def vchain(v, inst):
        t = inc(inst, "ve")
        v.wait_ge(sems["ve"], t)
        return t

    tok = {}

    def ph0_gp(g):
        inc(g.memset(ones_k[:], 1.0), "gc")
        inc(g.memset(ones_b[:], 1.0), "gc")
        inc(g.memset(neg_b[:], -1.0), "gc")
        inc(g.memset(ones_k32[:], 1.0), "gc")
        inc(g.memset(ones_b32[:], 1.0), "gc")
        inc(g.memset(rows[:], 0.0), "gc")
        tok["consts"] = cnt["gc"]
        inc(g.dma_start(out=f_in, in_=fct), "ts", 16)
        tok["fct_ld"] = cnt["ts"]
        inc(g.dma_start(
            out=masks[:], in_=maskt.rearrange("(m p) q -> p m q", p=128)), "gi", 16)
        inc(g.dma_start(out=cosf[:], in_=ropec), "gi", 16)
        inc(g.dma_start(out=sinf[:], in_=ropes), "gi", 16)
        inc(g.dma_start(out=h128[:], in_=h128t), "gi", 16)
        inc(g.dma_start(
            out=stream[:], in_=x_fm.rearrange("(c p) t -> p c t", p=128)),
            "gi", 16)
        tok["gi_all"] = cnt["gi"]
        W(g, "ts", tok["fct_ld"])
        tok["ar1"] = inc(g.collective_compute(
            "AllReduce", A.add, replica_groups=ALL8,
            ins=[f_in.opt()], outs=[f_out.opt()]), "cc")
    blk.gpsimd(ph0_gp)

    def ph0_sy(s_):
        inc(s_.dma_start(out=bias_sb[:], in_=bias0), "si", 16)
        tok["si_all"] = cnt["si"]
    blk.sync(ph0_sy)

    t_slabs = []
    GCH = (FF // NCORES) // 512
    for r in range(D // 128):
        for ch in range(GCH):
            t_slabs.append((tg_sl[r * 128:(r + 1) * 128,
                                  ch * 512:(ch + 1) * 512], 0, 512))
    for r in range(D // 128):
        for ch in range(GCH):
            t_slabs.append((tu_sl[r * 128:(r + 1) * 128,
                                  ch * 512:(ch + 1) * 512], 1, 512))
    for r in range(FF // 128):
        t_slabs.append((td_sl[r * 128:(r + 1) * 128, :], 2, D // NCORES))
    NSLAB = len(t_slabs)
    mi_first = [min(i for i, s in enumerate(t_slabs) if s[1] == m)
                for m in range(3)]
    mi_last = [max(i for i, s in enumerate(t_slabs) if s[1] == m)
               for m in range(3)]
    sh_state = {"sb": [cnt["s0"], cnt["s1"]], "ab0": cnt["ab"],
                "tfs0": cnt["tf"], "sc_i": 0, "te_i": 0, "ntile": 0}

    def ph1_sy(s_):
        ab0 = sh_state["ab0"]
        for i, (src, mi, cols) in enumerate(t_slabs):
            if i >= 2:
                W(s_, "ab", ab0 + i - 1)
            inc(s_.dma_start(out=tstage[:, i % 2, 0:cols], in_=src),
                "s%d" % (i % 2), 16)
    blk.sync(ph1_sy)

    SROWS = [psD[0:1, 0:512], psE[0:1, 0:512], psB[0:1, 0, 0:512]]

    def _acc_mm(t, i):
        src, mi, cols = t_slabs[i]
        W(t, "ab", sh_state["ab0"] + i + 1)
        inc(t.matmul(SROWS[mi][:, 0:cols], ones_k[:],
                     ttrn[:, i % 2, 0:cols],
                     start=(i == mi_first[mi]),
                     stop=(i == mi_last[mi])), "tf")

    def mk_scale_hook(ntiles):
        per = -(-NSLAB // ntiles)
        sh_state["te_done"] = 0

        def sc_hook(sc):
            i = sh_state["sc_i"]
            for j in range(per):
                if i >= NSLAB:
                    break
                src, mi, cols = t_slabs[i]
                W(sc, "s%d" % (i % 2),
                  sh_state["sb"][i % 2] + 16 * (i // 2 + 1))
                if i >= 2:
                    W(sc, "tf", sh_state["tfs0"] + i - 1)
                inc(sc.activation(ttrn[:, i % 2, 0:cols],
                                  tstage[:, i % 2, 0:cols], AF.Abs), "ab")
                i += 1
            sh_state["sc_i"] = i

        def te_hook(t):
            k = sh_state["te_i"]
            for i in range(sh_state["te_done"], min(k * per, NSLAB)):
                _acc_mm(t, i)
                sh_state["te_done"] = i + 1
            sh_state["te_i"] = k + 1

        def te_flush():
            def _tf(t):
                for i in range(sh_state["te_done"],
                               min(sh_state["sc_i"], NSLAB)):
                    _acc_mm(t, i)
                    sh_state["te_done"] = i + 1
            blk.tensor(_tf)
        return sc_hook, te_hook, te_flush

    def ph1_finish():
        tfs0 = sh_state["tfs0"]

        def _tf(t):
            for i in range(sh_state["te_done"], NSLAB):
                _acc_mm(t, i)
                sh_state["te_done"] = i + 1
        blk.tensor(_tf)

        def _ve(v):
            W(v, "tf", tfs0 + NSLAB)
            for m in range(3):
                cols = 512 if m < 2 else (D // NCORES)
                vchain(v, v.tensor_reduce(rows.ap()[0:1, m:m + 1],
                                          SROWS[m][:, 0:cols], AX.X, A.add))
            tok["sctot"] = cnt["ve"]
        blk.vector(_ve)

        def _te(t):
            W(t, "ve", tok["sctot"])
            tok["scbc"] = inc(t.matmul(psD[:, 0:4], ones_b32[:],
                                       rows.ap()[0:1, 0:4],
                                       start=True, stop=True), "pe")
        blk.tensor(_te)

        def _sc(sc):
            W(sc, "pe", tok["scbc"])
            t1 = inc(sc.activation(rtmp[:, 0:4], psD[:, 0:4], AF.Copy), "ac")
            sc.wait_ge(sems["ac"], t1)
            tok["scst"] = inc(sc.dma_start(out=f_sc_in, in_=rtmp[:, 0:4]),
                              "cf", 16)
        blk.scalar(_sc)

        def _gp(g):
            W(g, "cf", tok["scst"])
            tok["ar2"] = inc(g.collective_compute(
                "AllReduce", A.add, replica_groups=ALL8,
                ins=[f_sc_in.opt()], outs=[f_sc_out.opt()]), "cc")
        blk.gpsimd(_gp)

    def proj(wsrc, evac_half, cast_wait=None, rhs_z=False, slab_gate=None,
             te_hook=None, use_cast=False, te_gate=None):
        pe0 = cnt["pe"]; af0 = cnt["af"]
        gb = [cnt["g0"], cnt["g1"]]
        nts = 4 * NB_H
        nth = 16 * NB_H
        info = {"pe0": pe0, "af0": af0, "nth": nth, "cast": [0, 0], "NT": NT}
        for h in range(2):
            if use_cast:
                def _cast(v, h=h):
                    if cast_wait is not None:
                        cast_wait(v, h)
                    W(v, "pe", pe0 + (nth if h == 1 else 0))
                    for c in range(NC):
                        inc(v.tensor_copy(xbh[:, c, :],
                                          stream[:, c, h * TH:(h + 1) * TH]),
                            "ve")
                    info["cast"][h] = cnt["ve"]
                blk.vector(_cast)

            def _slabs(g, h=h):
                if slab_gate is not None and h == 0:
                    slab_gate(g)
                for i in range(4):
                    gi = h * 4 + i
                    W(g, "pe", pe0 + (gi - 1) * nts if gi >= 2 else pe0)
                    inc(g.dma_start(
                        out=ws[:, gi % 2, :, :],
                        in_=wsrc[:, i * 512:(i + 1) * 512]
                        .rearrange("(c p) o -> p c o", p=128)),
                        "g%d" % (gi % 2), 16)
            blk.gpsimd(_slabs)

            def _te(t, h=h):
                if use_cast:
                    W(t, "ve", info["cast"][h])
                elif rhs_z:
                    W(t, "af", af0)
                elif te_gate is not None and h == 0:
                    te_gate(t)
                n = 0
                for i in range(4):
                    gi0 = h * 4 + i
                    W(t, "g%d" % (gi0 % 2), gb[gi0 % 2] + 16 * (gi0 // 2 + 1))
                    gi = h * 4 + i
                    for j in range(4):
                        for tb in range(NB_H):
                            gn = h * nth + n
                            W(t, "af", af0 + gn - 1 if gn >= 2 else af0)
                            last = None
                            for c in range(NC):
                                if rhs_z:
                                    rhs = QV[:, 16 + c, h * TH + tb * NT:
                                             h * TH + (tb + 1) * NT]
                                elif use_cast:
                                    rhs = xbh[:, c, tb * NT:(tb + 1) * NT]
                                else:
                                    rhs = stream[:, c,
                                                 h * TH + tb * NT:
                                                 h * TH + (tb + 1) * NT]
                                last = t.matmul(
                                    psA[:, gn % 2, 0:NT],
                                    ws[:, gi % 2, c, j * 128:(j + 1) * 128],
                                    rhs, start=(c == 0), stop=(c == NC - 1))
                            inc(last, "pe")
                            if te_hook is not None:
                                te_hook(t)
                            n += 1
            blk.tensor(_te)
            evac_half(h, info)
        return info

    def tile_oc_tb(n):
        return n // NB_H, n % NB_H

    def evac_rope_mk(dest_q, layer, sc_hook=None):
        st = {"ac0": cnt["ac"], "rp0": cnt["rp"],
              "ckb": [cnt["ck0"], cnt["ck1"]],
              "snap": {k: cnt[k] for k in
                       ("ck0", "ck1", "cv0", "cv1", "ct0", "ct1")}}

        def emit(h, info):
            ac0 = st["ac0"]; rp0 = st["rp0"]; ckb = st["ckb"]
            nth = info["nth"]; pe0 = info["pe0"]; af0 = info["af0"]

            def _sc(sc):
                def store(gn):
                    oc, tb = tile_oc_tb(gn % nth)
                    t0 = (gn // nth) * TH + tb * NT
                    W(sc, "af", af0 + gn + 1)
                    inc(sc.dma_start(
                        out=k_in[layer][oc * 128:(oc + 1) * 128, t0:t0 + NT],
                        in_=rk[:, gn % 2, 0:NT]), "ck%d" % (gn % 2), 16)
                for n in range(nth):
                    gn = h * nth + n
                    W(sc, "pe", pe0 + gn + 1)
                    if gn < 2:
                        for k2, v2 in st["snap"].items():
                            W(sc, k2, v2)
                    if gn >= 2:
                        W(sc, "rp", rp0 + gn - 1)
                    if not dest_q and gn >= 2:
                        store(gn - 2)
                    inc(sc.activation(sc512[:, gn % 2, 0:NT],
                                      psA[:, gn % 2, 0:NT], AF.Copy), "ac")
                    if sc_hook is not None:
                        sc_hook(sc)
                if not dest_q and h == 1:
                    store(2 * nth - 2)
                    store(2 * nth - 1)
            blk.scalar(_sc)

            def _ve(v):
                for n in range(nth):
                    gn = h * nth + n
                    oc, tb = tile_oc_tb(n)
                    t0 = h * TH + tb * NT
                    W(v, "ac", ac0 + gn + 1)
                    if gn >= 1:
                        W(v, "af", af0 + gn)
                    if not dest_q and gn >= 2:
                        W(v, "ck%d" % (gn % 2),
                          ckb[gn % 2] + 16 * ((gn - 2) // 2 + 1))
                    inc(v.tensor_copy(rsw[0:64, 0:NT],
                                      sc512[64:128, gn % 2, 0:NT]), "ve")
                    t2 = inc(v.tensor_copy(rsw[64:128, 0:NT],
                                           sc512[0:64, gn % 2, 0:NT]), "ve")
                    v.wait_ge(sems["ve"], t2)
                    t3 = inc(v.tensor_tensor(rsw[:, 0:NT], rsw[:, 0:NT],
                                             sinf[:, t0:t0 + NT], A.mult), "ve")
                    t4 = inc(v.tensor_tensor(rk[:, gn % 2, 0:NT],
                                             sc512[:, gn % 2, 0:NT],
                                             cosf[:, t0:t0 + NT], A.mult), "rp")
                    v.wait_ge(sems["ve"], t3)
                    v.wait_ge(sems["rp"], t4)
                    dst = QV[:, oc, t0:t0 + NT] if dest_q else rk[:, gn % 2, 0:NT]
                    inc(v.tensor_tensor(dst, rk[:, gn % 2, 0:NT],
                                        rsw[:, 0:NT], A.add), "af")
            blk.vector(_ve)
        return emit

    def evac_copy_mk(layer, mode, qscale=None, sc_hook=None):
        sem_pfx = "cv" if mode == "v" else "ck"
        st = {"cb2": [cnt[sem_pfx + "0"], cnt[sem_pfx + "1"]],
              "snap": {k: cnt[k] for k in
                       ("ck0", "ck1", "cv0", "cv1", "ct0", "ct1")}}
        vv = v_in[layer].rearrange("(h t) e -> t h e", t=T)

        def emit(h, info):
            cb2 = st["cb2"]
            af0 = info["af0"]; nth = info["nth"]; pe0 = info["pe0"]
            ntw = info["NT"]

            def _sc(sc):
                def store(gn):
                    n2 = gn % nth
                    h2 = gn // nth
                    W(sc, "af", af0 + gn + 1)
                    if mode == "v":
                        i = n2 // (TH // 128)
                        tb = n2 % (TH // 128)
                        tt0 = h2 * TH + tb * 128
                        srcap = sc512.ap().rearrange(
                            "p b (hh e) -> p b hh e", e=128)[:, gn % 2, :, :]
                        inc(sc.dma_start(out=vv[tt0:tt0 + 128,
                                                4 * i:4 * i + 4, :],
                                         in_=srcap), "cv%d" % (gn % 2), 16)
                    else:
                        oc, tb = tile_oc_tb(n2)
                        tt0 = h2 * TH + tb * NT
                        inc(sc.dma_start(
                            out=k_in[layer][oc * 128:(oc + 1) * 128,
                                            tt0:tt0 + NT],
                            in_=sc512[:, gn % 2, 0:NT]), "ck%d" % (gn % 2), 16)

                for n in range(nth):
                    gn = h * nth + n
                    W(sc, "pe", pe0 + gn + 1)
                    if gn < 2 and mode != "q":
                        for k2, v2 in st["snap"].items():
                            W(sc, k2, v2)
                    if mode == "q":
                        oc, tb = tile_oc_tb(n)
                        t0 = h * TH + tb * NT
                        if qscale is None:
                            inc(sc.activation(QV[:, oc, t0:t0 + NT],
                                              psA[:, gn % 2, 0:NT],
                                              AF.Copy), "af")
                        else:
                            inc(sc.activation(QV[:, oc, t0:t0 + NT],
                                              psA[:, gn % 2, 0:NT],
                                              AF.Copy, scale=qscale), "af")
                    else:
                        if gn >= 2:
                            store(gn - 2)
                            W(sc, sem_pfx + str(gn % 2),
                              cb2[gn % 2] + 16 * ((gn - 2) // 2 + 1))
                        inc(sc.activation(sc512[:, gn % 2, 0:ntw],
                                          psA[:, gn % 2, 0:ntw], AF.Copy), "af")
                    if sc_hook is not None:
                        sc_hook(sc)
                if mode != "q" and h == 1:
                    store(2 * nth - 2)
                    store(2 * nth - 1)
            blk.scalar(_sc)
        return emit

    def proj_tokmaj(wsrc, evac_half, te_hook=None, te_gate=None):
        pe0 = cnt["pe"]; af0 = cnt["af"]
        gb = [cnt["g0"], cnt["g1"]]
        ntb_h = TH // 128
        nth = 4 * ntb_h
        info = {"pe0": pe0, "af0": af0, "nth": nth, "cast": [0, 0], "NT": 512}
        for h in range(2):
            def _slabs(g, h=h):
                for i in range(4):
                    gi = h * 4 + i
                    W(g, "pe", pe0 + (gi - 1) * ntb_h if gi >= 2 else pe0)
                    inc(g.dma_start(
                        out=ws[:, gi % 2, :, :],
                        in_=wsrc[:, i * 512:(i + 1) * 512]
                        .rearrange("(c p) o -> p c o", p=128)),
                        "g%d" % (gi % 2), 16)
            blk.gpsimd(_slabs)

            def _te(t, h=h):
                if te_gate is not None and h == 0:
                    te_gate(t)
                n = 0
                for i in range(4):
                    gi0 = h * 4 + i
                    W(t, "g%d" % (gi0 % 2), gb[gi0 % 2] + 16 * (gi0 // 2 + 1))
                    gi = h * 4 + i
                    for tb in range(ntb_h):
                        gn = h * nth + n
                        W(t, "af", af0 + gn - 1 if gn >= 2 else af0)
                        last = None
                        for c in range(NC):
                            last = t.matmul(
                                psA[:, gn % 2, 0:512],
                                stream[:, c, h * TH + tb * 128:
                                       h * TH + (tb + 1) * 128],
                                ws[:, gi % 2, c, :],
                                start=(c == 0), stop=(c == NC - 1))
                        inc(last, "pe")
                        if te_hook is not None:
                            te_hook(t)
                        n += 1
            blk.tensor(_te)
            evac_half(h, info)
        return info

    def tg_x(t):
        W(t, "gi", tok["gi_all"])

    NTILE_K = 2 * 16 * NB_H
    NTILE_V = 2 * 4 * (TH // 128)
    sc_hook, te_hook, te_flush = mk_scale_hook(NTILE_K + NTILE_V)
    proj(wk0, evac_rope_mk(dest_q=False, layer=0, sc_hook=sc_hook),
         te_gate=tg_x, te_hook=te_hook)
    te_flush()
    kst0 = (cnt["ck0"], cnt["ck1"])
    proj_tokmaj(wv0, evac_copy_mk(0, "v", sc_hook=sc_hook), te_hook=te_hook)
    vst0 = (cnt["cv0"], cnt["cv1"])
    ph1_finish()
    proj(wq0, evac_rope_mk(dest_q=True, layer=0))
    tok["q0_af"] = cnt["af"]

    if upto == -1:
        qaf = cnt["af"]

        def _dbgst(g):
            W(g, "af", qaf)
            inc(g.dma_start(out=y_fm.rearrange("(c p) t -> p c t", p=128),
                            in_=QV[:, 0:16, :]), "gi", 16)
            W(g, "gi", cnt["gi"])
        blk.gpsimd(_dbgst)
        for t_ in reversed(ctx):
            t_.__exit__(None, None, None)
        return nc

    def ag0(g):
        W(g, "ck0", kst0[0]); W(g, "ck1", kst0[1])
        tok["agk0"] = inc(g.collective_compute(
            "AllGather", A.bypass, replica_groups=PAIRS,
            ins=[k_in[0].opt()], outs=[k_out[0].opt()]), "cc")
        W(g, "cv0", vst0[0]); W(g, "cv1", vst0[1])
        tok["agv0"] = inc(g.collective_compute(
            "AllGather", A.bypass, replica_groups=PAIRS,
            ins=[v_in[0].opt()], outs=[v_out[0].opt()]), "cc")
    blk.gpsimd(ag0)

    DN = D // NCORES
    GB = 128 * 16 * 512
    DB = 128 * 16 * DN

    def d_src(i):
        return td_sl[i * 128:(i + 1) * 128, :]

    def d_dst(i):
        base = 2 * GT + (i // 16) * DB
        return tern_in[base:base + DB].rearrange(
            "(p c o) -> p c o", c=16, o=DN)[:, i % 16, :]

    def g_src(j):
        mat, r, ch = j // 32, (j % 32) // GCH, j % GCH
        wsrc = tg_sl if mat == 0 else tu_sl
        return wsrc[r * 128:(r + 1) * 128, ch * 512:(ch + 1) * 512]

    def g_dst(j):
        mat, r, ch = j // 32, (j % 32) // GCH, j % GCH
        base = mat * GT + ch * GB
        return tern_in[base:base + GB].rearrange(
            "(p c o) -> p c o", c=16, o=512)[:, r, :]

    NP = FF // 128
    NV = 64
    tq_st = {}

    def tq_gp(g):
        tb = [cnt["t0"], cnt["t1"]]
        ctb = [cnt["ct0"], cnt["ct1"]]
        tfq0 = cnt["tf"]
        W(g, "ab", sh_state["ab0"] + NSLAB)
        for i in (0, 1):
            inc(g.dma_start(out=tstage[:, i % 2, 0:DN], in_=d_src(i)),
                "t%d" % (i % 2), 16)
        W(g, "cc", tok["ar2"])
        inc(g.dma_start(out=scl[:, 0:4], in_=f_sc_out), "ts", 16)
        g.wait_ge(sems["ts"], cnt["ts"])
        t1 = inc(g.tensor_scalar(scl[:, 4:8], scl[:, 0:4], 1.0 / NW, EPS_Q,
                                 A.mult, A.max), "gc")
        g.wait_ge(sems["gc"], t1)
        t2 = inc(g.tensor_scalar(scl[:, 0:4], scl[:, 4:8], 0.5, None,
                                 A.mult), "gc")
        g.wait_ge(sems["gc"], t2)
        tq_st["scl"] = t2
        for i in range(NP):
            W(g, "t%d" % (i % 2), tb[i % 2] + 16 * (i // 2 + 1))
            if i >= 2:
                W(g, "ct%d" % (i % 2), ctb[i % 2] + 16 * ((i - 2) // 2 + 1))
            else:
                W(g, "tf", tfq0)
            t1 = inc(g.tensor_scalar(ttmp[:, 0:DN], tstage[:, i % 2, 0:DN],
                                     -1.0, scl[:, 2:3], A.mult, A.is_gt),
                     "gc")
            t2 = inc(g.tensor_scalar(ttrn[:, i % 2, 0:DN],
                                     tstage[:, i % 2, 0:DN],
                                     scl[:, 2:3], None, A.is_gt), "gc")
            g.wait_ge(sems["gc"], t2)
            t3 = inc(g.tensor_tensor(ttrn[:, i % 2, 0:DN],
                                     ttrn[:, i % 2, 0:DN],
                                     ttmp[:, 0:DN], A.subtract), "tf")
            g.wait_ge(sems["tf"], cnt["tf"])
            inc(g.dma_start(out=d_dst(i), in_=ttrn[:, i % 2, 0:DN]),
                "ct%d" % (i % 2), 16)
            if i + 2 < NP:
                inc(g.dma_start(out=tstage[:, i % 2, 0:DN],
                                in_=d_src(i + 2)), "t%d" % (i % 2), 16)
        tq_st["tf_pool"] = cnt["tf"]
        tq_st["ct"] = (cnt["ct0"], cnt["ct1"])
    blk.gpsimd(tq_gp)

    def mk_vq_hook():
        ub = [cnt["u0"], cnt["u1"]]
        cub = [cnt["cu0"], cnt["cu1"]]
        tfv0 = cnt["tf"]
        st = {"j": 0}

        def _ld(s_):
            for j in range(NV):
                if j >= 2:
                    W(s_, "tf", tfv0 + j - 1)
                else:
                    W(s_, "tf", tq_st["tf_pool"])
                inc(s_.dma_start(out=tstage[:, j % 2, 0:512],
                                 in_=g_src(j)), "u%d" % (j % 2), 16)
        blk.sync(_ld)

        def vq(v):
            j0 = st["j"]
            if j0 >= NV:
                return
            if j0 == 0:
                W(v, "gc", tq_st["scl"])
                W(v, "tf", tq_st["tf_pool"])
                W(v, "ct0", tq_st["ct"][0])
                W(v, "ct1", tq_st["ct"][1])
            for j in range(j0, min(j0 + 2, NV)):
                mat = j // 32
                W(v, "u%d" % (j % 2), ub[j % 2] + 16 * (j // 2 + 1))
                if j >= 2:
                    W(v, "cu%d" % (j % 2),
                      cub[j % 2] + 16 * ((j - 2) // 2 + 1))
                if j >= 1:
                    W(v, "tf", tfv0 + j)
                t1 = inc(v.tensor_scalar(ttmp[:, 0:512],
                                         tstage[:, j % 2, 0:512],
                                         -1.0, scl[:, mat:mat + 1],
                                         A.mult, A.is_gt), "ve")
                v.wait_ge(sems["ve"], t1)
                inc(v.scalar_tensor_tensor(ttrn[:, j % 2, 0:512],
                                           tstage[:, j % 2, 0:512],
                                           scl[:, mat:mat + 1],
                                           ttmp[:, 0:512],
                                           A.is_gt, A.subtract), "tf")
            st["j"] = j0 + 2

        def stores():
            def _st(sc):
                for j in range(NV):
                    W(sc, "tf", tfv0 + j + 1)
                    inc(sc.dma_start(out=g_dst(j),
                                     in_=ttrn[:, j % 2, 0:512]),
                        "cu%d" % (j % 2), 16)
            blk.scalar(_st)
        return vq, stores

    def tq_ag():
        def _gp(g):
            W(g, "ct0", tq_st["ct"][0])
            W(g, "ct1", tq_st["ct"][1])
            W(g, "cu0", cnt["cu0"])
            W(g, "cu1", cnt["cu1"])
            tok["agt"] = inc(g.collective_compute(
                "AllGather", A.bypass, replica_groups=ALL8,
                ins=[tern_in.opt()], outs=[tern_out.opt()]), "cc")
        blk.gpsimd(_gp)

    def attn(layer):
        tiles = []
        hg_sizes = []
        for h in range(NH):
            for g in range(NQ):
                lst = []
                for j in range(LQB * g):
                    for r in range(2):
                        lst.append((r, j, None))
                for m in range(NBM):
                    lst.append((m % 2, LQB * g + m // 2, m))
                hg_sizes.append(len(lst))
                for it, (r, j, bm) in enumerate(lst):
                    tiles.append(dict(h=h, g=g, r=r, j=j, bm=bm,
                                      first=(it == 0),
                                      last=(it == len(lst) - 1),
                                      hg=h * NQ + g))
        for i, td in enumerate(tiles):
            td["ti"] = i
        NHG = NH * NQ
        thru = np.cumsum(hg_sizes).tolist()
        band_pre = []
        nb = 0
        for td in tiles:
            if td["bm"] is not None:
                nb += 1
            band_pre.append(nb)
        tiles_by_hg = [[] for _ in range(NHG)]
        for td in tiles:
            tiles_by_hg[td["hg"]].append(td)

        pe0 = cnt["pe"]; sf0 = cnt["sf"]; pf0 = cnt["pf"]; zf0 = cnt["zf"]
        ac0 = cnt["ac"]; bm0 = cnt["bm"]; df0 = cnt["df"]; af0 = cnt["af"]
        rp0 = cnt["rp"]; q10 = cnt["q1"]; q20 = cnt["q2"]; q30 = cnt["q3"]
        q50 = cnt["q5"]; cb0 = cnt["cb"]
        kvb = [cnt["k0"], cnt["k1"]]
        q_af = tok["q%d_af" % layer]
        stab_n = 3 * KBq
        RIN, ROUT = 1024, 1536

        def _sy(s_):
            W(s_, "cc", tok["agk%d" % layer])
            W(s_, "cc", tok["agv%d" % layer])
            for h in range(NH):
                if h >= 2:
                    W(s_, "zf", zf0 + thru[(h - 1) * NQ - 1])
                for r in range(2):
                    inc(s_.dma_start(
                        out=kd[:, h % 2, r, :],
                        in_=k_out[layer][r * D + h * 128:
                                         r * D + (h + 1) * 128, :]),
                        "k%d" % (h % 2), 16)
                    inc(s_.dma_start(
                        out=vd[:, h % 2, r * NTB:(r + 1) * NTB, :],
                        in_=v_out[layer][(r * NH + h) * T:
                                         (r * NH + h + 1) * T, :]
                        .rearrange("(b p) e -> p b e", p=128)),
                        "k%d" % (h % 2), 16)
        blk.sync(_sy)

        def kv_wait(eng, h):
            W(eng, "k%d" % (h % 2), kvb[h % 2] + 64 * (h // 2 + 1))

        def q2_of_rb(hg):
            return q20 + hg + 1

        def _sc(sc):
            def sq2_of(td):
                ti = td["ti"]
                if td["bm"] is not None:
                    W(sc, "bm", bm0 + band_pre[ti])
                else:
                    W(sc, "sf", sf0 + ti + 1)
                if ti >= 2:
                    W(sc, "pf", pf0 + ti - 1)
                    W(sc, "zf", zf0 + ti - 1)
                inc(sc.activation(pt[:, ti % 2, :],
                                  qtmp[:, ti % 2, 0:QS], AF.Square), "ac")
                if td["last"] and td["hg"] >= 1:
                    hg = td["hg"]
                    W(sc, "q2", q2_of_rb(hg - 1))
                    if hg >= 3:
                        W(sc, "af", af0 + hg - 2)
                    inc(sc.activation(
                        rbc[:, ((hg - 1) % 2) * QS:((hg - 1) % 2 + 1) * QS],
                        psE[:, 0:QS], AF.Copy), "q3")

            spend = None
            for h in range(NH):
                if spend is not None:
                    sq2_of(spend)
                    spend = None
                if layer == 1:
                    for r in range(2):
                        for b in range(KBq):
                            i = r * KBq + b
                            kv_wait(sc, h)
                            if h > 0 or i >= 2:
                                W(sc, "cb", cb0 + stab_n * h + i - 1)
                            inc(sc.activation(
                                sc512[:, i % 2, 0:NKB],
                                kd[:, h % 2, r, b * NKB:(b + 1) * NKB],
                                AF.Square), "q1")
                    for b in range(KBq):
                        i = 2 * KBq + b
                        W(sc, "af", q_af)
                        if h > 0 or i >= 2:
                            W(sc, "cb", cb0 + stab_n * h + i - 1)
                        inc(sc.activation(sc512[:, i % 2, 0:NKB],
                                          QV[:, h, b * NKB:(b + 1) * NKB],
                                          AF.Square), "q1")
                    W(sc, "q5", q50 + (stab_n + 1) * (h + 1))
                    inc(sc.activation(browz.ap()[0:1, 0:T],
                                      browz.ap()[0:1, 0:T], AF.Sqrt), "rp")
                for g in range(NQ):
                    hg = h * NQ + g
                    for td in tiles_by_hg[hg]:
                        ti = td["ti"]
                        if layer == 1:
                            W(sc, "pe", pe0 + ti + 1)
                            if ti >= 2:
                                W(sc, "pf", pf0 + ti - 1)
                                W(sc, "zf", zf0 + ti - 1)
                            inc(sc.activation(pt[:, ti % 2, :],
                                              psB[:, ti % 2, 0:QS], AF.Exp),
                                "sf")
                        else:
                            W(sc, "pe", pe0 + ti + 1)
                            if ti >= 2:
                                W(sc, "ac", ac0 + ti - 1)
                            inc(sc.activation(qtmp[:, ti % 2, 0:QS],
                                              psB[:, ti % 2, 0:QS],
                                              AF.Square), "sf")
                            if spend is not None:
                                sq2_of(spend)
                            spend = td
                    if layer == 1 and hg >= 1:
                        W(sc, "q2", q2_of_rb(hg - 1))
                        if hg >= 3:
                            W(sc, "af", af0 + hg - 2)
                        inc(sc.activation(
                            rbc[:, ((hg - 1) % 2) * QS:((hg - 1) % 2 + 1) * QS],
                            psE[:, 0:QS], AF.Copy), "q3")
            if spend is not None:
                sq2_of(spend)
            W(sc, "q2", q2_of_rb(NHG - 1))
            W(sc, "af", af0 + NHG - 2)
            inc(sc.activation(rbc[:, ((NHG - 1) % 2) * QS:
                                   ((NHG - 1) % 2 + 1) * QS],
                              psE[:, 0:QS], AF.Copy), "q3")
        blk.scalar(_sc)

        def _epilogue(v, phg):
            W(v, "pf", pf0 + thru[phg])
            if phg >= 1:
                W(v, "q2", q2_of_rb(phg - 1))
            if layer == 0:
                t6 = inc(v.tensor_scalar(rows.ap()[0:1, RIN:RIN + QS],
                                         psA[0:1, phg % 2, 0:QS],
                                         EPS_L0, None, A.max), "ve")
            else:
                t6 = inc(v.tensor_copy(rows.ap()[0:1, RIN:RIN + QS],
                                       psA[0:1, phg % 2, 0:QS]), "ve")
            v.wait_ge(sems["ve"], t6)
            t7 = inc(v.reciprocal(rows.ap()[0:1, ROUT:ROUT + QS],
                                  rows.ap()[0:1, RIN:RIN + QS]), "ve")
            v.wait_ge(sems["ve"], t7)
            inc(v.tensor_copy(browz.ap()[0:1, 1024:1024 + QS],
                              rows.ap()[0:1, ROUT:ROUT + QS]), "df")

        def _zevac(v, zhg):
            zh, zg = zhg // NQ, zhg % NQ
            W(v, "zf", zf0 + thru[zhg])
            W(v, "q3", q30 + zhg + 1)
            inc(v.tensor_tensor(QV[:, 16 + zh, zg * QS:(zg + 1) * QS],
                                psC[:, zhg % 2, 0:QS],
                                rbc[:, (zhg % 2) * QS:(zhg % 2 + 1) * QS],
                                A.mult), "af")

        def _ve(v):
            W(v, "gi", tok["gi_all"])
            for h in range(NH):
                if layer == 1:
                    for i in range(stab_n):
                        W(v, "cb", cb0 + stab_n * h + i + 1)
                        if i < 2 * KBq:
                            inc(v.tensor_reduce(
                                rtmp[0:1, 128 + i:129 + i],
                                psE[0:1, 0:NKB], AX.X, A.max), "q5")
                        else:
                            dst = rows.ap()[0:1, (i - 2 * KBq) * NKB:
                                            (i - 2 * KBq + 1) * NKB]
                            inc(v.tensor_copy(dst, psE[0:1, 0:NKB]), "q5")
                    W(v, "q5", q50 + (stab_n + 1) * h + stab_n)
                    vchain(v, v.tensor_reduce(rtmp[0:1, 132:133],
                                              rtmp[0:1, 128:128 + 2 * KBq],
                                              AX.X, A.max))
                    if h >= 1:
                        W(v, "pe", pe0 + thru[h * NQ - 1])
                    with nc.allow_low_precision(reason="bf16 c_q row"):
                        t5 = inc(v.tensor_scalar(browz.ap()[0:1, 0:T],
                                                 rows.ap()[0:1, 0:T],
                                                 rtmp[0:1, 132:133],
                                                 None, A.mult), "q5")
                    v.wait_ge(sems["q5"], t5)
                for g in range(NQ):
                    hg = h * NQ + g
                    if hg >= 1:
                        _epilogue(v, hg - 1)
                    if hg >= 2:
                        _zevac(v, hg - 2)
                    for td in tiles_by_hg[hg]:
                        ti = td["ti"]
                        if td["bm"] is not None:
                            W(v, "sf", sf0 + ti + 1)
                            dst = (pt[:, ti % 2, :] if layer == 1
                                   else qtmp[:, ti % 2, 0:QS])
                            inc(v.tensor_tensor(dst, dst,
                                                masks[:, td["bm"], :],
                                                A.mult), "bm")
            _epilogue(v, NHG - 1)
            for zhg in (NHG - 2, NHG - 1):
                _zevac(v, zhg)
        blk.vector(_ve)

        def _te(t):
            def s_mm(td):
                ti = td["ti"]
                W(t, "sf", sf0 + ti - 1 if ti >= 2 else sf0)
                mm = t.matmul(psB[:, ti % 2, 0:QS],
                              kd[:, td["h"] % 2, td["r"],
                                 td["j"] * 128:(td["j"] + 1) * 128],
                              QV[:, td["h"], td["g"] * QS:
                                 (td["g"] + 1) * QS],
                              start=True, stop=(layer == 0))
                if layer == 1:
                    mm = t.matmul(psB[:, ti % 2, 0:QS], neg_b[:],
                                  browz.ap()[0:1, td["g"] * QS:
                                             (td["g"] + 1) * QS],
                                  start=False, stop=True)
                inc(mm, "pe")

            def p_ready(td):
                ti = td["ti"]
                if layer == 1 and td["bm"] is not None:
                    W(t, "bm", bm0 + band_pre[ti])
                elif layer == 1:
                    W(t, "sf", sf0 + ti + 1)
                else:
                    W(t, "ac", ac0 + ti + 1)

            def avd(td):
                ti = td["ti"]; hg = td["hg"]
                p_ready(td)
                if td["first"] and hg >= 2:
                    W(t, "df", df0 + hg - 1)
                inc(t.matmul(psA[0:1, hg % 2, 0:QS], ones_k[:],
                             pt[:, ti % 2, :],
                             start=td["first"], stop=td["last"]), "pf")
                p_ready(td)
                if td["first"] and hg >= 2:
                    W(t, "af", af0 + hg - 1)
                inc(t.matmul(psC[:, hg % 2, 0:QS],
                             vd[:, td["h"] % 2,
                                td["r"] * NTB + td["j"], :],
                             pt[:, ti % 2, :],
                             start=td["first"], stop=td["last"]), "zf")
                if td["last"]:
                    if hg >= 1:
                        W(t, "df", df0 + hg)
                        if hg >= 2:
                            W(t, "q3", q30 + hg - 1)
                        inc(t.matmul(psE[:, 0:QS], ones_b[:],
                                     browz.ap()[0:1, 1024:1024 + QS],
                                     start=True, stop=True), "q2")

            W(t, "af", q_af)
            pend = []
            for h in range(NH):
                for pd in pend:
                    avd(pd)
                pend = []
                kv_wait(t, h)
                if layer == 1:
                    for i in range(stab_n):
                        W(t, "q1", q10 + stab_n * h + i + 1)
                        if h > 0 or i > 0:
                            W(t, "q5", q50 + (stab_n + 1) * h + i)
                        W(t, "q3", q30 + h * NQ - 1 if h > 0 else 0)
                        inc(t.matmul(psE[0:1, 0:NKB], ones_k[:],
                                     sc512[:, i % 2, 0:NKB],
                                     start=True, stop=True), "cb")
                    W(t, "rp", rp0 + h + 1)
                for g in range(NQ):
                    hg = h * NQ + g
                    for td in tiles_by_hg[hg]:
                        s_mm(td)
                        if len(pend) >= 2:
                            avd(pend.pop(0))
                        pend.append(td)
            for pd in pend:
                avd(pd)
            W(t, "df", df0 + NHG)
            W(t, "q3", q30 + NHG - 1)
            inc(t.matmul(psE[:, 0:QS], ones_b[:],
                         browz.ap()[0:1, 1024:1024 + QS],
                         start=True, stop=True), "q2")
        blk.tensor(_te)
        tok["attn%d_af" % layer] = cnt["af"]
        tok["attn%d_zf" % layer] = cnt["zf"]

    attn(0)

    if upto == -3:
        zaf = cnt["af"]

        def _dbgst3(g):
            W(g, "af", zaf)
            inc(g.dma_start(out=y_fm.rearrange("(c p) t -> p c t", p=128),
                            in_=QV[:, 16:32, :]), "gi", 16)
            W(g, "gi", cnt["gi"])
        blk.gpsimd(_dbgst3)
        for t_ in reversed(ctx):
            t_.__exit__(None, None, None)
        return nc
    if upto == -4:
        zaf = cnt["af"]

        def _dbgst4(g):
            W(g, "af", zaf)
            inc(g.dma_start(out=y_fm.rearrange("(c p) t -> p c t", p=128),
                            in_=QV[:, 0:16, :]), "gi", 16)
            W(g, "gi", cnt["gi"])
        blk.gpsimd(_dbgst4)
        for t_ in reversed(ctx):
            t_.__exit__(None, None, None)
        return nc

    def mk_castwait(sem_name):
        snap = cnt[sem_name]
        def f(v, h):
            W(v, sem_name, snap)
        return f

    def _finish():
        def fin(sc):
            W(sc, "cy", tok["yst1"])
        blk.scalar(fin)

    def _close(nc_):
        for t in reversed(ctx):
            t.__exit__(None, None, None)
        return nc_

    def evac_res_mk(vq=None):
        def emit(h, info):
            nth = info["nth"]; pe0 = info["pe0"]

            def _ve(v):
                for n in range(nth):
                    gn = h * nth + n
                    oc, tb = tile_oc_tb(n)
                    t0 = h * TH + tb * NT
                    W(v, "pe", pe0 + gn + 1)
                    inc(v.scalar_tensor_tensor(
                        stream[:, oc, t0:t0 + NT], psA[:, gn % 2, 0:NT], 1.0,
                        stream[:, oc, t0:t0 + NT], A.mult, A.add), "af")
                    if vq is not None:
                        vq(v)
            blk.vector(_ve)
        return emit

    vq_hook, vq_stores = mk_vq_hook()
    proj(wo0, evac_res_mk(vq=vq_hook), rhs_z=True)
    vq_stores()
    if upto == -6:
        paf = cnt["af"]

        def _dbgst6(g):
            W(g, "af", paf)
            inc(g.dma_start(out=y_fm.rearrange("(c p) t -> p c t", p=128),
                            in_=QV[:, 16:32, :]), "gi", 16)
            W(g, "gi", cnt["gi"])
        blk.gpsimd(_dbgst6)
        for t_ in reversed(ctx):
            t_.__exit__(None, None, None)
        return nc
    if upto == -5:
        paf = cnt["af"]

        def _dbgst5(g):
            W(g, "af", paf)
            inc(g.dma_start(out=y_fm.rearrange("(c p) t -> p c t", p=128),
                            in_=stream[:]), "gi", 16)
            W(g, "gi", cnt["gi"])
        blk.gpsimd(_dbgst5)
        for t_ in reversed(ctx):
            t_.__exit__(None, None, None)
        return nc

    def layernorm(store_half=False, limit=99, halves=2):
        for h in range(halves):
            af_in = cnt["af"]
            ve0 = cnt["ve"]; ac0 = cnt["ac"]; pe0 = cnt["pe"]

            def _cast(v, h=h):
                W(v, "af", af_in)
                for c in range(NC):
                    inc(v.tensor_copy(xbh[:, c, :],
                                      stream[:, c, h * TH:(h + 1) * TH]), "ve")
            blk.vector(_cast)
            cast_tok = cnt["ve"]

            def _sq(sc, h=h):
                W(sc, "ve", cast_tok)
                for c in range(NC):
                    if c >= 2:
                        W(sc, "pe", pe0 + 2 * (c - 1))
                    inc(sc.activation(sc512[:, c % 2, 0:TH], xbh[:, c, :],
                                      AF.Square), "ac")
            if limit < 1:
                continue
            blk.scalar(_sq)

            def _mm(t, h=h):
                W(t, "ve", cast_tok)
                for c in range(NC):
                    inc(t.matmul(psD[0:1, 0:TH], ones_k[:], xbh[:, c, :],
                                 start=(c == 0), stop=(c == NC - 1)), "pe")
                    W(t, "ac", ac0 + c + 1)
                    inc(t.matmul(psE[0:1, 0:TH], ones_k[:],
                                 sc512[:, c % 2, 0:TH],
                                 start=(c == 0), stop=(c == NC - 1)), "pe")
            if limit < 2:
                continue
            blk.tensor(_mm)
            mm_tok = cnt["pe"]

            def _rows(v, h=h):
                W(v, "pe", mm_tok)
                vchain(v, v.tensor_scalar(rows.ap()[0:1, 0:TH], psD[0:1, 0:TH],
                                          1.0 / D, None, A.mult))
                vchain(v, v.tensor_scalar(rows.ap()[0:1, 512:512 + TH],
                                          psE[0:1, 0:TH], 1.0 / D, None,
                                          A.mult))
                vchain(v, v.tensor_tensor(rows.ap()[0:1, 1024:1024 + TH],
                                          rows.ap()[0:1, 0:TH],
                                          rows.ap()[0:1, 0:TH], A.mult))
                vchain(v, v.tensor_tensor(rows.ap()[0:1, 512:512 + TH],
                                          rows.ap()[0:1, 512:512 + TH],
                                          rows.ap()[0:1, 1024:1024 + TH],
                                          A.subtract))
                inc(v.tensor_scalar(rows.ap()[0:1, 512:512 + TH],
                                    rows.ap()[0:1, 512:512 + TH],
                                    EPS_LN, None, A.add), "df")
            if limit < 3:
                continue
            blk.vector(_rows)
            var_tok = cnt["df"]

            def _sqrt(sc, h=h):
                W(sc, "df", var_tok)
                inc(sc.activation(rows.ap()[0:1, 1024:1024 + TH],
                                  rows.ap()[0:1, 512:512 + TH], AF.Sqrt),
                    "q1")
            if limit < 4:
                continue
            blk.scalar(_sqrt)
            sd_tok = cnt["q1"]

            def _ab(v, h=h):
                W(v, "q1", sd_tok)
                vchain(v, v.reciprocal(rows.ap()[0:1, 1536:1536 + TH],
                                       rows.ap()[0:1, 1024:1024 + TH]))
                vchain(v, v.tensor_tensor(rows.ap()[0:1, 0:TH],
                                          rows.ap()[0:1, 0:TH],
                                          rows.ap()[0:1, 1536:1536 + TH],
                                          A.mult))
                inc(v.tensor_scalar(rows.ap()[0:1, 0:TH],
                                    rows.ap()[0:1, 0:TH],
                                    -1.0, None, A.mult), "q3")
            if limit < 5:
                continue
            blk.vector(_ab)
            ab_tok = cnt["q3"]

            def _bc(t, h=h):
                W(t, "q3", ab_tok)
                inc(t.matmul(psD[:, 0:TH], ones_b32[:],
                             rows.ap()[0:1, 1536:1536 + TH],
                             start=True, stop=True), "pe")
                inc(t.matmul(psE[:, 0:TH], ones_b32[:],
                             rows.ap()[0:1, 0:TH],
                             start=True, stop=True), "pe")
            if limit < 6:
                continue
            blk.tensor(_bc)
            bc_tok = cnt["pe"]

            def _norm(v, h=h):
                W(v, "pe", bc_tok)
                for c in range(NC):
                    t1 = inc(v.tensor_tensor(qtmp[:, c % 2, 0:TH],
                                             stream[:, c, h * TH:(h + 1) * TH],
                                             psD[:, 0:TH], A.mult), "ve")
                    if c >= 2:
                        pass
                    v.wait_ge(sems["ve"], t1)
                    inc(v.tensor_tensor(stream[:, c, h * TH:(h + 1) * TH],
                                        qtmp[:, c % 2, 0:TH],
                                        psE[:, 0:TH], A.add), "af")
            if limit < 7:
                continue
            blk.vector(_norm)
            norm_tok = cnt["af"]

            if store_half:
                def _yst(g2, h=h):
                    W(g2, "af", norm_tok)
                    inc(g2.dma_start(
                        out=y_fm[:, h * TH:(h + 1) * TH]
                        .rearrange("(c p) t -> p c t", p=128),
                        in_=stream[:, :, h * TH:(h + 1) * TH]), "cy", 16)
                    tok["yst%d" % h] = cnt["cy"]
                blk.gpsimd(_yst)

    layernorm(store_half=(upto == 1))
    if upto == 1:
        _finish()
        return _close(nc)

    def ffn0_gate(g):
        W(g, "cc", tok["ar1"])

    ffn0_q3 = {"q30": None}

    def evac_ffn0_half(h, info):
        nth = info["nth"]; pe0 = info["pe0"]
        if ffn0_q3["q30"] is None:
            ffn0_q3["q30"] = cnt["q3"]
        q30 = ffn0_q3["q30"]
        af0 = info["af0"]

        def _ve(v):
            for n in range(nth):
                gn = h * nth + n
                oc, tb = tile_oc_tb(n)
                t0 = h * TH + tb * NT
                W(v, "pe", pe0 + gn + 1)
                if gn >= 2:
                    W(v, "af", af0 + gn - 1)
                inc(v.scalar_tensor_tensor(
                    qtmp[:, gn % 2, 0:NT], psA[:, gn % 2, 0:NT],
                    INV_SQRT8, stream[:, oc, t0:t0 + NT],
                    A.mult, A.add), "q3")
        blk.vector(_ve)

        def _sc(sc):
            W(sc, "si", tok["si_all"])
            for n in range(nth):
                gn = h * nth + n
                oc, tb = tile_oc_tb(n)
                t0 = h * TH + tb * NT
                W(sc, "q3", q30 + gn + 1)
                inc(sc.activation(stream[:, oc, t0:t0 + NT],
                                  qtmp[:, gn % 2, 0:NT], AF.Identity,
                                  bias=bias_sb[:, oc:oc + 1]), "af")
        blk.scalar(_sc)

    proj(f_out, evac_ffn0_half, cast_wait=mk_castwait("af"),
         slab_gate=ffn0_gate, use_cast=True)
    tq_ag()
    layernorm(store_half=(upto == 2))
    if upto == 2:
        _finish()
        return _close(nc)

    lnaf = cnt["af"]

    def tg_ln(t):
        W(t, "af", lnaf)
    proj(wk1, evac_copy_mk(1, "k"), te_gate=tg_ln)
    kst1 = (cnt["ck0"], cnt["ck1"])
    proj_tokmaj(wv1, evac_copy_mk(1, "v"))
    vst1 = (cnt["cv0"], cnt["cv1"])
    proj(wq1, evac_copy_mk(1, "q", qscale=1.0 / math.sqrt(HD)))
    tok["q1_af"] = cnt["af"]

    def ag1(g):
        W(g, "ck0", kst1[0]); W(g, "ck1", kst1[1])
        tok["agk1"] = inc(g.collective_compute(
            "AllGather", A.bypass, replica_groups=PAIRS,
            ins=[k_in[1].opt()], outs=[k_out[1].opt()]), "cc")
        W(g, "cv0", vst1[0]); W(g, "cv1", vst1[1])
        tok["agv1"] = inc(g.collective_compute(
            "AllGather", A.bypass, replica_groups=PAIRS,
            ins=[v_in[1].opt()], outs=[v_out[1].opt()]), "cc")
    blk.gpsimd(ag1)

    attn(1)

    proj(wo1, evac_res_mk(), rhs_z=True)
    layernorm(store_half=(upto == 3))
    if upto == 3:
        _finish()
        return _close(nc)

    def ffn1_half(h, dbg=0):
        af_in = cnt["af"]

        ac0 = cnt["ac"]

        def _h128(t):
            W(t, "af", af_in)
            W(t, "gi", tok["gi_all"])
            for c in range(NC):
                if c >= 2:
                    W(t, "ac", ac0 + c - 1)
                inc(t.matmul(psB[:, c % 2, 0:TH], h128[:],
                             stream[:, c, h * TH:(h + 1) * TH],
                             start=True, stop=True), "pe")
        blk.tensor(_h128)
        h128_pe = cnt["pe"] - NC

        def _h128ev(sc):
            for c in range(NC):
                W(sc, "pe", h128_pe + c + 1)
                inc(sc.activation(HID[:, c, :], psB[:, c % 2, 0:TH],
                                  AF.Copy), "ac")
        blk.scalar(_h128ev)
        h128_ac = cnt["ac"]

        def _h16(v):
            W(v, "ac", h128_ac)
            W(v, "tf", cnt["tf"])
            for s in (1, 2, 4, 8):
                for i in range(NC):
                    if i & s:
                        continue
                    a, b = i, i + s
                    t1 = inc(v.tensor_copy(ttmp[:, 0:TH], HID[:, a, :]), "ve")
                    v.wait_ge(sems["ve"], t1)
                    t2 = inc(v.tensor_tensor(HID[:, a, :], HID[:, a, :],
                                             HID[:, b, :], A.add), "ve")
                    v.wait_ge(sems["ve"], t2)
                    t3 = inc(v.tensor_tensor(HID[:, b, :], ttmp[:, 0:TH],
                                             HID[:, b, :], A.subtract), "ve")
                    v.wait_ge(sems["ve"], t3)
            t1 = inc(v.tensor_tensor(qtmp[:, 0, 0:TH], HID[:, 0, :],
                                     HID[:, 1, :], A.max), "ve")
            t2 = inc(v.tensor_tensor(qtmp[:, 1, 0:TH], HID[:, 0, :],
                                     HID[:, 1, :], A.min), "ve")
            v.wait_ge(sems["ve"], t2)
            for c in range(2, NC):
                t1 = inc(v.tensor_tensor(qtmp[:, 0, 0:TH], qtmp[:, 0, 0:TH],
                                         HID[:, c, :], A.max), "ve")
                t2 = inc(v.tensor_tensor(qtmp[:, 1, 0:TH], qtmp[:, 1, 0:TH],
                                         HID[:, c, :], A.min), "ve")
                v.wait_ge(sems["ve"], t2)
            t1 = inc(v.scalar_tensor_tensor(qtmp[:, 0, 0:TH], qtmp[:, 1, 0:TH],
                                            -1.0, qtmp[:, 0, 0:TH],
                                            A.mult, A.max), "ve")
            v.wait_ge(sems["ve"], t1)
            tok["fold1"] = cnt["ve"]
        blk.vector(_h16)

        def _par(g):
            W(g, "ve", tok["fold1"])
            tok["par1"] = inc(g.reduce_max(rows.ap()[0:1, 1536:1536 + TH],
                                           qtmp[:, 0, 0:TH], AX.C), "gc")
        blk.gpsimd(_par)

        def _rows1(v):
            W(v, "gc", tok["par1"])
            vchain(v, v.tensor_scalar(rows.ap()[0:1, 0:TH],
                                      rows.ap()[0:1, 1536:1536 + TH],
                                      1.0 / math.sqrt(D), EPS_Q,
                                      A.mult, A.max))
            vchain(v, v.reciprocal(rows.ap()[0:1, 512:512 + TH],
                                   rows.ap()[0:1, 0:TH]))
            vchain(v, v.tensor_scalar(rows.ap()[0:1, 512:512 + TH],
                                      rows.ap()[0:1, 512:512 + TH],
                                      127.0 / math.sqrt(D), None, A.mult))
            inc(v.tensor_scalar(rows.ap()[0:1, 1024:1024 + TH],
                                rows.ap()[0:1, 0:TH],
                                1.0 / 127.0, None, A.mult), "q3")
        blk.vector(_rows1)
        r1_tok = cnt["q3"]

        def _bc1(t):
            W(t, "q3", r1_tok)
            inc(t.matmul(psD[:, 0:TH], ones_b32[:],
                         rows.ap()[0:1, 512:512 + TH],
                         start=True, stop=True), "pe")
            inc(t.matmul(psE[:, 0:TH], ones_b32[:],
                         rows.ap()[0:1, 1024:1024 + TH],
                         start=True, stop=True), "pe")
        blk.tensor(_bc1)
        bc1_tok = cnt["pe"]

        def _cp1(sc):
            W(sc, "pe", bc1_tok)
            t1 = inc(sc.activation(cbc[:, 0:TH], psD[:, 0:TH], AF.Copy), "ac")
            inc(sc.activation(qtmp[:, 1, 0:TH], psE[:, 0:TH], AF.Copy), "ac")
        blk.scalar(_cp1)
        cp1_tok = cnt["ac"]

        def _round1(v):
            W(v, "ac", cp1_tok)
            tfb = cnt["tf"]
            for c in range(NC):
                if c >= 1:
                    W(v, "tf", tfb + c)
                t1 = inc(v.tensor_tensor(qtmp[:, 0, 0:TH], HID[:, c, :],
                                         cbc[:, 0:TH], A.mult), "ve")
                v.wait_ge(sems["ve"], t1)
                t2 = inc(v.tensor_scalar(qint[:, 0:TH], qtmp[:, 0, 0:TH],
                                         256.5, None, A.add), "ve")
                v.wait_ge(sems["ve"], t2)
                inc(v.scalar_tensor_tensor(xq1[:, c, :], qint[:, 0:TH],
                                           -256.0, qtmp[:, 1, 0:TH],
                                           A.add, A.mult), "tf")
        blk.vector(_round1)
        xq1_tok = cnt["tf"]

        if dbg == -20:
            def _d20(g):
                W(g, "tf", xq1_tok)
                inc(g.dma_start(out=y_fm[:, 0:TH]
                                .rearrange("(c p) t -> p c t", p=128),
                                in_=xq1[:]), "gi", 16)
                W(g, "gi", cnt["gi"])
            blk.gpsimd(_d20)
            return

        ssb = [cnt["s0"], cnt["s1"]]
        pe2 = cnt["pe"]; zf2 = cnt["zf"]
        q12 = cnt["q1"]; q32 = cnt["q3"]
        ngrp = (FF // NCORES // 512) * NCORES // 1
        NGRP = NCORES * 2

        def gate_off(r):
            return r * TERN

        def _syg(s_):
            W(s_, "cc", tok["agt"])
            nsl = 0
            for r in range(NCORES):
                for sub in range(2):
                    for mat, moff in ((0, 0), (1, GT)):
                        if nsl >= 2:
                            pg, pm = (nsl - 2) // 2, (nsl - 2) % 2
                            W(s_, "pe" if pm == 0 else "zf",
                              (pe2 if pm == 0 else zf2) + 4 * (pg + 1))
                        else:
                            W(s_, "pe", pe2)
                        base = r * TERN + moff + sub * GB
                        src = tern_out[base:base + GB] \
                            .rearrange("(p c o) -> p c o", c=16, o=512)
                        inc(s_.dma_start(out=ws[:, nsl % 2, :, :], in_=src),
                            "s%d" % (nsl % 2), 16)
                        nsl += 1
        blk.sync(_syg)

        gbanks = [psA[:, 0, 0:TH], psA[:, 1, 0:TH],
                  psB[:, 0, 0:TH], psB[:, 1, 0:TH]]
        ubanks = [psC[:, 0, 0:TH], psC[:, 1, 0:TH],
                  psD[:, 0:TH], psE[:, 0:TH]]

        def _teg(t):
            W(t, "tf", xq1_tok)
            nsl = 0
            for grp in range(NGRP):
                for mat in range(2):
                    W(t, "s%d" % (nsl % 2), ssb[nsl % 2] + 16 * (nsl // 2 + 1))
                    if mat == 0:
                        if grp >= 1:
                            W(t, "q1", q12 + 4 * grp)
                    else:
                        if grp >= 1:
                            W(t, "q3", q32 + 4 * grp)
                    banks = gbanks if mat == 0 else ubanks
                    for c in range(NC):
                        for oc in range(4):
                            iv = t.matmul(banks[oc],
                                          ws[:, nsl % 2, c,
                                             oc * 128:(oc + 1) * 128],
                                          xq1[:, c, :],
                                          start=(c == 0), stop=(c == NC - 1))
                            if c == NC - 1:
                                inc(iv, "pe" if mat == 0 else "zf")
                    nsl += 1
        blk.tensor(_teg)

        def _silu(sc):
            for grp in range(NGRP):
                for oc in range(4):
                    W(sc, "pe", pe2 + 4 * grp + oc + 1)
                    if grp >= 1 or oc >= 2:
                        W(sc, "q3", q32 + 4 * grp + oc - 1)
                    inc(sc.activation(sc512[:, oc % 2, 0:TH],
                                      gbanks[oc], AF.Silu,
                                      scale=scl[:, 4:5]), "q1")
        blk.scalar(_silu)

        def _stt(v):
            for grp in range(NGRP):
                for oc in range(4):
                    fc = grp * 4 + oc
                    W(v, "zf", zf2 + 4 * grp + oc + 1)
                    W(v, "q1", q12 + 4 * grp + oc + 1)
                    inc(v.scalar_tensor_tensor(HID[:, fc, :], ubanks[oc],
                                               scl[:, 5:6],
                                               sc512[:, oc % 2, 0:TH],
                                               A.mult, A.mult), "q3")
        blk.vector(_stt)
        gu_done = cnt["q3"]

        if dbg == -21:
            def _d21(g):
                W(g, "q3", gu_done)
                inc(g.dma_start(out=y_fm[:, 0:TH]
                                .rearrange("(c p) t -> p c t", p=128),
                                in_=HID[:, 0:16, :]), "gi", 16)
                inc(g.dma_start(out=y_fm[:, TH:2 * TH]
                                .rearrange("(c p) t -> p c t", p=128),
                                in_=HID[:, 16:32, :]), "gi", 16)
                W(g, "gi", cnt["gi"])
            blk.gpsimd(_d21)
            return

        q1_all = cnt["q1"]
        hb_pe = cnt["pe"]
        hb_ac0 = cnt["ac"]

        def _h128b(t):
            W(t, "q3", gu_done)
            W(t, "q1", q1_all)
            for c in range(NFC):
                if c >= 2:
                    W(t, "ac", hb_ac0 + c - 1)
                inc(t.matmul(psB[:, c % 2, 0:TH], h128[:], HID[:, c, :],
                             start=True, stop=True), "pe")
        blk.tensor(_h128b)

        def _h128bev(sc):
            for c in range(NFC):
                W(sc, "pe", hb_pe + c + 1)
                inc(sc.activation(HID[:, c, :], psB[:, c % 2, 0:TH],
                                  AF.Copy), "ac")
        blk.scalar(_h128bev)
        hb_ac = cnt["ac"]

        def _h64(v):
            W(v, "ac", hb_ac)
            for s in (1, 2, 4, 8, 16, 32):
                for i in range(NFC):
                    if i & s:
                        continue
                    a, b = i, i + s
                    t1 = inc(v.tensor_copy(ttmp[:, 0:TH], HID[:, a, :]), "ve")
                    v.wait_ge(sems["ve"], t1)
                    t2 = inc(v.tensor_tensor(HID[:, a, :], HID[:, a, :],
                                             HID[:, b, :], A.add), "ve")
                    v.wait_ge(sems["ve"], t2)
                    t3 = inc(v.tensor_tensor(HID[:, b, :], ttmp[:, 0:TH],
                                             HID[:, b, :], A.subtract), "ve")
                    v.wait_ge(sems["ve"], t3)
            t1 = inc(v.tensor_tensor(qtmp[:, 0, 0:TH], HID[:, 0, :],
                                     HID[:, 1, :], A.max), "ve")
            t2 = inc(v.tensor_tensor(qtmp[:, 1, 0:TH], HID[:, 0, :],
                                     HID[:, 1, :], A.min), "ve")
            v.wait_ge(sems["ve"], t2)
            for c in range(2, NFC):
                t1 = inc(v.tensor_tensor(qtmp[:, 0, 0:TH], qtmp[:, 0, 0:TH],
                                         HID[:, c, :], A.max), "ve")
                t2 = inc(v.tensor_tensor(qtmp[:, 1, 0:TH], qtmp[:, 1, 0:TH],
                                         HID[:, c, :], A.min), "ve")
                v.wait_ge(sems["ve"], t2)
            t1 = inc(v.scalar_tensor_tensor(qtmp[:, 0, 0:TH], qtmp[:, 1, 0:TH],
                                            -1.0, qtmp[:, 0, 0:TH],
                                            A.mult, A.max), "ve")
            v.wait_ge(sems["ve"], t1)
            tok["fold2"] = cnt["ve"]
        blk.vector(_h64)

        def _par2(g):
            W(g, "ve", tok["fold2"])
            tok["par2"] = inc(g.reduce_max(rows.ap()[0:1, 1536:1536 + TH],
                                           qtmp[:, 0, 0:TH], AX.C), "gc")
        blk.gpsimd(_par2)

        def _rows2(v):
            W(v, "gc", tok["par2"])
            vchain(v, v.tensor_scalar(rows.ap()[0:1, 0:TH],
                                      rows.ap()[0:1, 1536:1536 + TH],
                                      1.0 / math.sqrt(FF), EPS_Q,
                                      A.mult, A.max))
            vchain(v, v.reciprocal(rows.ap()[0:1, 512:512 + TH],
                                   rows.ap()[0:1, 0:TH]))
            vchain(v, v.tensor_scalar(rows.ap()[0:1, 512:512 + TH],
                                      rows.ap()[0:1, 512:512 + TH],
                                      127.0 / math.sqrt(FF), None, A.mult))
            inc(v.tensor_scalar(rows.ap()[0:1, 1024:1024 + TH],
                                rows.ap()[0:1, 0:TH],
                                1.0 / 127.0, None, A.mult), "q3")
        blk.vector(_rows2)
        r2_tok = cnt["q3"]

        def _bc2(t):
            W(t, "q3", r2_tok)
            inc(t.matmul(psD[:, 0:TH], ones_b32[:],
                         rows.ap()[0:1, 512:512 + TH],
                         start=True, stop=True), "pe")
            inc(t.matmul(psE[:, 0:TH], ones_b32[:],
                         rows.ap()[0:1, 1024:1024 + TH],
                         start=True, stop=True), "pe")
        blk.tensor(_bc2)
        bc2_tok = cnt["pe"]

        def _cp2(sc):
            W(sc, "pe", bc2_tok)
            inc(sc.activation(cbc[:, 0:TH], psD[:, 0:TH], AF.Copy), "ac")
            inc(sc.activation(qtmp[:, 1, 0:TH], psE[:, 0:TH], AF.Copy), "ac")
        blk.scalar(_cp2)
        cp2_tok = cnt["ac"]

        tf2 = cnt["tf"]

        def _round2(v):
            W(v, "ac", cp2_tok)
            tfb = cnt["tf"]
            for c in range(NFC):
                if c >= 1:
                    W(v, "tf", tfb + c)
                t1 = inc(v.tensor_tensor(qtmp[:, 0, 0:TH], HID[:, c, :],
                                         cbc[:, 0:TH], A.mult), "ve")
                v.wait_ge(sems["ve"], t1)
                t2 = inc(v.tensor_scalar(qint[:, 0:TH], qtmp[:, 0, 0:TH],
                                         256.5, None, A.add), "ve")
                v.wait_ge(sems["ve"], t2)
                inc(v.scalar_tensor_tensor(HID[:, c, :], qint[:, 0:TH],
                                           -256.0, qtmp[:, 1, 0:TH],
                                           A.add, A.mult), "tf")
        blk.vector(_round2)

        if dbg == -22:
            def _d22(g):
                W(g, "tf", cnt["tf"])
                inc(g.dma_start(out=y_fm[:, 0:TH]
                                .rearrange("(c p) t -> p c t", p=128),
                                in_=HID[:, 0:16, :]), "gi", 16)
                inc(g.dma_start(out=y_fm[:, TH:2 * TH]
                                .rearrange("(c p) t -> p c t", p=128),
                                in_=HID[:, 16:32, :]), "gi", 16)
                W(g, "gi", cnt["gi"])
            blk.gpsimd(_d22)
            return

        sdb = [cnt["s0"], cnt["s1"]]
        pf3 = cnt["pf"]; af3 = cnt["af"]

        def _syd(s_):
            nsl = 0
            for r in range(NCORES):
                for s4 in range(4):
                    base = r * TERN + 2 * GT + s4 * DB
                    src = tern_out[base:base + DB] \
                        .rearrange("(p c o) -> p c o", c=16, o=DN)
                    if nsl >= 2:
                        W(s_, "pf", pf3 + nsl - 1)
                    else:
                        W(s_, "pe", pe2 + 4 * NGRP)
                        W(s_, "zf", zf2 + 4 * NGRP)
                    inc(s_.dma_start(out=ws[:, nsl % 2, :, 0:256],
                                     in_=src), "s%d" % (nsl % 2), 16)
                    nsl += 1
        blk.sync(_syd)

        def _ted(t):
            nsl = 0
            for r in range(NCORES):
                if r >= 1:
                    W(t, "af", af3 + 2 * r)
                for s4 in range(4):
                    W(t, "s%d" % (nsl % 2), sdb[nsl % 2] + 16 * (nsl // 2 + 1))
                    W(t, "tf", tf2 + (s4 + 1) * 16)
                    for c16 in range(16):
                        cg = s4 * 16 + c16
                        for oc in range(2):
                            iv = t.matmul(psA[:, oc, 0:TH],
                                          ws[:, nsl % 2, c16,
                                             oc * 128:(oc + 1) * 128],
                                          HID[:, cg, :],
                                          start=(cg == 0), stop=(cg == NFC - 1))
                            if c16 == 15 and oc == 1:
                                inc(iv, "pf")
                    nsl += 1
        blk.tensor(_ted)

        def _evd(v):
            for r in range(NCORES):
                for oc in range(2):
                    W(v, "pf", pf3 + 4 * (r + 1))
                    ocg = 2 * r + oc
                    inc(v.scalar_tensor_tensor(
                        stream[:, ocg, h * TH:(h + 1) * TH],
                        psA[:, oc, 0:TH], scl[:, 6:7],
                        stream[:, ocg, h * TH:(h + 1) * TH],
                        A.mult, A.add), "af")
        blk.vector(_evd)

        if dbg == -23:
            def _d23(g):
                W(g, "af", cnt["af"])
                inc(g.dma_start(out=y_fm.rearrange("(c p) t -> p c t", p=128),
                                in_=stream[:]), "gi", 16)
                W(g, "gi", cnt["gi"])
            blk.gpsimd(_d23)
            return

    if upto in (-20, -21, -22, -23):
        import os as _os
        ffn1_half(0, dbg=upto)
        for t_ in reversed(ctx):
            t_.__exit__(None, None, None)
        return nc

    for h in range(2):
        ffn1_half(h)
    layernorm(store_half=True)

    _finish()
    return _close(nc)


_CACHE = {}


def _host_stage(inputs, S):
    T = S // 2
    QS = min(512, T)
    NBM = 2 * (QS // 128)
    d = {}
    d["wq0"] = np.ascontiguousarray(inputs["L0_attn_Wq"].T)
    d["wk0"] = np.ascontiguousarray(inputs["L0_attn_Wk"].T)
    d["wv0"] = np.ascontiguousarray(inputs["L0_attn_Wv"].T)
    d["wo0"] = np.ascontiguousarray(inputs["L0_attn_Wo"].T)
    d["fct"] = None
    d["bias0"] = np.ascontiguousarray(
        np.asarray(inputs["L0_ffn_bias"]).reshape(NC, 128).T)
    d["wq1"] = np.ascontiguousarray(
        np.transpose(inputs["L1_attn_W_Q"], (2, 0, 1)).reshape(D, D))
    d["wk1"] = np.ascontiguousarray(
        np.transpose(inputs["L1_attn_W_K"], (2, 0, 1)).reshape(D, D))
    d["wv1"] = np.ascontiguousarray(
        np.transpose(inputs["L1_attn_W_V"], (2, 0, 1)).reshape(D, D))
    d["wo1"] = np.ascontiguousarray(
        np.transpose(inputs["L1_attn_W_O"], (0, 2, 1)).reshape(D, D))
    gT = np.ascontiguousarray(inputs["L1_ffn_w_gate"].T)
    uT = np.ascontiguousarray(inputs["L1_ffn_w_up"].T)
    dT = np.ascontiguousarray(inputs["L1_ffn_w_down"].T)
    H = np.array([[1.0]], dtype=np.float32)
    while H.shape[0] < 128:
        H = np.block([[H, H], [H, -H]])
    d["h128t"] = np.ascontiguousarray(H.astype(np.float32))

    x = np.asarray(inputs["L0_attn_x"])
    B = x.shape[0]
    maps = []
    for core in range(NCORES):
        b = core // 2
        p = core % 2
        blocks = [2 * j + p for j in range(T // 128)]
        cols = np.concatenate([np.arange(128 * t, 128 * (t + 1))
                               for t in blocks])
        m = dict(d)
        m["fct"] = np.asarray(inputs["L0_ffn_factors"][core])
        xc = np.ascontiguousarray(x[b].T[:, cols])
        m["x_fm"] = xc
        m["tg_sl"] = np.ascontiguousarray(
            gT[:, core * (FF // NCORES):(core + 1) * (FF // NCORES)])
        m["tu_sl"] = np.ascontiguousarray(
            uT[:, core * (FF // NCORES):(core + 1) * (FF // NCORES)])
        m["td_sl"] = np.ascontiguousarray(
            dT[:, core * (D // NCORES):(core + 1) * (D // NCORES)])
        mk = np.zeros((NBM, 128, QS), dtype=np.float32)
        kp = np.arange(128)
        qc = np.arange(QS)
        qblk = qc // 128
        qoff = 128 * (2 * qblk + p) + (qc % 128)
        for mm in range(NBM):
            koff = 128 * mm + kp
            mk[mm] = (koff[:, None] <= qoff[None, :]).astype(np.float32)
        m["maskt"] = mk.reshape(NBM * 128, QS)
        pos = cols.astype(np.float32)
        half = HD // 2
        theta = (10000.0 ** (-np.arange(half, dtype=np.float32) * 2.0 / HD))
        ang = theta[:, None] * pos[None, :]
        cosv = np.cos(ang).astype(np.float32)
        sinv = np.sin(ang).astype(np.float32)
        m["ropec"] = np.ascontiguousarray(
            np.concatenate([cosv, cosv], axis=0))
        m["ropes"] = np.ascontiguousarray(
            np.concatenate([-sinv, sinv], axis=0))
        maps.append(m)
    return maps, x.shape


def kernel(**inputs):
    S = np.asarray(inputs["L0_attn_x"]).shape[1]
    if S not in _CACHE:
        _CACHE[S] = build(S)
    nc = _CACHE[S]
    maps, xshape = _host_stage(inputs, S)
    res = run_bass_kernel_spmd(nc, maps, core_ids=list(range(NCORES)))
    B, S_, D_ = xshape
    T = S // 2
    out = np.empty((B, S_, D_), dtype=np.float32)
    for core in range(NCORES):
        b = core // 2
        p = core % 2
        y = res.results[core]["y_fm"]
        for j in range(T // 128):
            tblk = 2 * j + p
            out[b, 128 * tblk:128 * (tblk + 1), :] = \
                y[:, 128 * j:128 * (j + 1)].T
    return out


# version 08 (final)
# speedup vs baseline: 1.3536x; 1.0218x over previous
import math
import numpy as np
import concourse.bass as bass
import concourse.mybir as mybir
from concourse.bass_utils import run_bass_kernel_spmd
from concourse.bass_isa import ReduceOp

F32 = mybir.dt.float32
BF16 = mybir.dt.bfloat16
A = mybir.AluOpType
AF = mybir.ActivationFunctionType
AX = mybir.AxisListType

D = 2048
NC = D // 128
NH = 16
HD = 128
FF = 8192
NFC = FF // 128
EPS_LN = 1e-5
EPS_L0 = 1e-6
EPS_Q = 1e-6
NCORES = 8
GT = D * (FF // NCORES)
DT = FF * (D // NCORES)
TERN = 2 * GT + DT
NW = float(D * FF)
ALL8 = [list(range(NCORES))]
PAIRS = [[0, 1], [2, 3], [4, 5], [6, 7]]
INV_SQRT8 = 1.0 / math.sqrt(8.0)
MAGIC = 256.5


def build(S=2048, upto=4):
    T = S // 2
    TH = T // 2
    NTB = T // 128
    QS = min(512, T)
    NQ = T // QS
    LQB = QS // 128
    NBM = 2 * LQB
    NT = min(512, TH)
    NB_H = TH // NT
    KBq = max(1, T // 512)
    NKB = min(512, T)
    nc = bass.Bass("TRN2", target_bir_lowering=False, debug=False)

    def din(name, shape):
        return nc.dram_tensor(name, shape, F32, kind="ExternalInput").ap()

    x_fm = din("x_fm", [D, T])
    wq0 = din("wq0", [D, D]); wk0 = din("wk0", [D, D])
    wv0 = din("wv0", [D, D]); wo0 = din("wo0", [D, D])
    wq1 = din("wq1", [D, D]); wk1 = din("wk1", [D, D])
    wv1 = din("wv1", [D, D]); wo1 = din("wo1", [D, D])
    fct = din("fct", [D, D])
    bias0 = din("bias0", [128, NC])
    tg_sl = din("tg_sl", [D, FF // NCORES])
    tu_sl = din("tu_sl", [D, FF // NCORES])
    td_sl = din("td_sl", [FF, D // NCORES])
    maskt = din("maskt", [NBM * 128, QS])
    ropec = din("ropec", [128, T])
    ropes = din("ropes", [128, T])
    h128t = din("h128t", [128, 128])
    y_fm = nc.dram_tensor("y_fm", [D, T], F32, kind="ExternalOutput").ap()

    f_in = nc.dram_tensor("f_in", [D, D], F32).ap()
    f_out = nc.dram_tensor("f_out", [D, D], F32, addr_space="Shared").ap()
    f_sc_in = nc.dram_tensor("f_sc_in", [128, 4], F32).ap()
    f_sc_out = nc.dram_tensor("f_sc_out", [128, 4], F32).ap()
    tern_in = nc.dram_tensor("tern_in", [TERN], BF16).ap()
    tern_out = nc.dram_tensor("tern_out", [NCORES * TERN], BF16,
                              addr_space="Shared").ap()
    k_in = [nc.dram_tensor(f"k_in{l}", [D, T], BF16).ap() for l in range(2)]
    k_out = [nc.dram_tensor(f"k_out{l}", [2 * D, T], BF16).ap() for l in range(2)]
    v_in = [nc.dram_tensor(f"v_in{l}", [NH * T, HD], BF16).ap() for l in range(2)]
    v_out = [nc.dram_tensor(f"v_out{l}", [2 * NH * T, HD], BF16).ap()
             for l in range(2)]

    ctx = []

    def sbt(name, shape, dt=BF16):
        t = nc.sbuf_tensor(name, shape, dt); ctx.append(t); return t.__enter__()

    def pst(name, shape):
        t = nc.psum_tensor(name, shape, F32); ctx.append(t); return t.__enter__()

    stream = sbt("stream", [128, NC, T])
    xbh = sbt("xbh", [128, NC, TH])
    big = sbt("big", [128, 32 * T])
    QV = big.ap().rearrange("p (c t) -> p c t", t=T)
    HID = big.ap().rearrange("p (c t) -> p c t", t=TH)

    masks = sbt("masks", [128, NBM, QS])
    pt = sbt("pt", [128, 2, QS])
    qtmp = sbt("qtmp", [128, 2, 512], F32)
    cbc = sbt("cbc", [128, 512], F32)
    rbc = sbt("rbc", [128, 2 * QS])
    kvq = sbt("kvq", [128, 8 * T])
    kd = kvq.ap()[:, 0:4 * T].rearrange("p (b r t) -> p b r t", b=2, r=2)
    vd = kvq.ap()[:, 4 * T:8 * T].rearrange("p (b k e) -> p b k e",
                                            b=2, e=HD)
    xq1 = kvq.ap().rearrange("p (c t) -> p c t", t=TH)[:, 0:NC, :]
    ws = sbt("ws", [128, 2, NC, 512])
    tstage = sbt("tstage", [128, 2, 512], F32)
    qint = tstage.ap()[:, 0, :].bitcast(mybir.dt.int32)
    ttmp = sbt("ttmp", [128, 512])
    ttrn = sbt("ttrn", [128, 2, 512])
    cosf = sbt("cosf", [128, T])
    sinf = sbt("sinf", [128, T])
    h128 = sbt("h128", [128, 128])
    sc512 = sbt("sc512", [128, 2, 512])
    rk = sbt("rk", [128, 2, 512])
    rsw = sbt("rsw", [128, 512])
    ones_k = sbt("ones_k", [128, 1])
    ones_b = sbt("ones_b", [1, 128])
    neg_b = sbt("neg_b", [1, 128])
    ones_k32 = sbt("ones_k32", [128, 1], F32)
    ones_b32 = sbt("ones_b32", [1, 128], F32)
    browz = sbt("browz", [1, 1536])
    bias_sb = sbt("bias_sb", [128, NC], F32)
    scl = sbt("scl", [128, 8], F32)
    rtmp = sbt("rtmp", [128, 136], F32)
    rows = sbt("rows", [1, 2048], F32)

    psA = pst("psA", [128, 2, 512])
    psB = pst("psB", [128, 2, 512])
    psC = pst("psC", [128, 2, 512])
    psD = pst("psD", [128, 512])
    psE = pst("psE", [128, 512])

    blk = nc.Block().__enter__()
    ctx.append(blk)

    SEMS = ["gi", "si", "cf", "cy", "g0", "g1", "s0", "s1", "k0", "k1",
            "ck0", "ck1", "cv0", "cv1", "ct0", "ct1", "cu0", "cu1",
            "t0", "t1", "u0", "u1", "ab", "ts",
            "cc", "gc", "pe", "ve", "ac", "sf", "pf", "zf",
            "kf", "af", "df", "cb", "rp", "q1", "q2", "q3", "q5", "tf", "bm"]
    sems = {}
    cnt = {}
    for sname in SEMS:
        sems[sname] = nc.semaphore(sname).__enter__()
        cnt[sname] = 0

    def inc(inst, name, n=1):
        inst.then_inc(sems[name], n)
        cnt[name] += n
        return cnt[name]

    def W(eng, name, v):
        if v > 0:
            eng.wait_ge(sems[name], v)

    def vchain(v, inst):
        t = inc(inst, "ve")
        v.wait_ge(sems["ve"], t)
        return t

    tok = {}

    def ph0_gp(g):
        inc(g.memset(ones_k[:], 1.0), "gc")
        inc(g.memset(ones_b[:], 1.0), "gc")
        inc(g.memset(neg_b[:], -1.0), "gc")
        inc(g.memset(ones_k32[:], 1.0), "gc")
        inc(g.memset(ones_b32[:], 1.0), "gc")
        inc(g.memset(rows[:], 0.0), "gc")
        tok["consts"] = cnt["gc"]
        inc(g.dma_start(out=f_in, in_=fct), "ts", 16)
        tok["fct_ld"] = cnt["ts"]
        inc(g.dma_start(
            out=masks[:], in_=maskt.rearrange("(m p) q -> p m q", p=128)), "gi", 16)
        inc(g.dma_start(out=cosf[:], in_=ropec), "gi", 16)
        inc(g.dma_start(out=sinf[:], in_=ropes), "gi", 16)
        inc(g.dma_start(out=h128[:], in_=h128t), "gi", 16)
        inc(g.dma_start(
            out=stream[:], in_=x_fm.rearrange("(c p) t -> p c t", p=128)),
            "gi", 16)
        tok["gi_all"] = cnt["gi"]
        W(g, "ts", tok["fct_ld"])
        tok["ar1"] = inc(g.collective_compute(
            "AllReduce", A.add, replica_groups=ALL8,
            ins=[f_in.opt()], outs=[f_out.opt()]), "cc")
    blk.gpsimd(ph0_gp)

    def ph0_sy(s_):
        inc(s_.dma_start(out=bias_sb[:], in_=bias0), "si", 16)
        tok["si_all"] = cnt["si"]
    blk.sync(ph0_sy)

    t_slabs = []
    GCH = (FF // NCORES) // 512
    for r in range(D // 128):
        for ch in range(GCH):
            t_slabs.append((tg_sl[r * 128:(r + 1) * 128,
                                  ch * 512:(ch + 1) * 512], 0, 512))
    for r in range(D // 128):
        for ch in range(GCH):
            t_slabs.append((tu_sl[r * 128:(r + 1) * 128,
                                  ch * 512:(ch + 1) * 512], 1, 512))
    for r in range(FF // 128):
        t_slabs.append((td_sl[r * 128:(r + 1) * 128, :], 2, D // NCORES))
    NSLAB = len(t_slabs)
    mi_first = [min(i for i, s in enumerate(t_slabs) if s[1] == m)
                for m in range(3)]
    mi_last = [max(i for i, s in enumerate(t_slabs) if s[1] == m)
               for m in range(3)]
    sh_state = {"sb": [cnt["s0"], cnt["s1"]], "ab0": cnt["ab"],
                "tfs0": cnt["tf"], "sc_i": 0, "te_i": 0, "ntile": 0}

    def ph1_sy(s_):
        ab0 = sh_state["ab0"]
        for i, (src, mi, cols) in enumerate(t_slabs):
            if i >= 2:
                W(s_, "ab", ab0 + i - 1)
            inc(s_.dma_start(out=tstage[:, i % 2, 0:cols], in_=src),
                "s%d" % (i % 2), 16)
    blk.sync(ph1_sy)

    SROWS = [psD[0:1, 0:512], psE[0:1, 0:512], psB[0:1, 0, 0:512]]

    def _acc_mm(t, i):
        src, mi, cols = t_slabs[i]
        W(t, "ab", sh_state["ab0"] + i + 1)
        inc(t.matmul(SROWS[mi][:, 0:cols], ones_k[:],
                     ttrn[:, i % 2, 0:cols],
                     start=(i == mi_first[mi]),
                     stop=(i == mi_last[mi])), "tf")

    def mk_scale_hook(ntiles):
        per = -(-NSLAB // ntiles)
        sh_state["te_done"] = 0

        def sc_hook(sc):
            i = sh_state["sc_i"]
            for j in range(per):
                if i >= NSLAB:
                    break
                src, mi, cols = t_slabs[i]
                W(sc, "s%d" % (i % 2),
                  sh_state["sb"][i % 2] + 16 * (i // 2 + 1))
                if i >= 2:
                    W(sc, "tf", sh_state["tfs0"] + i - 1)
                inc(sc.activation(ttrn[:, i % 2, 0:cols],
                                  tstage[:, i % 2, 0:cols], AF.Abs), "ab")
                i += 1
            sh_state["sc_i"] = i

        def te_hook(t):
            k = sh_state["te_i"]
            for i in range(sh_state["te_done"], min(k * per, NSLAB)):
                _acc_mm(t, i)
                sh_state["te_done"] = i + 1
            sh_state["te_i"] = k + 1

        def te_flush():
            def _tf(t):
                for i in range(sh_state["te_done"],
                               min(sh_state["sc_i"], NSLAB)):
                    _acc_mm(t, i)
                    sh_state["te_done"] = i + 1
            blk.tensor(_tf)
        return sc_hook, te_hook, te_flush

    def ph1_finish():
        tfs0 = sh_state["tfs0"]

        def _tf(t):
            for i in range(sh_state["te_done"], NSLAB):
                _acc_mm(t, i)
                sh_state["te_done"] = i + 1
        blk.tensor(_tf)

        def _ve(v):
            W(v, "tf", tfs0 + NSLAB)
            for m in range(3):
                cols = 512 if m < 2 else (D // NCORES)
                vchain(v, v.tensor_reduce(rows.ap()[0:1, m:m + 1],
                                          SROWS[m][:, 0:cols], AX.X, A.add))
            tok["sctot"] = cnt["ve"]
        blk.vector(_ve)

        def _te(t):
            W(t, "ve", tok["sctot"])
            tok["scbc"] = inc(t.matmul(psD[:, 0:4], ones_b32[:],
                                       rows.ap()[0:1, 0:4],
                                       start=True, stop=True), "pe")
        blk.tensor(_te)

        def _sc(sc):
            W(sc, "pe", tok["scbc"])
            t1 = inc(sc.activation(rtmp[:, 0:4], psD[:, 0:4], AF.Copy), "ac")
            sc.wait_ge(sems["ac"], t1)
            tok["scst"] = inc(sc.dma_start(out=f_sc_in, in_=rtmp[:, 0:4]),
                              "cf", 16)
        blk.scalar(_sc)

        def _gp(g):
            W(g, "cf", tok["scst"])
            tok["ar2"] = inc(g.collective_compute(
                "AllReduce", A.add, replica_groups=ALL8,
                ins=[f_sc_in.opt()], outs=[f_sc_out.opt()]), "cc")
        blk.gpsimd(_gp)

    def proj(wsrc, evac_half, cast_wait=None, rhs_z=False, slab_gate=None,
             te_hook=None, use_cast=False, te_gate=None):
        pe0 = cnt["pe"]; af0 = cnt["af"]
        gb = [cnt["g0"], cnt["g1"]]
        nts = 4 * NB_H
        nth = 16 * NB_H
        info = {"pe0": pe0, "af0": af0, "nth": nth, "cast": [0, 0], "NT": NT}
        for h in range(2):
            if use_cast:
                def _cast(v, h=h):
                    if cast_wait is not None:
                        cast_wait(v, h)
                    W(v, "pe", pe0 + (nth if h == 1 else 0))
                    for c in range(NC):
                        inc(v.tensor_copy(xbh[:, c, :],
                                          stream[:, c, h * TH:(h + 1) * TH]),
                            "ve")
                    info["cast"][h] = cnt["ve"]
                blk.vector(_cast)

            def _slabs(g, h=h):
                if slab_gate is not None and h == 0:
                    slab_gate(g)
                for i in range(4):
                    gi = h * 4 + i
                    W(g, "pe", pe0 + (gi - 1) * nts if gi >= 2 else pe0)
                    inc(g.dma_start(
                        out=ws[:, gi % 2, :, :],
                        in_=wsrc[:, i * 512:(i + 1) * 512]
                        .rearrange("(c p) o -> p c o", p=128)),
                        "g%d" % (gi % 2), 16)
            blk.gpsimd(_slabs)

            def _te(t, h=h):
                if use_cast:
                    W(t, "ve", info["cast"][h])
                elif rhs_z:
                    W(t, "af", af0)
                elif te_gate is not None and h == 0:
                    te_gate(t)
                n = 0
                for i in range(4):
                    gi0 = h * 4 + i
                    W(t, "g%d" % (gi0 % 2), gb[gi0 % 2] + 16 * (gi0 // 2 + 1))
                    gi = h * 4 + i
                    for j in range(4):
                        for tb in range(NB_H):
                            gn = h * nth + n
                            W(t, "af", af0 + gn - 1 if gn >= 2 else af0)
                            last = None
                            for c in range(NC):
                                if rhs_z:
                                    rhs = QV[:, 16 + c, h * TH + tb * NT:
                                             h * TH + (tb + 1) * NT]
                                elif use_cast:
                                    rhs = xbh[:, c, tb * NT:(tb + 1) * NT]
                                else:
                                    rhs = stream[:, c,
                                                 h * TH + tb * NT:
                                                 h * TH + (tb + 1) * NT]
                                last = t.matmul(
                                    psA[:, gn % 2, 0:NT],
                                    ws[:, gi % 2, c, j * 128:(j + 1) * 128],
                                    rhs, start=(c == 0), stop=(c == NC - 1))
                            inc(last, "pe")
                            if te_hook is not None:
                                te_hook(t)
                            n += 1
            blk.tensor(_te)
            evac_half(h, info)
        return info

    def tile_oc_tb(n):
        return n // NB_H, n % NB_H

    def evac_rope_mk(dest_q, layer, sc_hook=None):
        st = {"ac0": cnt["ac"], "rp0": cnt["rp"],
              "ckb": [cnt["ck0"], cnt["ck1"]],
              "snap": {k: cnt[k] for k in
                       ("ck0", "ck1", "cv0", "cv1", "ct0", "ct1")}}

        def emit(h, info):
            ac0 = st["ac0"]; rp0 = st["rp0"]; ckb = st["ckb"]
            nth = info["nth"]; pe0 = info["pe0"]; af0 = info["af0"]

            def _sc(sc):
                def store(gn):
                    oc, tb = tile_oc_tb(gn % nth)
                    t0 = (gn // nth) * TH + tb * NT
                    W(sc, "af", af0 + gn + 1)
                    inc(sc.dma_start(
                        out=k_in[layer][oc * 128:(oc + 1) * 128, t0:t0 + NT],
                        in_=rk[:, gn % 2, 0:NT]), "ck%d" % (gn % 2), 16)
                for n in range(nth):
                    gn = h * nth + n
                    W(sc, "pe", pe0 + gn + 1)
                    if gn < 2:
                        for k2, v2 in st["snap"].items():
                            W(sc, k2, v2)
                    if gn >= 2:
                        W(sc, "rp", rp0 + gn - 1)
                    if not dest_q and gn >= 2:
                        store(gn - 2)
                    inc(sc.activation(sc512[:, gn % 2, 0:NT],
                                      psA[:, gn % 2, 0:NT], AF.Copy), "ac")
                    if sc_hook is not None:
                        sc_hook(sc)
                if not dest_q and h == 1:
                    store(2 * nth - 2)
                    store(2 * nth - 1)
            blk.scalar(_sc)

            def _ve(v):
                for n in range(nth):
                    gn = h * nth + n
                    oc, tb = tile_oc_tb(n)
                    t0 = h * TH + tb * NT
                    W(v, "ac", ac0 + gn + 1)
                    if gn >= 1:
                        W(v, "af", af0 + gn)
                    if not dest_q and gn >= 2:
                        W(v, "ck%d" % (gn % 2),
                          ckb[gn % 2] + 16 * ((gn - 2) // 2 + 1))
                    inc(v.tensor_copy(rsw[0:64, 0:NT],
                                      sc512[64:128, gn % 2, 0:NT]), "ve")
                    t2 = inc(v.tensor_copy(rsw[64:128, 0:NT],
                                           sc512[0:64, gn % 2, 0:NT]), "ve")
                    v.wait_ge(sems["ve"], t2)
                    t3 = inc(v.tensor_tensor(rsw[:, 0:NT], rsw[:, 0:NT],
                                             sinf[:, t0:t0 + NT], A.mult), "ve")
                    t4 = inc(v.tensor_tensor(rk[:, gn % 2, 0:NT],
                                             sc512[:, gn % 2, 0:NT],
                                             cosf[:, t0:t0 + NT], A.mult), "rp")
                    v.wait_ge(sems["ve"], t3)
                    v.wait_ge(sems["rp"], t4)
                    dst = QV[:, oc, t0:t0 + NT] if dest_q else rk[:, gn % 2, 0:NT]
                    inc(v.tensor_tensor(dst, rk[:, gn % 2, 0:NT],
                                        rsw[:, 0:NT], A.add), "af")
            blk.vector(_ve)
        return emit

    def evac_copy_mk(layer, mode, qscale=None, sc_hook=None):
        sem_pfx = "cv" if mode == "v" else "ck"
        st = {"cb2": [cnt[sem_pfx + "0"], cnt[sem_pfx + "1"]],
              "snap": {k: cnt[k] for k in
                       ("ck0", "ck1", "cv0", "cv1", "ct0", "ct1")}}
        vv = v_in[layer].rearrange("(h t) e -> t h e", t=T)

        def emit(h, info):
            cb2 = st["cb2"]
            af0 = info["af0"]; nth = info["nth"]; pe0 = info["pe0"]
            ntw = info["NT"]

            def _sc(sc):
                def store(gn):
                    n2 = gn % nth
                    h2 = gn // nth
                    W(sc, "af", af0 + gn + 1)
                    if mode == "v":
                        i = n2 // (TH // 128)
                        tb = n2 % (TH // 128)
                        tt0 = h2 * TH + tb * 128
                        srcap = sc512.ap().rearrange(
                            "p b (hh e) -> p b hh e", e=128)[:, gn % 2, :, :]
                        inc(sc.dma_start(out=vv[tt0:tt0 + 128,
                                                4 * i:4 * i + 4, :],
                                         in_=srcap), "cv%d" % (gn % 2), 16)
                    else:
                        oc, tb = tile_oc_tb(n2)
                        tt0 = h2 * TH + tb * NT
                        inc(sc.dma_start(
                            out=k_in[layer][oc * 128:(oc + 1) * 128,
                                            tt0:tt0 + NT],
                            in_=sc512[:, gn % 2, 0:NT]), "ck%d" % (gn % 2), 16)

                for n in range(nth):
                    gn = h * nth + n
                    W(sc, "pe", pe0 + gn + 1)
                    if gn < 2 and mode != "q":
                        for k2, v2 in st["snap"].items():
                            W(sc, k2, v2)
                    if mode == "q":
                        oc, tb = tile_oc_tb(n)
                        t0 = h * TH + tb * NT
                        if qscale is None:
                            inc(sc.activation(QV[:, oc, t0:t0 + NT],
                                              psA[:, gn % 2, 0:NT],
                                              AF.Copy), "af")
                        else:
                            inc(sc.activation(QV[:, oc, t0:t0 + NT],
                                              psA[:, gn % 2, 0:NT],
                                              AF.Copy, scale=qscale), "af")
                    else:
                        if gn >= 2:
                            store(gn - 2)
                            W(sc, sem_pfx + str(gn % 2),
                              cb2[gn % 2] + 16 * ((gn - 2) // 2 + 1))
                        inc(sc.activation(sc512[:, gn % 2, 0:ntw],
                                          psA[:, gn % 2, 0:ntw], AF.Copy), "af")
                    if sc_hook is not None:
                        sc_hook(sc)
                if mode != "q" and h == 1:
                    store(2 * nth - 2)
                    store(2 * nth - 1)
            blk.scalar(_sc)
        return emit

    def proj_tokmaj(wsrc, evac_half, te_hook=None, te_gate=None):
        pe0 = cnt["pe"]; af0 = cnt["af"]
        gb = [cnt["g0"], cnt["g1"]]
        ntb_h = TH // 128
        nth = 4 * ntb_h
        info = {"pe0": pe0, "af0": af0, "nth": nth, "cast": [0, 0], "NT": 512}
        for h in range(2):
            def _slabs(g, h=h):
                for i in range(4):
                    gi = h * 4 + i
                    W(g, "pe", pe0 + (gi - 1) * ntb_h if gi >= 2 else pe0)
                    inc(g.dma_start(
                        out=ws[:, gi % 2, :, :],
                        in_=wsrc[:, i * 512:(i + 1) * 512]
                        .rearrange("(c p) o -> p c o", p=128)),
                        "g%d" % (gi % 2), 16)
            blk.gpsimd(_slabs)

            def _te(t, h=h):
                if te_gate is not None and h == 0:
                    te_gate(t)
                n = 0
                for i in range(4):
                    gi0 = h * 4 + i
                    W(t, "g%d" % (gi0 % 2), gb[gi0 % 2] + 16 * (gi0 // 2 + 1))
                    gi = h * 4 + i
                    for tb in range(ntb_h):
                        gn = h * nth + n
                        W(t, "af", af0 + gn - 1 if gn >= 2 else af0)
                        last = None
                        for c in range(NC):
                            last = t.matmul(
                                psA[:, gn % 2, 0:512],
                                stream[:, c, h * TH + tb * 128:
                                       h * TH + (tb + 1) * 128],
                                ws[:, gi % 2, c, :],
                                start=(c == 0), stop=(c == NC - 1))
                        inc(last, "pe")
                        if te_hook is not None:
                            te_hook(t)
                        n += 1
            blk.tensor(_te)
            evac_half(h, info)
        return info

    def tg_x(t):
        W(t, "gi", tok["gi_all"])

    NTILE_K = 2 * 16 * NB_H
    NTILE_V = 2 * 4 * (TH // 128)
    sc_hook, te_hook, te_flush = mk_scale_hook(NTILE_K + NTILE_V)
    proj(wk0, evac_rope_mk(dest_q=False, layer=0, sc_hook=sc_hook),
         te_gate=tg_x, te_hook=te_hook)
    te_flush()
    kst0 = (cnt["ck0"], cnt["ck1"])
    proj_tokmaj(wv0, evac_copy_mk(0, "v", sc_hook=sc_hook), te_hook=te_hook)
    vst0 = (cnt["cv0"], cnt["cv1"])
    ph1_finish()
    proj(wq0, evac_rope_mk(dest_q=True, layer=0))
    tok["q0_af"] = cnt["af"]

    if upto == -1:
        qaf = cnt["af"]

        def _dbgst(g):
            W(g, "af", qaf)
            inc(g.dma_start(out=y_fm.rearrange("(c p) t -> p c t", p=128),
                            in_=QV[:, 0:16, :]), "gi", 16)
            W(g, "gi", cnt["gi"])
        blk.gpsimd(_dbgst)
        for t_ in reversed(ctx):
            t_.__exit__(None, None, None)
        return nc

    def ag0(g):
        W(g, "ck0", kst0[0]); W(g, "ck1", kst0[1])
        tok["agk0"] = inc(g.collective_compute(
            "AllGather", A.bypass, replica_groups=PAIRS,
            ins=[k_in[0].opt()], outs=[k_out[0].opt()]), "cc")
        W(g, "cv0", vst0[0]); W(g, "cv1", vst0[1])
        tok["agv0"] = inc(g.collective_compute(
            "AllGather", A.bypass, replica_groups=PAIRS,
            ins=[v_in[0].opt()], outs=[v_out[0].opt()]), "cc")
    blk.gpsimd(ag0)

    DN = D // NCORES
    GB = 128 * 16 * 512
    DB = 128 * 16 * DN

    def d_src(i):
        return td_sl[i * 128:(i + 1) * 128, :]

    def d_dst(i):
        base = 2 * GT + (i // 16) * DB
        return tern_in[base:base + DB].rearrange(
            "(p c o) -> p c o", c=16, o=DN)[:, i % 16, :]

    def g_src(j):
        mat, r, ch = j // 32, (j % 32) // GCH, j % GCH
        wsrc = tg_sl if mat == 0 else tu_sl
        return wsrc[r * 128:(r + 1) * 128, ch * 512:(ch + 1) * 512]

    def g_dst(j):
        mat, r, ch = j // 32, (j % 32) // GCH, j % GCH
        base = mat * GT + ch * GB
        return tern_in[base:base + GB].rearrange(
            "(p c o) -> p c o", c=16, o=512)[:, r, :]

    NP = FF // 128
    NV = 64
    tq_st = {}

    def tq_gp(g):
        tb = [cnt["t0"], cnt["t1"]]
        ctb = [cnt["ct0"], cnt["ct1"]]
        tfq0 = cnt["tf"]
        W(g, "ab", sh_state["ab0"] + NSLAB)
        for i in (0, 1):
            inc(g.dma_start(out=tstage[:, i % 2, 0:DN], in_=d_src(i)),
                "t%d" % (i % 2), 16)
        W(g, "cc", tok["ar2"])
        inc(g.dma_start(out=scl[:, 0:4], in_=f_sc_out), "ts", 16)
        g.wait_ge(sems["ts"], cnt["ts"])
        t1 = inc(g.tensor_scalar(scl[:, 4:8], scl[:, 0:4], 1.0 / NW, EPS_Q,
                                 A.mult, A.max), "gc")
        g.wait_ge(sems["gc"], t1)
        t2 = inc(g.tensor_scalar(scl[:, 0:4], scl[:, 4:8], 0.5, None,
                                 A.mult), "gc")
        g.wait_ge(sems["gc"], t2)
        tq_st["scl"] = t2
        for i in range(NP):
            W(g, "t%d" % (i % 2), tb[i % 2] + 16 * (i // 2 + 1))
            if i >= 2:
                W(g, "ct%d" % (i % 2), ctb[i % 2] + 16 * ((i - 2) // 2 + 1))
            else:
                W(g, "tf", tfq0)
            t1 = inc(g.tensor_scalar(ttmp[:, 0:DN], tstage[:, i % 2, 0:DN],
                                     -1.0, scl[:, 2:3], A.mult, A.is_gt),
                     "gc")
            t2 = inc(g.tensor_scalar(ttrn[:, i % 2, 0:DN],
                                     tstage[:, i % 2, 0:DN],
                                     scl[:, 2:3], None, A.is_gt), "gc")
            g.wait_ge(sems["gc"], t2)
            t3 = inc(g.tensor_tensor(ttrn[:, i % 2, 0:DN],
                                     ttrn[:, i % 2, 0:DN],
                                     ttmp[:, 0:DN], A.subtract), "tf")
            g.wait_ge(sems["tf"], cnt["tf"])
            inc(g.dma_start(out=d_dst(i), in_=ttrn[:, i % 2, 0:DN]),
                "ct%d" % (i % 2), 16)
            if i + 2 < NP:
                inc(g.dma_start(out=tstage[:, i % 2, 0:DN],
                                in_=d_src(i + 2)), "t%d" % (i % 2), 16)
        tq_st["tf_pool"] = cnt["tf"]
        tq_st["ct"] = (cnt["ct0"], cnt["ct1"])
    blk.gpsimd(tq_gp)

    def mk_vq_hook():
        ub = [cnt["u0"], cnt["u1"]]
        cub = [cnt["cu0"], cnt["cu1"]]
        tfv0 = cnt["tf"]
        st = {"j": 0}

        def _ld(s_):
            for j in range(NV):
                if j >= 2:
                    W(s_, "tf", tfv0 + j - 1)
                else:
                    W(s_, "tf", tq_st["tf_pool"])
                inc(s_.dma_start(out=tstage[:, j % 2, 0:512],
                                 in_=g_src(j)), "u%d" % (j % 2), 16)
        blk.sync(_ld)

        def vq(v):
            j0 = st["j"]
            if j0 >= NV:
                return
            if j0 == 0:
                W(v, "gc", tq_st["scl"])
                W(v, "tf", tq_st["tf_pool"])
                W(v, "ct0", tq_st["ct"][0])
                W(v, "ct1", tq_st["ct"][1])
            for j in range(j0, min(j0 + 2, NV)):
                mat = j // 32
                W(v, "u%d" % (j % 2), ub[j % 2] + 16 * (j // 2 + 1))
                if j >= 2:
                    W(v, "cu%d" % (j % 2),
                      cub[j % 2] + 16 * ((j - 2) // 2 + 1))
                if j >= 1:
                    W(v, "tf", tfv0 + j)
                t1 = inc(v.tensor_scalar(ttmp[:, 0:512],
                                         tstage[:, j % 2, 0:512],
                                         -1.0, scl[:, mat:mat + 1],
                                         A.mult, A.is_gt), "ve")
                v.wait_ge(sems["ve"], t1)
                inc(v.scalar_tensor_tensor(ttrn[:, j % 2, 0:512],
                                           tstage[:, j % 2, 0:512],
                                           scl[:, mat:mat + 1],
                                           ttmp[:, 0:512],
                                           A.is_gt, A.subtract), "tf")
            st["j"] = j0 + 2

        def stores():
            def _st(sc):
                for j in range(NV):
                    W(sc, "tf", tfv0 + j + 1)
                    inc(sc.dma_start(out=g_dst(j),
                                     in_=ttrn[:, j % 2, 0:512]),
                        "cu%d" % (j % 2), 16)
            blk.scalar(_st)
        return vq, stores

    def tq_ag():
        def _gp(g):
            W(g, "ct0", tq_st["ct"][0])
            W(g, "ct1", tq_st["ct"][1])
            W(g, "cu0", cnt["cu0"])
            W(g, "cu1", cnt["cu1"])
            tok["agt"] = inc(g.collective_compute(
                "AllGather", A.bypass, replica_groups=ALL8,
                ins=[tern_in.opt()], outs=[tern_out.opt()]), "cc")
        blk.gpsimd(_gp)

    def attn(layer):
        tiles = []
        hg_sizes = []
        for h in range(NH):
            for g in range(NQ):
                lst = []
                for j in range(LQB * g):
                    for r in range(2):
                        lst.append((r, j, None))
                for m in range(NBM):
                    lst.append((m % 2, LQB * g + m // 2, m))
                hg_sizes.append(len(lst))
                for it, (r, j, bm) in enumerate(lst):
                    tiles.append(dict(h=h, g=g, r=r, j=j, bm=bm,
                                      first=(it == 0),
                                      last=(it == len(lst) - 1),
                                      hg=h * NQ + g))
        for i, td in enumerate(tiles):
            td["ti"] = i
        NHG = NH * NQ
        thru = np.cumsum(hg_sizes).tolist()
        band_pre = []
        nb = 0
        for td in tiles:
            if td["bm"] is not None:
                nb += 1
            band_pre.append(nb)
        tiles_by_hg = [[] for _ in range(NHG)]
        for td in tiles:
            tiles_by_hg[td["hg"]].append(td)

        pe0 = cnt["pe"]; sf0 = cnt["sf"]; pf0 = cnt["pf"]; zf0 = cnt["zf"]
        ac0 = cnt["ac"]; bm0 = cnt["bm"]; df0 = cnt["df"]; af0 = cnt["af"]
        rp0 = cnt["rp"]; q10 = cnt["q1"]; q20 = cnt["q2"]; q30 = cnt["q3"]
        q50 = cnt["q5"]; cb0 = cnt["cb"]
        kvb = [cnt["k0"], cnt["k1"]]
        q_af = tok["q%d_af" % layer]
        stab_n = 3 * KBq
        RIN, ROUT = 1024, 1536

        def _sy(s_):
            W(s_, "cc", tok["agk%d" % layer])
            W(s_, "cc", tok["agv%d" % layer])
            for h in range(NH):
                if h >= 2:
                    W(s_, "zf", zf0 + thru[(h - 1) * NQ - 1])
                for r in range(2):
                    inc(s_.dma_start(
                        out=kd[:, h % 2, r, :],
                        in_=k_out[layer][r * D + h * 128:
                                         r * D + (h + 1) * 128, :]),
                        "k%d" % (h % 2), 16)
                    inc(s_.dma_start(
                        out=vd[:, h % 2, r * NTB:(r + 1) * NTB, :],
                        in_=v_out[layer][(r * NH + h) * T:
                                         (r * NH + h + 1) * T, :]
                        .rearrange("(b p) e -> p b e", p=128)),
                        "k%d" % (h % 2), 16)
        blk.sync(_sy)

        def kv_wait(eng, h):
            W(eng, "k%d" % (h % 2), kvb[h % 2] + 64 * (h // 2 + 1))

        def q2_of_rb(hg):
            return q20 + hg + 1

        def _sc(sc):
            def sq2_of(td):
                ti = td["ti"]
                if td["bm"] is not None:
                    W(sc, "bm", bm0 + band_pre[ti])
                else:
                    W(sc, "sf", sf0 + ti + 1)
                if ti >= 2:
                    W(sc, "pf", pf0 + ti - 1)
                    W(sc, "zf", zf0 + ti - 1)
                inc(sc.activation(pt[:, ti % 2, :],
                                  qtmp[:, ti % 2, 0:QS], AF.Square), "ac")
                if td["last"] and td["hg"] >= 1:
                    hg = td["hg"]
                    W(sc, "q2", q2_of_rb(hg - 1))
                    if hg >= 3:
                        W(sc, "af", af0 + hg - 2)
                    inc(sc.activation(
                        rbc[:, ((hg - 1) % 2) * QS:((hg - 1) % 2 + 1) * QS],
                        psE[:, 0:QS], AF.Copy), "q3")

            spend = None
            for h in range(NH):
                if layer == 1 and spend is not None:
                    sq2_of(spend)
                    spend = None
                if layer == 1:
                    for r in range(2):
                        for b in range(KBq):
                            i = r * KBq + b
                            kv_wait(sc, h)
                            if h > 0 or i >= 2:
                                W(sc, "cb", cb0 + stab_n * h + i - 1)
                            inc(sc.activation(
                                sc512[:, i % 2, 0:NKB],
                                kd[:, h % 2, r, b * NKB:(b + 1) * NKB],
                                AF.Square), "q1")
                    for b in range(KBq):
                        i = 2 * KBq + b
                        W(sc, "af", q_af)
                        if h > 0 or i >= 2:
                            W(sc, "cb", cb0 + stab_n * h + i - 1)
                        inc(sc.activation(sc512[:, i % 2, 0:NKB],
                                          QV[:, h, b * NKB:(b + 1) * NKB],
                                          AF.Square), "q1")
                    W(sc, "q5", q50 + (stab_n + 1) * (h + 1))
                    inc(sc.activation(browz.ap()[0:1, 0:T],
                                      browz.ap()[0:1, 0:T], AF.Sqrt), "rp")
                for g in range(NQ):
                    hg = h * NQ + g
                    for td in tiles_by_hg[hg]:
                        ti = td["ti"]
                        if layer == 1:
                            W(sc, "pe", pe0 + ti + 1)
                            if ti >= 2:
                                W(sc, "pf", pf0 + ti - 1)
                                W(sc, "zf", zf0 + ti - 1)
                            inc(sc.activation(pt[:, ti % 2, :],
                                              psB[:, ti % 2, 0:QS], AF.Exp),
                                "sf")
                        else:
                            W(sc, "pe", pe0 + ti + 1)
                            if ti >= 2:
                                W(sc, "ac", ac0 + ti - 1)
                            inc(sc.activation(qtmp[:, ti % 2, 0:QS],
                                              psB[:, ti % 2, 0:QS],
                                              AF.Square), "sf")
                            if spend is not None:
                                sq2_of(spend)
                            spend = td
                    if layer == 1 and hg >= 1:
                        W(sc, "q2", q2_of_rb(hg - 1))
                        if hg >= 3:
                            W(sc, "af", af0 + hg - 2)
                        inc(sc.activation(
                            rbc[:, ((hg - 1) % 2) * QS:((hg - 1) % 2 + 1) * QS],
                            psE[:, 0:QS], AF.Copy), "q3")
            if spend is not None:
                sq2_of(spend)
            W(sc, "q2", q2_of_rb(NHG - 1))
            W(sc, "af", af0 + NHG - 2)
            inc(sc.activation(rbc[:, ((NHG - 1) % 2) * QS:
                                   ((NHG - 1) % 2 + 1) * QS],
                              psE[:, 0:QS], AF.Copy), "q3")
        blk.scalar(_sc)

        def _epilogue(v, phg):
            W(v, "pf", pf0 + thru[phg])
            if phg >= 1:
                W(v, "q2", q2_of_rb(phg - 1))
            if layer == 0:
                t6 = inc(v.tensor_scalar(rows.ap()[0:1, RIN:RIN + QS],
                                         psA[0:1, phg % 2, 0:QS],
                                         EPS_L0, None, A.max), "ve")
            else:
                t6 = inc(v.tensor_copy(rows.ap()[0:1, RIN:RIN + QS],
                                       psA[0:1, phg % 2, 0:QS]), "ve")
            v.wait_ge(sems["ve"], t6)
            t7 = inc(v.reciprocal(rows.ap()[0:1, ROUT:ROUT + QS],
                                  rows.ap()[0:1, RIN:RIN + QS]), "ve")
            v.wait_ge(sems["ve"], t7)
            inc(v.tensor_copy(browz.ap()[0:1, 1024:1024 + QS],
                              rows.ap()[0:1, ROUT:ROUT + QS]), "df")

        def _zevac(v, zhg):
            zh, zg = zhg // NQ, zhg % NQ
            W(v, "zf", zf0 + thru[zhg])
            W(v, "q3", q30 + zhg + 1)
            inc(v.tensor_tensor(QV[:, 16 + zh, zg * QS:(zg + 1) * QS],
                                psC[:, zhg % 2, 0:QS],
                                rbc[:, (zhg % 2) * QS:(zhg % 2 + 1) * QS],
                                A.mult), "af")

        def _ve(v):
            W(v, "gi", tok["gi_all"])
            for h in range(NH):
                if layer == 1:
                    for i in range(stab_n):
                        W(v, "cb", cb0 + stab_n * h + i + 1)
                        if i < 2 * KBq:
                            inc(v.tensor_reduce(
                                rtmp[0:1, 128 + i:129 + i],
                                psE[0:1, 0:NKB], AX.X, A.max), "q5")
                        else:
                            dst = rows.ap()[0:1, (i - 2 * KBq) * NKB:
                                            (i - 2 * KBq + 1) * NKB]
                            inc(v.tensor_copy(dst, psE[0:1, 0:NKB]), "q5")
                    W(v, "q5", q50 + (stab_n + 1) * h + stab_n)
                    vchain(v, v.tensor_reduce(rtmp[0:1, 132:133],
                                              rtmp[0:1, 128:128 + 2 * KBq],
                                              AX.X, A.max))
                    if h >= 1:
                        W(v, "pe", pe0 + thru[h * NQ - 1])
                    with nc.allow_low_precision(reason="bf16 c_q row"):
                        t5 = inc(v.tensor_scalar(browz.ap()[0:1, 0:T],
                                                 rows.ap()[0:1, 0:T],
                                                 rtmp[0:1, 132:133],
                                                 None, A.mult), "q5")
                    v.wait_ge(sems["q5"], t5)
                for g in range(NQ):
                    hg = h * NQ + g
                    if hg >= 1:
                        _epilogue(v, hg - 1)
                    if hg >= 2:
                        _zevac(v, hg - 2)
                    for td in tiles_by_hg[hg]:
                        ti = td["ti"]
                        if td["bm"] is not None:
                            W(v, "sf", sf0 + ti + 1)
                            dst = (pt[:, ti % 2, :] if layer == 1
                                   else qtmp[:, ti % 2, 0:QS])
                            inc(v.tensor_tensor(dst, dst,
                                                masks[:, td["bm"], :],
                                                A.mult), "bm")
            _epilogue(v, NHG - 1)
            for zhg in (NHG - 2, NHG - 1):
                _zevac(v, zhg)
        blk.vector(_ve)

        def _te(t):
            def s_mm(td):
                ti = td["ti"]
                W(t, "sf", sf0 + ti - 1 if ti >= 2 else sf0)
                mm = t.matmul(psB[:, ti % 2, 0:QS],
                              kd[:, td["h"] % 2, td["r"],
                                 td["j"] * 128:(td["j"] + 1) * 128],
                              QV[:, td["h"], td["g"] * QS:
                                 (td["g"] + 1) * QS],
                              start=True, stop=(layer == 0))
                if layer == 1:
                    mm = t.matmul(psB[:, ti % 2, 0:QS], neg_b[:],
                                  browz.ap()[0:1, td["g"] * QS:
                                             (td["g"] + 1) * QS],
                                  start=False, stop=True)
                inc(mm, "pe")

            def p_ready(td):
                ti = td["ti"]
                if layer == 1 and td["bm"] is not None:
                    W(t, "bm", bm0 + band_pre[ti])
                elif layer == 1:
                    W(t, "sf", sf0 + ti + 1)
                else:
                    W(t, "ac", ac0 + ti + 1)

            def avd(td):
                ti = td["ti"]; hg = td["hg"]
                p_ready(td)
                if td["first"] and hg >= 2:
                    W(t, "df", df0 + hg - 1)
                inc(t.matmul(psA[0:1, hg % 2, 0:QS], ones_k[:],
                             pt[:, ti % 2, :],
                             start=td["first"], stop=td["last"]), "pf")
                p_ready(td)
                if td["first"] and hg >= 2:
                    W(t, "af", af0 + hg - 1)
                inc(t.matmul(psC[:, hg % 2, 0:QS],
                             vd[:, td["h"] % 2,
                                td["r"] * NTB + td["j"], :],
                             pt[:, ti % 2, :],
                             start=td["first"], stop=td["last"]), "zf")
                if td["last"]:
                    if hg >= 1:
                        W(t, "df", df0 + hg)
                        if hg >= 2:
                            W(t, "q3", q30 + hg - 1)
                        inc(t.matmul(psE[:, 0:QS], ones_b[:],
                                     browz.ap()[0:1, 1024:1024 + QS],
                                     start=True, stop=True), "q2")

            W(t, "af", q_af)
            pend = []
            for h in range(NH):
                if layer == 1:
                    for pd in pend:
                        avd(pd)
                    pend = []
                kv_wait(t, h)
                if layer == 1:
                    for i in range(stab_n):
                        W(t, "q1", q10 + stab_n * h + i + 1)
                        if h > 0 or i > 0:
                            W(t, "q5", q50 + (stab_n + 1) * h + i)
                        W(t, "q3", q30 + h * NQ - 1 if h > 0 else 0)
                        inc(t.matmul(psE[0:1, 0:NKB], ones_k[:],
                                     sc512[:, i % 2, 0:NKB],
                                     start=True, stop=True), "cb")
                    W(t, "rp", rp0 + h + 1)
                for g in range(NQ):
                    hg = h * NQ + g
                    for td in tiles_by_hg[hg]:
                        s_mm(td)
                        if len(pend) >= 2:
                            avd(pend.pop(0))
                        pend.append(td)
            for pd in pend:
                avd(pd)
            W(t, "df", df0 + NHG)
            W(t, "q3", q30 + NHG - 1)
            inc(t.matmul(psE[:, 0:QS], ones_b[:],
                         browz.ap()[0:1, 1024:1024 + QS],
                         start=True, stop=True), "q2")
        blk.tensor(_te)
        tok["attn%d_af" % layer] = cnt["af"]
        tok["attn%d_zf" % layer] = cnt["zf"]

    attn(0)

    if upto == -3:
        zaf = cnt["af"]

        def _dbgst3(g):
            W(g, "af", zaf)
            inc(g.dma_start(out=y_fm.rearrange("(c p) t -> p c t", p=128),
                            in_=QV[:, 16:32, :]), "gi", 16)
            W(g, "gi", cnt["gi"])
        blk.gpsimd(_dbgst3)
        for t_ in reversed(ctx):
            t_.__exit__(None, None, None)
        return nc
    if upto == -4:
        zaf = cnt["af"]

        def _dbgst4(g):
            W(g, "af", zaf)
            inc(g.dma_start(out=y_fm.rearrange("(c p) t -> p c t", p=128),
                            in_=QV[:, 0:16, :]), "gi", 16)
            W(g, "gi", cnt["gi"])
        blk.gpsimd(_dbgst4)
        for t_ in reversed(ctx):
            t_.__exit__(None, None, None)
        return nc

    def mk_castwait(sem_name):
        snap = cnt[sem_name]
        def f(v, h):
            W(v, sem_name, snap)
        return f

    def _finish():
        def fin(sc):
            W(sc, "cy", tok["yst1"])
        blk.scalar(fin)

    def _close(nc_):
        for t in reversed(ctx):
            t.__exit__(None, None, None)
        return nc_

    def evac_res_mk(vq=None):
        def emit(h, info):
            nth = info["nth"]; pe0 = info["pe0"]

            def _ve(v):
                for n in range(nth):
                    gn = h * nth + n
                    oc, tb = tile_oc_tb(n)
                    t0 = h * TH + tb * NT
                    W(v, "pe", pe0 + gn + 1)
                    inc(v.scalar_tensor_tensor(
                        stream[:, oc, t0:t0 + NT], psA[:, gn % 2, 0:NT], 1.0,
                        stream[:, oc, t0:t0 + NT], A.mult, A.add), "af")
                    if vq is not None:
                        vq(v)
            blk.vector(_ve)
        return emit

    vq_hook, vq_stores = mk_vq_hook()
    proj(wo0, evac_res_mk(vq=vq_hook), rhs_z=True)
    vq_stores()
    if upto == -6:
        paf = cnt["af"]

        def _dbgst6(g):
            W(g, "af", paf)
            inc(g.dma_start(out=y_fm.rearrange("(c p) t -> p c t", p=128),
                            in_=QV[:, 16:32, :]), "gi", 16)
            W(g, "gi", cnt["gi"])
        blk.gpsimd(_dbgst6)
        for t_ in reversed(ctx):
            t_.__exit__(None, None, None)
        return nc
    if upto == -5:
        paf = cnt["af"]

        def _dbgst5(g):
            W(g, "af", paf)
            inc(g.dma_start(out=y_fm.rearrange("(c p) t -> p c t", p=128),
                            in_=stream[:]), "gi", 16)
            W(g, "gi", cnt["gi"])
        blk.gpsimd(_dbgst5)
        for t_ in reversed(ctx):
            t_.__exit__(None, None, None)
        return nc

    def layernorm(store_half=False, limit=99, halves=2):
        for h in range(halves):
            af_in = cnt["af"]
            ve0 = cnt["ve"]; ac0 = cnt["ac"]; pe0 = cnt["pe"]

            def _cast(v, h=h):
                W(v, "af", af_in)
                for c in range(NC):
                    inc(v.tensor_copy(xbh[:, c, :],
                                      stream[:, c, h * TH:(h + 1) * TH]), "ve")
            blk.vector(_cast)
            cast_tok = cnt["ve"]

            def _sq(sc, h=h):
                W(sc, "ve", cast_tok)
                for c in range(NC):
                    if c >= 2:
                        W(sc, "pe", pe0 + 2 * (c - 1))
                    inc(sc.activation(sc512[:, c % 2, 0:TH], xbh[:, c, :],
                                      AF.Square), "ac")
            if limit < 1:
                continue
            blk.scalar(_sq)

            def _mm(t, h=h):
                W(t, "ve", cast_tok)
                for c in range(NC):
                    inc(t.matmul(psD[0:1, 0:TH], ones_k[:], xbh[:, c, :],
                                 start=(c == 0), stop=(c == NC - 1)), "pe")
                    W(t, "ac", ac0 + c + 1)
                    inc(t.matmul(psE[0:1, 0:TH], ones_k[:],
                                 sc512[:, c % 2, 0:TH],
                                 start=(c == 0), stop=(c == NC - 1)), "pe")
            if limit < 2:
                continue
            blk.tensor(_mm)
            mm_tok = cnt["pe"]

            def _rows(v, h=h):
                W(v, "pe", mm_tok)
                vchain(v, v.tensor_scalar(rows.ap()[0:1, 0:TH], psD[0:1, 0:TH],
                                          1.0 / D, None, A.mult))
                vchain(v, v.tensor_scalar(rows.ap()[0:1, 512:512 + TH],
                                          psE[0:1, 0:TH], 1.0 / D, None,
                                          A.mult))
                vchain(v, v.tensor_tensor(rows.ap()[0:1, 1024:1024 + TH],
                                          rows.ap()[0:1, 0:TH],
                                          rows.ap()[0:1, 0:TH], A.mult))
                vchain(v, v.tensor_tensor(rows.ap()[0:1, 512:512 + TH],
                                          rows.ap()[0:1, 512:512 + TH],
                                          rows.ap()[0:1, 1024:1024 + TH],
                                          A.subtract))
                inc(v.tensor_scalar(rows.ap()[0:1, 512:512 + TH],
                                    rows.ap()[0:1, 512:512 + TH],
                                    EPS_LN, None, A.add), "df")
            if limit < 3:
                continue
            blk.vector(_rows)
            var_tok = cnt["df"]

            def _sqrt(sc, h=h):
                W(sc, "df", var_tok)
                inc(sc.activation(rows.ap()[0:1, 1024:1024 + TH],
                                  rows.ap()[0:1, 512:512 + TH], AF.Sqrt),
                    "q1")
            if limit < 4:
                continue
            blk.scalar(_sqrt)
            sd_tok = cnt["q1"]

            def _ab(v, h=h):
                W(v, "q1", sd_tok)
                vchain(v, v.reciprocal(rows.ap()[0:1, 1536:1536 + TH],
                                       rows.ap()[0:1, 1024:1024 + TH]))
                vchain(v, v.tensor_tensor(rows.ap()[0:1, 0:TH],
                                          rows.ap()[0:1, 0:TH],
                                          rows.ap()[0:1, 1536:1536 + TH],
                                          A.mult))
                inc(v.tensor_scalar(rows.ap()[0:1, 0:TH],
                                    rows.ap()[0:1, 0:TH],
                                    -1.0, None, A.mult), "q3")
            if limit < 5:
                continue
            blk.vector(_ab)
            ab_tok = cnt["q3"]

            def _bc(t, h=h):
                W(t, "q3", ab_tok)
                inc(t.matmul(psD[:, 0:TH], ones_b32[:],
                             rows.ap()[0:1, 1536:1536 + TH],
                             start=True, stop=True), "pe")
                inc(t.matmul(psE[:, 0:TH], ones_b32[:],
                             rows.ap()[0:1, 0:TH],
                             start=True, stop=True), "pe")
            if limit < 6:
                continue
            blk.tensor(_bc)
            bc_tok = cnt["pe"]

            def _norm(v, h=h):
                W(v, "pe", bc_tok)
                for c in range(NC):
                    t1 = inc(v.tensor_tensor(qtmp[:, c % 2, 0:TH],
                                             stream[:, c, h * TH:(h + 1) * TH],
                                             psD[:, 0:TH], A.mult), "ve")
                    if c >= 2:
                        pass
                    v.wait_ge(sems["ve"], t1)
                    inc(v.tensor_tensor(stream[:, c, h * TH:(h + 1) * TH],
                                        qtmp[:, c % 2, 0:TH],
                                        psE[:, 0:TH], A.add), "af")
            if limit < 7:
                continue
            blk.vector(_norm)
            norm_tok = cnt["af"]

            if store_half:
                def _yst(g2, h=h):
                    W(g2, "af", norm_tok)
                    inc(g2.dma_start(
                        out=y_fm[:, h * TH:(h + 1) * TH]
                        .rearrange("(c p) t -> p c t", p=128),
                        in_=stream[:, :, h * TH:(h + 1) * TH]), "cy", 16)
                    tok["yst%d" % h] = cnt["cy"]
                blk.gpsimd(_yst)

    layernorm(store_half=(upto == 1))
    if upto == 1:
        _finish()
        return _close(nc)

    def ffn0_gate(g):
        W(g, "cc", tok["ar1"])

    ffn0_q3 = {"q30": None}

    def evac_ffn0_half(h, info):
        nth = info["nth"]; pe0 = info["pe0"]
        if ffn0_q3["q30"] is None:
            ffn0_q3["q30"] = cnt["q3"]
        q30 = ffn0_q3["q30"]
        af0 = info["af0"]

        def _ve(v):
            for n in range(nth):
                gn = h * nth + n
                oc, tb = tile_oc_tb(n)
                t0 = h * TH + tb * NT
                W(v, "pe", pe0 + gn + 1)
                if gn >= 2:
                    W(v, "af", af0 + gn - 1)
                inc(v.scalar_tensor_tensor(
                    qtmp[:, gn % 2, 0:NT], psA[:, gn % 2, 0:NT],
                    INV_SQRT8, stream[:, oc, t0:t0 + NT],
                    A.mult, A.add), "q3")
        blk.vector(_ve)

        def _sc(sc):
            W(sc, "si", tok["si_all"])
            for n in range(nth):
                gn = h * nth + n
                oc, tb = tile_oc_tb(n)
                t0 = h * TH + tb * NT
                W(sc, "q3", q30 + gn + 1)
                inc(sc.activation(stream[:, oc, t0:t0 + NT],
                                  qtmp[:, gn % 2, 0:NT], AF.Identity,
                                  bias=bias_sb[:, oc:oc + 1]), "af")
        blk.scalar(_sc)

    proj(f_out, evac_ffn0_half, cast_wait=mk_castwait("af"),
         slab_gate=ffn0_gate, use_cast=True)
    tq_ag()
    layernorm(store_half=(upto == 2))
    if upto == 2:
        _finish()
        return _close(nc)

    lnaf = cnt["af"]

    def tg_ln(t):
        W(t, "af", lnaf)
    proj(wk1, evac_copy_mk(1, "k"), te_gate=tg_ln)
    kst1 = (cnt["ck0"], cnt["ck1"])
    proj_tokmaj(wv1, evac_copy_mk(1, "v"))
    vst1 = (cnt["cv0"], cnt["cv1"])
    proj(wq1, evac_copy_mk(1, "q", qscale=1.0 / math.sqrt(HD)))
    tok["q1_af"] = cnt["af"]

    def ag1(g):
        W(g, "ck0", kst1[0]); W(g, "ck1", kst1[1])
        tok["agk1"] = inc(g.collective_compute(
            "AllGather", A.bypass, replica_groups=PAIRS,
            ins=[k_in[1].opt()], outs=[k_out[1].opt()]), "cc")
        W(g, "cv0", vst1[0]); W(g, "cv1", vst1[1])
        tok["agv1"] = inc(g.collective_compute(
            "AllGather", A.bypass, replica_groups=PAIRS,
            ins=[v_in[1].opt()], outs=[v_out[1].opt()]), "cc")
    blk.gpsimd(ag1)

    attn(1)

    proj(wo1, evac_res_mk(), rhs_z=True)
    layernorm(store_half=(upto == 3))
    if upto == 3:
        _finish()
        return _close(nc)

    def ffn1_half(h, dbg=0):
        af_in = cnt["af"]

        ac0 = cnt["ac"]

        def _h128(t):
            W(t, "af", af_in)
            W(t, "gi", tok["gi_all"])
            for c in range(NC):
                if c >= 2:
                    W(t, "ac", ac0 + c - 1)
                inc(t.matmul(psB[:, c % 2, 0:TH], h128[:],
                             stream[:, c, h * TH:(h + 1) * TH],
                             start=True, stop=True), "pe")
        blk.tensor(_h128)
        h128_pe = cnt["pe"] - NC

        def _h128ev(sc):
            for c in range(NC):
                W(sc, "pe", h128_pe + c + 1)
                inc(sc.activation(HID[:, c, :], psB[:, c % 2, 0:TH],
                                  AF.Copy), "ac")
        blk.scalar(_h128ev)
        h128_ac = cnt["ac"]

        h16_ac0 = h128_ac - NC

        def _h16(v):
            W(v, "tf", cnt["tf"])
            for s in (1, 2, 4, 8):
                for i in range(NC):
                    if i & s:
                        continue
                    a, b = i, i + s
                    if s == 1:
                        W(v, "ac", h16_ac0 + b + 1)
                    t1 = inc(v.tensor_copy(ttmp[:, 0:TH], HID[:, a, :]), "ve")
                    v.wait_ge(sems["ve"], t1)
                    t2 = inc(v.tensor_tensor(HID[:, a, :], HID[:, a, :],
                                             HID[:, b, :], A.add), "ve")
                    v.wait_ge(sems["ve"], t2)
                    t3 = inc(v.tensor_tensor(HID[:, b, :], ttmp[:, 0:TH],
                                             HID[:, b, :], A.subtract), "ve")
                    v.wait_ge(sems["ve"], t3)
            t1 = inc(v.tensor_tensor(qtmp[:, 0, 0:TH], HID[:, 0, :],
                                     HID[:, 1, :], A.max), "ve")
            t2 = inc(v.tensor_tensor(qtmp[:, 1, 0:TH], HID[:, 0, :],
                                     HID[:, 1, :], A.min), "ve")
            v.wait_ge(sems["ve"], t2)
            for c in range(2, NC):
                t1 = inc(v.tensor_tensor(qtmp[:, 0, 0:TH], qtmp[:, 0, 0:TH],
                                         HID[:, c, :], A.max), "ve")
                t2 = inc(v.tensor_tensor(qtmp[:, 1, 0:TH], qtmp[:, 1, 0:TH],
                                         HID[:, c, :], A.min), "ve")
                v.wait_ge(sems["ve"], t2)
            t1 = inc(v.scalar_tensor_tensor(qtmp[:, 0, 0:TH], qtmp[:, 1, 0:TH],
                                            -1.0, qtmp[:, 0, 0:TH],
                                            A.mult, A.max), "ve")
            v.wait_ge(sems["ve"], t1)
            tok["fold1"] = cnt["ve"]
        blk.vector(_h16)

        def _par(g):
            W(g, "ve", tok["fold1"])
            tok["par1"] = inc(g.reduce_max(rows.ap()[0:1, 1536:1536 + TH],
                                           qtmp[:, 0, 0:TH], AX.C), "gc")
        blk.gpsimd(_par)

        def _rows1(v):
            W(v, "gc", tok["par1"])
            vchain(v, v.tensor_scalar(rows.ap()[0:1, 0:TH],
                                      rows.ap()[0:1, 1536:1536 + TH],
                                      1.0 / math.sqrt(D), EPS_Q,
                                      A.mult, A.max))
            vchain(v, v.reciprocal(rows.ap()[0:1, 512:512 + TH],
                                   rows.ap()[0:1, 0:TH]))
            vchain(v, v.tensor_scalar(rows.ap()[0:1, 512:512 + TH],
                                      rows.ap()[0:1, 512:512 + TH],
                                      127.0 / math.sqrt(D), None, A.mult))
            inc(v.tensor_scalar(rows.ap()[0:1, 1024:1024 + TH],
                                rows.ap()[0:1, 0:TH],
                                1.0 / 127.0, None, A.mult), "q3")
        blk.vector(_rows1)
        r1_tok = cnt["q3"]

        def _bc1(t):
            W(t, "q3", r1_tok)
            inc(t.matmul(psD[:, 0:TH], ones_b32[:],
                         rows.ap()[0:1, 512:512 + TH],
                         start=True, stop=True), "pe")
            inc(t.matmul(psE[:, 0:TH], ones_b32[:],
                         rows.ap()[0:1, 1024:1024 + TH],
                         start=True, stop=True), "pe")
        blk.tensor(_bc1)
        bc1_tok = cnt["pe"]

        def _cp1(sc):
            W(sc, "pe", bc1_tok)
            t1 = inc(sc.activation(cbc[:, 0:TH], psD[:, 0:TH], AF.Copy), "ac")
            inc(sc.activation(qtmp[:, 1, 0:TH], psE[:, 0:TH], AF.Copy), "ac")
        blk.scalar(_cp1)
        cp1_tok = cnt["ac"]

        def _round1(v):
            W(v, "ac", cp1_tok)
            tfb = cnt["tf"]
            for c in range(NC):
                if c >= 1:
                    W(v, "tf", tfb + c)
                t1 = inc(v.tensor_tensor(qtmp[:, 0, 0:TH], HID[:, c, :],
                                         cbc[:, 0:TH], A.mult), "ve")
                v.wait_ge(sems["ve"], t1)
                t2 = inc(v.tensor_scalar(qint[:, 0:TH], qtmp[:, 0, 0:TH],
                                         256.5, None, A.add), "ve")
                v.wait_ge(sems["ve"], t2)
                inc(v.scalar_tensor_tensor(xq1[:, c, :], qint[:, 0:TH],
                                           -256.0, qtmp[:, 1, 0:TH],
                                           A.add, A.mult), "tf")
        blk.vector(_round1)
        xq1_tok = cnt["tf"]

        if dbg == -20:
            def _d20(g):
                W(g, "tf", xq1_tok)
                inc(g.dma_start(out=y_fm[:, 0:TH]
                                .rearrange("(c p) t -> p c t", p=128),
                                in_=xq1[:]), "gi", 16)
                W(g, "gi", cnt["gi"])
            blk.gpsimd(_d20)
            return

        ssb = [cnt["s0"], cnt["s1"]]
        pe2 = cnt["pe"]; zf2 = cnt["zf"]
        q12 = cnt["q1"]; q32 = cnt["q3"]
        ngrp = (FF // NCORES // 512) * NCORES // 1
        NGRP = NCORES * 2

        def gate_off(r):
            return r * TERN

        def _syg(s_):
            W(s_, "cc", tok["agt"])
            nsl = 0
            for r in range(NCORES):
                for sub in range(2):
                    for mat, moff in ((0, 0), (1, GT)):
                        if nsl >= 2:
                            pg, pm = (nsl - 2) // 2, (nsl - 2) % 2
                            W(s_, "pe" if pm == 0 else "zf",
                              (pe2 if pm == 0 else zf2) + 4 * (pg + 1))
                        else:
                            W(s_, "pe", pe2)
                        base = r * TERN + moff + sub * GB
                        src = tern_out[base:base + GB] \
                            .rearrange("(p c o) -> p c o", c=16, o=512)
                        inc(s_.dma_start(out=ws[:, nsl % 2, :, :], in_=src),
                            "s%d" % (nsl % 2), 16)
                        nsl += 1
        blk.sync(_syg)

        gbanks = [psA[:, 0, 0:TH], psA[:, 1, 0:TH],
                  psB[:, 0, 0:TH], psB[:, 1, 0:TH]]
        ubanks = [psC[:, 0, 0:TH], psC[:, 1, 0:TH],
                  psD[:, 0:TH], psE[:, 0:TH]]

        def _teg(t):
            W(t, "tf", xq1_tok)
            nsl = 0
            for grp in range(NGRP):
                for mat in range(2):
                    W(t, "s%d" % (nsl % 2), ssb[nsl % 2] + 16 * (nsl // 2 + 1))
                    if mat == 0:
                        if grp >= 1:
                            W(t, "q1", q12 + 4 * grp)
                    else:
                        if grp >= 1:
                            W(t, "q3", q32 + 4 * grp)
                    banks = gbanks if mat == 0 else ubanks
                    for c in range(NC):
                        for oc in range(4):
                            iv = t.matmul(banks[oc],
                                          ws[:, nsl % 2, c,
                                             oc * 128:(oc + 1) * 128],
                                          xq1[:, c, :],
                                          start=(c == 0), stop=(c == NC - 1))
                            if c == NC - 1:
                                inc(iv, "pe" if mat == 0 else "zf")
                    nsl += 1
        blk.tensor(_teg)

        def _silu(sc):
            for grp in range(NGRP):
                for oc in range(4):
                    W(sc, "pe", pe2 + 4 * grp + oc + 1)
                    if grp >= 1 or oc >= 2:
                        W(sc, "q3", q32 + 4 * grp + oc - 1)
                    inc(sc.activation(sc512[:, oc % 2, 0:TH],
                                      gbanks[oc], AF.Silu,
                                      scale=scl[:, 4:5]), "q1")
        blk.scalar(_silu)

        def _stt(v):
            for grp in range(NGRP):
                for oc in range(4):
                    fc = grp * 4 + oc
                    W(v, "zf", zf2 + 4 * grp + oc + 1)
                    W(v, "q1", q12 + 4 * grp + oc + 1)
                    inc(v.scalar_tensor_tensor(HID[:, fc, :], ubanks[oc],
                                               scl[:, 5:6],
                                               sc512[:, oc % 2, 0:TH],
                                               A.mult, A.mult), "q3")
        blk.vector(_stt)
        gu_done = cnt["q3"]

        if dbg == -21:
            def _d21(g):
                W(g, "q3", gu_done)
                inc(g.dma_start(out=y_fm[:, 0:TH]
                                .rearrange("(c p) t -> p c t", p=128),
                                in_=HID[:, 0:16, :]), "gi", 16)
                inc(g.dma_start(out=y_fm[:, TH:2 * TH]
                                .rearrange("(c p) t -> p c t", p=128),
                                in_=HID[:, 16:32, :]), "gi", 16)
                W(g, "gi", cnt["gi"])
            blk.gpsimd(_d21)
            return

        q1_all = cnt["q1"]
        hb_pe = cnt["pe"]
        hb_ac0 = cnt["ac"]

        def _h128b(t):
            W(t, "q3", gu_done)
            W(t, "q1", q1_all)
            for c in range(NFC):
                if c >= 2:
                    W(t, "ac", hb_ac0 + c - 1)
                inc(t.matmul(psB[:, c % 2, 0:TH], h128[:], HID[:, c, :],
                             start=True, stop=True), "pe")
        blk.tensor(_h128b)

        def _h128bev(sc):
            for c in range(NFC):
                W(sc, "pe", hb_pe + c + 1)
                inc(sc.activation(HID[:, c, :], psB[:, c % 2, 0:TH],
                                  AF.Copy), "ac")
        blk.scalar(_h128bev)
        hb_ac = cnt["ac"]

        def _h64(v):
            for s in (1, 2, 4, 8, 16, 32):
                for i in range(NFC):
                    if i & s:
                        continue
                    a, b = i, i + s
                    if s == 1:
                        W(v, "ac", hb_ac - NFC + b + 1)
                    t1 = inc(v.tensor_copy(ttmp[:, 0:TH], HID[:, a, :]), "ve")
                    v.wait_ge(sems["ve"], t1)
                    t2 = inc(v.tensor_tensor(HID[:, a, :], HID[:, a, :],
                                             HID[:, b, :], A.add), "ve")
                    v.wait_ge(sems["ve"], t2)
                    t3 = inc(v.tensor_tensor(HID[:, b, :], ttmp[:, 0:TH],
                                             HID[:, b, :], A.subtract), "ve")
                    v.wait_ge(sems["ve"], t3)
            t1 = inc(v.tensor_tensor(qtmp[:, 0, 0:TH], HID[:, 0, :],
                                     HID[:, 1, :], A.max), "ve")
            t2 = inc(v.tensor_tensor(qtmp[:, 1, 0:TH], HID[:, 0, :],
                                     HID[:, 1, :], A.min), "ve")
            v.wait_ge(sems["ve"], t2)
            for c in range(2, NFC):
                t1 = inc(v.tensor_tensor(qtmp[:, 0, 0:TH], qtmp[:, 0, 0:TH],
                                         HID[:, c, :], A.max), "ve")
                t2 = inc(v.tensor_tensor(qtmp[:, 1, 0:TH], qtmp[:, 1, 0:TH],
                                         HID[:, c, :], A.min), "ve")
                v.wait_ge(sems["ve"], t2)
            t1 = inc(v.scalar_tensor_tensor(qtmp[:, 0, 0:TH], qtmp[:, 1, 0:TH],
                                            -1.0, qtmp[:, 0, 0:TH],
                                            A.mult, A.max), "ve")
            v.wait_ge(sems["ve"], t1)
            tok["fold2"] = cnt["ve"]
        blk.vector(_h64)

        def _par2(g):
            W(g, "ve", tok["fold2"])
            tok["par2"] = inc(g.reduce_max(rows.ap()[0:1, 1536:1536 + TH],
                                           qtmp[:, 0, 0:TH], AX.C), "gc")
        blk.gpsimd(_par2)

        def _rows2(v):
            W(v, "gc", tok["par2"])
            vchain(v, v.tensor_scalar(rows.ap()[0:1, 0:TH],
                                      rows.ap()[0:1, 1536:1536 + TH],
                                      1.0 / math.sqrt(FF), EPS_Q,
                                      A.mult, A.max))
            vchain(v, v.reciprocal(rows.ap()[0:1, 512:512 + TH],
                                   rows.ap()[0:1, 0:TH]))
            vchain(v, v.tensor_scalar(rows.ap()[0:1, 512:512 + TH],
                                      rows.ap()[0:1, 512:512 + TH],
                                      127.0 / math.sqrt(FF), None, A.mult))
            inc(v.tensor_scalar(rows.ap()[0:1, 1024:1024 + TH],
                                rows.ap()[0:1, 0:TH],
                                1.0 / 127.0, None, A.mult), "q3")
        blk.vector(_rows2)
        r2_tok = cnt["q3"]

        def _bc2(t):
            W(t, "q3", r2_tok)
            inc(t.matmul(psD[:, 0:TH], ones_b32[:],
                         rows.ap()[0:1, 512:512 + TH],
                         start=True, stop=True), "pe")
            inc(t.matmul(psE[:, 0:TH], ones_b32[:],
                         rows.ap()[0:1, 1024:1024 + TH],
                         start=True, stop=True), "pe")
        blk.tensor(_bc2)
        bc2_tok = cnt["pe"]

        def _cp2(sc):
            W(sc, "pe", bc2_tok)
            inc(sc.activation(cbc[:, 0:TH], psD[:, 0:TH], AF.Copy), "ac")
            inc(sc.activation(qtmp[:, 1, 0:TH], psE[:, 0:TH], AF.Copy), "ac")
        blk.scalar(_cp2)
        cp2_tok = cnt["ac"]

        tf2 = cnt["tf"]

        def _round2(v):
            W(v, "ac", cp2_tok)
            tfb = cnt["tf"]
            for c in range(NFC):
                if c >= 1:
                    W(v, "tf", tfb + c)
                t1 = inc(v.tensor_tensor(qtmp[:, 0, 0:TH], HID[:, c, :],
                                         cbc[:, 0:TH], A.mult), "ve")
                v.wait_ge(sems["ve"], t1)
                t2 = inc(v.tensor_scalar(qint[:, 0:TH], qtmp[:, 0, 0:TH],
                                         256.5, None, A.add), "ve")
                v.wait_ge(sems["ve"], t2)
                inc(v.scalar_tensor_tensor(HID[:, c, :], qint[:, 0:TH],
                                           -256.0, qtmp[:, 1, 0:TH],
                                           A.add, A.mult), "tf")
        blk.vector(_round2)

        if dbg == -22:
            def _d22(g):
                W(g, "tf", cnt["tf"])
                inc(g.dma_start(out=y_fm[:, 0:TH]
                                .rearrange("(c p) t -> p c t", p=128),
                                in_=HID[:, 0:16, :]), "gi", 16)
                inc(g.dma_start(out=y_fm[:, TH:2 * TH]
                                .rearrange("(c p) t -> p c t", p=128),
                                in_=HID[:, 16:32, :]), "gi", 16)
                W(g, "gi", cnt["gi"])
            blk.gpsimd(_d22)
            return

        sdb = [cnt["s0"], cnt["s1"]]
        pf3 = cnt["pf"]; af3 = cnt["af"]

        def _syd(s_):
            nsl = 0
            for r in range(NCORES):
                for s4 in range(4):
                    base = r * TERN + 2 * GT + s4 * DB
                    src = tern_out[base:base + DB] \
                        .rearrange("(p c o) -> p c o", c=16, o=DN)
                    if nsl >= 2:
                        W(s_, "pf", pf3 + nsl - 1)
                    else:
                        W(s_, "pe", pe2 + 4 * NGRP)
                        W(s_, "zf", zf2 + 4 * NGRP)
                    inc(s_.dma_start(out=ws[:, nsl % 2, :, 0:256],
                                     in_=src), "s%d" % (nsl % 2), 16)
                    nsl += 1
        blk.sync(_syd)

        def _ted(t):
            nsl = 0
            for r in range(NCORES):
                if r >= 1:
                    W(t, "af", af3 + 2 * r)
                for s4 in range(4):
                    W(t, "s%d" % (nsl % 2), sdb[nsl % 2] + 16 * (nsl // 2 + 1))
                    W(t, "tf", tf2 + (s4 + 1) * 16)
                    for c16 in range(16):
                        cg = s4 * 16 + c16
                        for oc in range(2):
                            iv = t.matmul(psA[:, oc, 0:TH],
                                          ws[:, nsl % 2, c16,
                                             oc * 128:(oc + 1) * 128],
                                          HID[:, cg, :],
                                          start=(cg == 0), stop=(cg == NFC - 1))
                            if c16 == 15 and oc == 1:
                                inc(iv, "pf")
                    nsl += 1
        blk.tensor(_ted)

        def _evd(v):
            for r in range(NCORES):
                for oc in range(2):
                    W(v, "pf", pf3 + 4 * (r + 1))
                    ocg = 2 * r + oc
                    inc(v.scalar_tensor_tensor(
                        stream[:, ocg, h * TH:(h + 1) * TH],
                        psA[:, oc, 0:TH], scl[:, 6:7],
                        stream[:, ocg, h * TH:(h + 1) * TH],
                        A.mult, A.add), "af")
        blk.vector(_evd)

        if dbg == -23:
            def _d23(g):
                W(g, "af", cnt["af"])
                inc(g.dma_start(out=y_fm.rearrange("(c p) t -> p c t", p=128),
                                in_=stream[:]), "gi", 16)
                W(g, "gi", cnt["gi"])
            blk.gpsimd(_d23)
            return

    if upto in (-20, -21, -22, -23):
        import os as _os
        ffn1_half(0, dbg=upto)
        for t_ in reversed(ctx):
            t_.__exit__(None, None, None)
        return nc

    for h in range(2):
        ffn1_half(h)
    layernorm(store_half=True)

    _finish()
    return _close(nc)


_CACHE = {}


def _host_stage(inputs, S):
    T = S // 2
    QS = min(512, T)
    NBM = 2 * (QS // 128)
    d = {}
    d["wq0"] = np.ascontiguousarray(inputs["L0_attn_Wq"].T)
    d["wk0"] = np.ascontiguousarray(inputs["L0_attn_Wk"].T)
    d["wv0"] = np.ascontiguousarray(inputs["L0_attn_Wv"].T)
    d["wo0"] = np.ascontiguousarray(inputs["L0_attn_Wo"].T)
    d["fct"] = None
    d["bias0"] = np.ascontiguousarray(
        np.asarray(inputs["L0_ffn_bias"]).reshape(NC, 128).T)
    d["wq1"] = np.ascontiguousarray(
        np.transpose(inputs["L1_attn_W_Q"], (2, 0, 1)).reshape(D, D))
    d["wk1"] = np.ascontiguousarray(
        np.transpose(inputs["L1_attn_W_K"], (2, 0, 1)).reshape(D, D))
    d["wv1"] = np.ascontiguousarray(
        np.transpose(inputs["L1_attn_W_V"], (2, 0, 1)).reshape(D, D))
    d["wo1"] = np.ascontiguousarray(
        np.transpose(inputs["L1_attn_W_O"], (0, 2, 1)).reshape(D, D))
    gT = np.ascontiguousarray(inputs["L1_ffn_w_gate"].T)
    uT = np.ascontiguousarray(inputs["L1_ffn_w_up"].T)
    dT = np.ascontiguousarray(inputs["L1_ffn_w_down"].T)
    H = np.array([[1.0]], dtype=np.float32)
    while H.shape[0] < 128:
        H = np.block([[H, H], [H, -H]])
    d["h128t"] = np.ascontiguousarray(H.astype(np.float32))

    x = np.asarray(inputs["L0_attn_x"])
    B = x.shape[0]
    maps = []
    for core in range(NCORES):
        b = core // 2
        p = core % 2
        blocks = [2 * j + p for j in range(T // 128)]
        cols = np.concatenate([np.arange(128 * t, 128 * (t + 1))
                               for t in blocks])
        m = dict(d)
        m["fct"] = np.asarray(inputs["L0_ffn_factors"][core])
        xc = np.ascontiguousarray(x[b].T[:, cols])
        m["x_fm"] = xc
        m["tg_sl"] = np.ascontiguousarray(
            gT[:, core * (FF // NCORES):(core + 1) * (FF // NCORES)])
        m["tu_sl"] = np.ascontiguousarray(
            uT[:, core * (FF // NCORES):(core + 1) * (FF // NCORES)])
        m["td_sl"] = np.ascontiguousarray(
            dT[:, core * (D // NCORES):(core + 1) * (D // NCORES)])
        mk = np.zeros((NBM, 128, QS), dtype=np.float32)
        kp = np.arange(128)
        qc = np.arange(QS)
        qblk = qc // 128
        qoff = 128 * (2 * qblk + p) + (qc % 128)
        for mm in range(NBM):
            koff = 128 * mm + kp
            mk[mm] = (koff[:, None] <= qoff[None, :]).astype(np.float32)
        m["maskt"] = mk.reshape(NBM * 128, QS)
        pos = cols.astype(np.float32)
        half = HD // 2
        theta = (10000.0 ** (-np.arange(half, dtype=np.float32) * 2.0 / HD))
        ang = theta[:, None] * pos[None, :]
        cosv = np.cos(ang).astype(np.float32)
        sinv = np.sin(ang).astype(np.float32)
        m["ropec"] = np.ascontiguousarray(
            np.concatenate([cosv, cosv], axis=0))
        m["ropes"] = np.ascontiguousarray(
            np.concatenate([-sinv, sinv], axis=0))
        maps.append(m)
    return maps, x.shape


def kernel(**inputs):
    S = np.asarray(inputs["L0_attn_x"]).shape[1]
    if S not in _CACHE:
        _CACHE[S] = build(S)
    nc = _CACHE[S]
    maps, xshape = _host_stage(inputs, S)
    res = run_bass_kernel_spmd(nc, maps, core_ids=list(range(NCORES)))
    B, S_, D_ = xshape
    T = S // 2
    out = np.empty((B, S_, D_), dtype=np.float32)
    for core in range(NCORES):
        b = core // 2
        p = core % 2
        y = res.results[core]["y_fm"]
        for j in range(T // 128):
            tblk = 2 * j + p
            out[b, 128 * tblk:128 * (tblk + 1), :] = \
                y[:, 128 * j:128 * (j + 1)].T
    return out
